# version 07
# speedup vs baseline: 1.3250x; 1.0098x over previous
import contextlib
import sys
import ml_dtypes
import numpy as np
import concourse.bass as bass
import concourse.mybir as mybir
from concourse.bass_utils import run_bass_kernel_spmd

F32 = mybir.dt.float32
BF16 = mybir.dt.bfloat16
AF = mybir.ActivationFunctionType
OP = mybir.AluOpType
AX = mybir.AxisListType

D = 1024
DT = D // 128
EPS = 1e-5
ENGS = ("v", "s", "g", "t", "y")


class Sem:
    __slots__ = ("name", "h", "n")

    def __init__(self, name):
        self.name = name
        self.h = None
        self.n = 0


class KB:
    def __init__(self, nc):
        self.nc = nc
        self.q = {e: [] for e in ENGS}
        self.sems = {}
        self.self_sem = {e: self.sem("self_" + e) for e in ("v", "s", "g")}
        self._st = contextlib.ExitStack()
        self._sb_real = {}
        self._dram_cache = {}
        self.arena = None
        self._ar_off = 0
        self._ar_cache = {}
        self._phase = 0

    def enable_arena(self, nbytes_cols_bf16):
        self.arena = self._st.enter_context(
            self.nc.sbuf_tensor("ARENA", [128, nbytes_cols_bf16], BF16))

    def arena_reset(self):
        self._ar_off = 0
        self._ar_cache = {}

    def dram(self, name, shape, dt):
        if name not in self._dram_cache:
            self._dram_cache[name] = self.nc.dram_tensor(name, shape, dt)
        return self._dram_cache[name]

    def new_phase(self, cc):
        self._phase += 1
        i = self._phase
        for e in ("v", "s", "g", "t", "y"):
            for sm in (cc.dsem, cc.dgi, cc.dws, cc.dout):
                self.wait(e, sm, sm.n)
        self.barrier()
        LIM = 30000
        if cc.dsem.n > LIM:
            cc.dsem = self.sem(f"dma_in_{i}")
        if cc.dgi.n > LIM:
            cc.dgi = self.sem(f"dma_in_g_{i}")
        if cc.dws.n > LIM:
            cc.dws = self.sem(f"dma_w_{i}")
        if cc.dout.n > LIM:
            cc.dout = self.sem(f"dma_out_{i}")
        if cc.tm.n > LIM:
            cc.tm = self.sem(f"tm_{i}")
        for e in ("v", "s", "g"):
            if self.self_sem[e].n > LIM:
                self.self_sem[e] = self.sem(f"self_{e}_{i}")
        self.arena_reset()

    def sem(self, name):
        if name not in self.sems:
            self.sems[name] = Sem(name)
        return self.sems[name]

    def sb(self, name, shape, dt):
        if self.arena is None:
            if name not in self._sb_real:
                self._sb_real[name] = self._st.enter_context(
                    self.nc.sbuf_tensor(name, shape, dt))
            return self._sb_real[name]
        if name in self._sb_real:
            return self._sb_real[name]
        if name in self._ar_cache:
            return self._ar_cache[name]
        rows, cols = shape
        bf_cols = cols if dt == BF16 else 2 * cols
        bf_cols = (bf_cols + 15) & ~15
        off = self._ar_off
        assert off + bf_cols <= self.arena.shape[1], \
            f"arena overflow at {name}: {off}+{bf_cols}"
        self._ar_off = off + bf_cols
        view = self.arena[:, off:off + bf_cols]
        if dt != BF16:
            view = view.bitcast(dt)
        ap = view[0:rows, 0:cols]
        self._ar_cache[name] = ap
        return ap

    def psum(self, name, shape, dt):
        return self._st.enter_context(self.nc.psum_tensor(name, shape, dt))

    def op(self, e, method, *a, waits=(), incs=(), selfsync=True, **kw):
        waits = list(waits)
        incs = list(incs)
        if e in ("v", "s", "g") and selfsync:
            ss = self.self_sem[e]
            if ss.n > 0:
                waits.append((ss, ss.n))
            incs.append((ss, 1))
        incs2 = []
        for (sm, n) in incs:
            if n == "cc":
                sm.n += 1
                incs2.append((sm, "cc", sm.n))
            else:
                sm.n += n
                incs2.append((sm, n, sm.n))
        incs = incs2
        fr = sys._getframe(1)
        while fr and fr.f_code.co_filename.endswith("kernel.py") and \
                fr.f_code.co_name in ("op", "dma", "mark"):
            fr = fr.f_back
        self.q[e].append([method, a, kw, waits, incs,
                          f"{fr.f_code.co_name}:{fr.f_lineno}" if fr else "?"])

    def mark(self, e, sem, n=1):
        sem.n += n
        self.q[e][-1][4].append((sem, n, sem.n))
        return sem.n

    def last(self, e):
        ss = self.self_sem[e]
        return (ss, ss.n)

    def wait(self, e, sem, val=None):
        self.q[e].append(["__wait__", (sem, sem.n if val is None else val), {}, (), ()])

    def dma(self, e, out, in_, sem, waits=(), **kw):
        w = list(waits) + ([(sem, sem.n)] if sem.n > 0 else [])
        self.op(e, "dma_start", out=out, in_=in_, waits=w, incs=[(sem, 16)],
                selfsync=False, **kw)
        return sem.n

    def _prod_point(self, e):
        if e in ("v", "s", "g"):
            ss = self.self_sem[e]
            return (ss, ss.n) if ss.n > 0 else None
        q = self.q[e]
        idx = None
        for i in range(len(q) - 1, -1, -1):
            if q[i][0] not in ("__wait__", "__inc__"):
                idx = i
                break
        if idx is None:
            return None
        if q[idx][4]:
            sm, n, after = q[idx][4][-1]
            return (sm, after)
        bm = self.sem("__bm_" + e)
        key = getattr(self, "_bar_last", {})
        if key.get(e, (None,))[0] == id(q[idx]):
            return (bm, key[e][1])
        bm.n += 1
        q[idx][4].append((bm, 1, bm.n))
        key[e] = (id(q[idx]), bm.n)
        self._bar_last = key
        return (bm, bm.n)

    def barrier(self, engines=("v", "s", "g", "t", "y")):
        pts = {e: self._prod_point(e) for e in engines}
        for e in engines:
            for e2 in engines:
                if e2 == e or pts[e2] is None:
                    continue
                self.wait(e, pts[e2][0], pts[e2][1])

    def finalize(self):
        nc = self.nc
        for s in self.sems.values():
            s.h = self._st.enter_context(nc.semaphore(s.name))
        eng_map = {"v": "vector", "s": "scalar", "g": "gpsimd", "t": "tensor",
                   "y": "sync"}
        with self._st:
            with nc.Block() as block:
                for e in ENGS:
                    if not self.q[e]:
                        continue

                    def make(run_q):
                        def _(eng):
                            for rec in run_q:
                                method, a, kw, waits, incs = rec[:5]
                                src_info = rec[5] if len(rec) > 5 else "?"

                                if method == "__wait__":
                                    sm, v = a
                                    eng.wait_ge(sm.h, v)
                                    continue
                                if method == "__inc__":
                                    sm, n = a
                                    eng.nop().then_inc(sm.h, n)
                                    continue
                                for (sm, v) in waits:
                                    eng.wait_ge(sm.h, v)
                                try:
                                    ins = getattr(eng, method)(*a, **kw)
                                except Exception as ex:
                                    raise RuntimeError(
                                        f"emit fail {method} from {src_info}: {ex}")
                                for inc in incs:
                                    if inc[1] == "cc":
                                        ins = ins.then_inc(inc[0].h)
                                    else:
                                        ins = ins.then_inc(inc[0].h, inc[1])
                        return _
                    getattr(block, eng_map[e])(make(self.q[e]))
        return nc


class CC:

    def __init__(self, kb, T):
        self.kb = kb
        self.T = T
        self.TC = T // 128
        self.N5 = T // 512 if T % 512 == 0 else 1
        self.C5 = 512 if T % 512 == 0 else T
        nc = kb.nc
        self.dsem = kb.sem("dma_in")
        self.dgi = kb.sem("dma_in_g")
        self.dws = kb.sem("dma_w")
        self.dout = kb.sem("dma_out")
        self.tm = kb.sem("tm")

        self.c_ones_col_bf = nc.dram_tensor("c_ones_col_bf", [128, 1], BF16,
                                            kind="ExternalInput")
        self.c_ones_row_bf = nc.dram_tensor("c_ones_row_bf", [1, 128], BF16,
                                            kind="ExternalInput")
        self.c_ones_row_f32 = nc.dram_tensor("c_ones_row_f32", [1, 128], F32,
                                             kind="ExternalInput")
        self.c_ident_bf = nc.dram_tensor("c_ident_bf", [128, 128], BF16,
                                         kind="ExternalInput")
        self.c_scal = nc.dram_tensor("c_scal", [128, 10], F32,
                                     kind="ExternalInput")
        self.c_ones_r65 = nc.dram_tensor("c_ones_r65", [65, 128], F32,
                                         kind="ExternalInput")
        self.ONES_COL = kb.sb("ONES_COL", [128, 1], BF16)
        self.ONES_ROWB = kb.sb("ONES_ROWB", [1, 128], BF16)
        self.ONES_ROW32 = kb.sb("ONES_ROW32", [1, 128], F32)
        self.IDENT = kb.sb("IDENT", [128, 128], BF16)
        self.SCAL = kb.sb("SCAL", [128, 10], F32)
        self.ONES65 = kb.sb("ONES65", [65, 128], F32)
        kb.dma("y", self.ONES_COL[:, :], self.c_ones_col_bf[:, :], self.dsem)
        kb.dma("y", self.ONES_ROWB[:, :], self.c_ones_row_bf[:, :], self.dsem)
        kb.dma("y", self.ONES_ROW32[:, :], self.c_ones_row_f32[:, :], self.dsem)
        kb.dma("y", self.IDENT[:, :], self.c_ident_bf[:, :], self.dsem)
        kb.dma("y", self.SCAL[:, :], self.c_scal[:, :], self.dsem)
        kb.dma("y", self.ONES65[:, :], self.c_ones_r65[:, :], self.dsem)

        self.P = [kb.psum(f"P{i}", [128, 512], F32) for i in range(8)]

        self.R_MU = kb.sb("R_MU", [1, 512], F32)[:, :]
        self.R_MS = kb.sb("R_MS", [1, 512], F32)[:, :]
        self.R_RS = self.R_MS
        self.R_NMR = kb.sb("R_NMR", [1, 512], F32)[:, :]
        self.RS = None
        self.NMR = None
        self.XB = kb.sb("XBSC", [128, 2 * 512], BF16)
        self.XQ = kb.sb("XQSC", [128, 2 * 512], BF16)

    SCAL_VALS = {1e-5: 0, -10.0: 1, -0.5: 2, 0.125: 3, 1.0: 4, 0.5: 5,
                 -1.0: 6, 1.0 / 1024: 7, 1.0 / 128: 8, 2.0: 9}

    def kf(self, val, p=128):
        c = self.SCAL_VALS[val]
        return self.SCAL[0:p, c:c + 1]

    def _ln_pipe(self, tile_fn, apply_cb, n_feat, ktiles, cast_X=None,
                 load_cb=None):
        kb, T = self.kb, self.T
        P = self.P
        RS_sb, NMR_sb = self.RS, self.NMR
        stats = [(P[0], P[1]), (P[2], P[3])]
        bcb = [(P[4], P[5]), (P[6], P[7])]
        rowmath_done = [0, 0]
        apply_gd = [0, 0]
        row_free = 0
        for ci, t0 in enumerate(range(0, T, 512)):
            w = min(512, T - t0)
            sl = ci % 2
            PA, PB = stats[sl]
            PC, PD = bcb[sl]
            if load_cb is not None:
                get = load_cb(t0, w)
            xmm = [0, 0]
            for k in range(ktiles):
                xsl = k % 2
                xo = xsl * 512
                if cast_X is not None or load_cb is not None:
                    srcap = (cast_X[:, k * T + t0:k * T + t0 + w]
                             if cast_X is not None else get(k, w))
                    vw = [(self.tm, xmm[xsl])] if xmm[xsl] else []
                    kb.op("v", "tensor_copy", self.XB[:, xo:xo + w],
                          srcap, waits=vw)
                    cast_v = kb.last("v")
                    kb.op("s", "activation", self.XQ[:, xo:xo + w],
                          self.XB[:, xo:xo + w], AF.Square, waits=[cast_v])
                    xb = self.XB[:, xo:xo + w]
                    kb.wait("t", *cast_v)
                else:
                    xb = tile_fn(k, t0, w)
                    if xmm[xsl]:
                        kb.wait("s", self.tm, xmm[xsl])
                    kb.op("s", "activation", self.XQ[:, xo:xo + w], xb,
                          AF.Square)
                sq = kb.last("s")
                if k == 0 and rowmath_done[sl]:
                    kb.wait("t", kb.self_sem["v"], rowmath_done[sl])
                kb.op("t", "matmul", PA[0:1, 0:w], self.ONES_COL[:, :], xb,
                      start=(k == 0), stop=(k == ktiles - 1))
                kb.wait("t", *sq)
                kb.op("t", "matmul", PB[0:1, 0:w], self.ONES_COL[:, :],
                      self.XQ[:, xo:xo + w], start=(k == 0),
                      stop=(k == ktiles - 1))
                xmm[xsl] = kb.mark("t", self.tm)
            smm = xmm[(ktiles - 1) % 2]
            mu = self.R_MU[:, 0:w]
            ms = self.R_MS[:, 0:w]
            nm = self.R_NMR[:, 0:w]
            if row_free:
                kb.wait("v", self.tm, row_free)
            kb.wait("v", self.tm, smm)
            kb.op("v", "tensor_scalar", mu, PA[0:1, 0:w], 1.0 / n_feat, None,
                  OP.mult)
            kb.op("v", "scalar_tensor_tensor", nm, PA[0:1, 0:w],
                  -1.0 / n_feat, mu, OP.mult, OP.mult)
            kb.op("v", "scalar_tensor_tensor", ms, PB[0:1, 0:w],
                  1.0 / n_feat, nm, OP.mult, OP.add)
            kb.op("s", "activation", ms, ms, AF.Ln, bias=self.kf(1e-5, 1),
                  waits=[kb.last("v")])
            kb.op("s", "activation", ms, ms, AF.Exp, scale=self.kf(-0.5, 1))
            kb.op("v", "scalar_tensor_tensor", nm, PA[0:1, 0:w],
                  -1.0 / n_feat, ms, OP.mult, OP.mult, waits=[kb.last("s")])
            rowmath_done[sl] = kb.last("v")[1]
            kb.wait("t", kb.self_sem["v"], rowmath_done[sl])
            if apply_gd[sl]:
                kb.wait("t", kb.self_sem["v"], apply_gd[sl])
            kb.op("t", "matmul", PC[:, 0:w], self.ONES_ROW32[0:1, :], ms,
                  start=True, stop=True)
            kb.op("t", "matmul", PD[:, 0:w], self.ONES_ROW32[0:1, :], nm,
                  start=True, stop=True)
            row_free = kb.mark("t", self.tm)
            kb.wait("v", self.tm, row_free)
            self.RS, self.NMR = PC, PD
            apply_cb(t0, w)
            self.RS, self.NMR = RS_sb, NMR_sb
            apply_gd[sl] = kb.self_sem["v"].n

    def ln_chunked(self, get_tile, apply_cb, n_feat=D, ktiles=DT):
        if isinstance(get_tile, _CastSrc):
            self._ln_pipe(None, apply_cb, n_feat, ktiles, cast_X=get_tile.X)
        else:
            self._ln_pipe(get_tile, apply_cb, n_feat, ktiles)

    def ln_stream(self, load_cb, apply_cb, n_feat=D, ktiles=DT):
        self._ln_pipe(None, apply_cb, n_feat, ktiles, load_cb=load_cb)

    def cast_tile_cb(self, X):
        return _CastSrc(X)


class _CastSrc:
    def __init__(self, X):
        self.X = X


def vcol(kb, name, dram, n=D, cc=None):
    t = kb.sb(name, [128, n // 128], F32)
    sem = cc.dgi if cc is not None else kb.sem("dma_in_g")
    kb.op("g", "dma_start", out=t[:, :], in_=dram.ap().rearrange("(k p) -> p k", p=128),
          incs=[(sem, 16)], selfsync=False,
          allow_slow_non_contiguous=True)
    return t


def vrow(kb, name, dram, n, dt=BF16, cc=None):
    t = kb.sb(name, [1, n], dt)
    sem = cc.dgi if cc is not None else kb.sem("dma_in_g")
    kb.op("g", "dma_start", out=t[:, :], in_=dram.ap().rearrange("(o n) -> o n", o=1),
          incs=[(sem, 16)], selfsync=False)
    return t


def emit_l0f(kb, cc, HD, nbatch):
    T, TC = cc.T, cc.TC
    P = cc.P
    NHP = 8

    X = kb.sb("l0f_X", [128, DT * T], F32)
    ABF = kb.sb("l0f_ABF", [128, 2 * DT * T], BF16)
    XN = ABF[:, 0 * DT * T:1 * DT * T]
    CXT = ABF[:, 1 * DT * T:2 * DT * T]
    AOF = CXT
    if DT * T >= 2 * DT * 1024:
        WEA = ABF[:, 1 * DT * T:1 * DT * T + 2 * DT * 1024]
    else:
        WEA = kb.sb("l0f_WEA", [128, 2 * DT * 1024], BF16)
    QK = kb.sb("l0f_QK", [128, max(2 * T, DT * 512)], BF16)
    QH = QK[:, 0:T]
    KH = QK[:, T:2 * T]
    VT = kb.sb("l0f_VT", [128, TC * 130], BF16)
    ATT = kb.sb("l0f_ATT", [128, 2 * 512], BF16)
    CTOK = kb.sb("l0f_CTOK", [128, 4 * 64], BF16)
    WPRJ = kb.sb("l0f_WPRJ", [128, 3 * DT * 128], BF16)
    WO = kb.sb("l0f_WO", [128, DT * 1024], BF16)
    WG = kb.sb("l0f_WG", [128, DT * 8], BF16)
    U1 = kb.sb("l0f_U1", [128, 512], F32)
    GP = kb.sb("l0f_GP", [128, 8], F32)
    GM = kb.sb("l0f_GM", [128, 2], F32)
    MSKK = kb.sb("l0f_MSKK", [128, TC * 8], BF16)
    MSKB = kb.sb("l0f_MSKB", [8, T], BF16)
    MSK_DR = kb.dram("l0f_mskdr", [8, T], BF16)
    MBC8 = kb.sb("l0f_MBC8", [128, 8 * 512], BF16)
    BGBC = kb.sb("l0f_BGBC", [128, 8], F32)
    XAE = kb.sb("l0f_XAE", [128, DT * 512], BF16)
    MO = QK[:, 0:DT * 512]
    BEX = kb.sb("l0f_BEX", [8, D], BF16)

    g1 = vcol(kb, "l0f_g1", HD["L0_ffn_g1"], cc=cc)
    be1 = vcol(kb, "l0f_be1", HD["L0_ffn_be1"], cc=cc)
    g2 = vcol(kb, "l0f_g2", HD["L0_ffn_g2"], cc=cc)
    be2 = vcol(kb, "l0f_be2", HD["L0_ffn_be2"], cc=cc)
    bq = vcol(kb, "l0f_bq", HD["L0_ffn_bq"], cc=cc)
    bk = vcol(kb, "l0f_bk", HD["L0_ffn_bk"], cc=cc)
    bo = vcol(kb, "l0f_bo", HD["L0_ffn_bo"], cc=cc)
    bv_row = vrow(kb, "l0f_bv", HD["L0_ffn_bv"], D, BF16, cc=cc)
    bg_row = vrow(kb, "l0f_bg", HD["L0_ffn_bg"], 8, F32, cc=cc)
    kb.dma("g", BEX[:, :], HD["L0_ffn_bexp"][:, :], cc.dgi)
    kb.dma("g", WG[:, :].rearrange("p (k e) -> p k e", e=8),
           HD["L0_ffn_Wg"].ap().rearrange("(k p) e -> p k e", p=128), cc.dgi)
    wq_r = HD["L0_ffn_Wq"].ap().rearrange("(k p) (hp c) -> p k hp c", p=128, c=128)
    wk_r = HD["L0_ffn_Wk"].ap().rearrange("(k p) (hp c) -> p k hp c", p=128, c=128)
    wv_r = HD["L0_ffn_Wv"].ap().rearrange("(k p) (hp c) -> p k hp c", p=128, c=128)
    wo_r = HD["L0_ffn_Wo"].ap().rearrange("(k p) o -> p k o", p=128)
    we_r = HD["L0_ffn_We"].ap().rearrange("e (k p) o -> e p k o", p=128)
    WE_BF = kb.dram("l0f_we_bf", [8, 128, DT * 1024], BF16)

    kb.barrier()
    for e in ("v", "s", "t", "g"):
        kb.wait(e, cc.dsem, cc.dsem.n)
        kb.wait(e, cc.dgi, cc.dgi.n)

    if not getattr(kb, "_l0f_we_cast", False):
        kb._l0f_we_cast = True
        sv_slot = [0, 0]
        for e8 in range(8):
            sl = e8 % 2
            buf = WEA[:, sl * DT * 512:sl * DT * 512 + DT * 512]
            for half in range(2):
                war = [(cc.dout, sv_slot[sl])] if sv_slot[sl] else []
                ld = kb.dma("g", buf.rearrange("p (k o) -> p k o", o=512),
                            we_r[e8][:, :, half * 512:(half + 1) * 512],
                            cc.dws, waits=war)
                sv_slot[sl] = kb.dma(
                    "y", WE_BF.ap()[e8].rearrange("p (k o) -> p k o", o=1024
                                                  )[:, :, half * 512:(half + 1) * 512],
                    buf.rearrange("p (k o) -> p k o", o=512), cc.dout,
                    waits=[(cc.dws, ld)])
        kb.wait("g", cc.dout, cc.dout.n)
        kb.wait("y", cc.dout, cc.dout.n)

    for b in range(nbatch):
        xld = kb.dma("y", X[:, :].rearrange("p (k t) -> p k t", k=DT),
                     HD["xin"][b].rearrange("(k p) t -> p k t", p=128), cc.dsem)
        kb.wait("v", cc.dsem, xld)
        def apply_a(t0, w):
            for k in range(DT):
                sl = slice(k * T + t0, k * T + t0 + w)
                kb.op("v", "tensor_mul", U1[:, 0:w], X[:, sl], cc.RS[:, 0:w])
                kb.op("v", "tensor_add", XN[:, sl], U1[:, 0:w], cc.NMR[:, 0:w])
        cc.ln_chunked(cc.cast_tile_cb(X), apply_a)
        kb.barrier()

        wprj_free = 0
        for hp in range(NHP):
            if wprj_free:
                kb.wait("g", cc.tm, wprj_free)
            kb.dma("g", WPRJ[:, 0 * DT * 128:1 * DT * 128].rearrange("p (k c) -> p k c", c=128), wq_r[:, :, hp, :], cc.dws)
            kb.dma("g", WPRJ[:, 1 * DT * 128:2 * DT * 128].rearrange("p (k c) -> p k c", c=128), wk_r[:, :, hp, :], cc.dws)
            wl = kb.dma("g", WPRJ[:, 2 * DT * 128:3 * DT * 128].rearrange(
                "p (k c) -> p k c", c=128), wv_r[:, :, hp, :], cc.dws)
            kb.wait("t", cc.dws, wl)
            kb.op("v", "memset",
                  VT[:, :].rearrange("p (tc c) -> p tc c", c=130)[:, :, 64:65], 1.0)
            kb.wait("t", *kb.last("v"))
            for t0 in range(0, T, 512):
                w = min(512, T - t0)
                for k in range(DT):
                    kb.op("t", "matmul", P[4][:, 0:w],
                          WPRJ[:, k * 128:(k + 1) * 128],
                          XN[:, k * T + t0:k * T + t0 + w],
                          start=(k == 0), stop=(k == DT - 1))
                for k in range(DT):
                    kb.op("t", "matmul", P[5][:, 0:w],
                          WPRJ[:, DT * 128 + k * 128:DT * 128 + (k + 1) * 128],
                          XN[:, k * T + t0:k * T + t0 + w],
                          start=(k == 0), stop=(k == DT - 1))
                tv = kb.mark("t", cc.tm)
                kb.wait("v", cc.tm, tv)
                kb.op("v", "tensor_scalar", QH[:, t0:t0 + w], P[4][:, 0:w],
                      bq[:, hp:hp + 1], None, OP.add)
                kb.op("v", "tensor_scalar", KH[:, t0:t0 + w], P[5][:, 0:w],
                      bk[:, hp:hp + 1], None, OP.add)
                kb.wait("t", *kb.last("v"))
            for tc in range(TC):
                t0 = tc * 128
                for k in range(DT):
                    kb.op("t", "matmul", P[6][:, 0:128],
                          XN[:, k * T + t0:k * T + t0 + 128],
                          WPRJ[:, 2 * DT * 128 + k * 128:2 * DT * 128 + (k + 1) * 128],
                          start=(k == 0), stop=False)
                kb.op("t", "matmul", P[6][:, 0:128], cc.ONES_ROWB[0:1, :],
                      bv_row[0:1, hp * 128:(hp + 1) * 128], start=False, stop=True)
                tv = kb.mark("t", cc.tm)
                kb.wait("v", cc.tm, tv)
                kb.op("v", "tensor_copy", VT[:, tc * 130:tc * 130 + 64],
                      P[6][:, 0:64])
                kb.op("v", "tensor_copy", VT[:, tc * 130 + 65:tc * 130 + 129],
                      P[6][:, 64:128])
                kb.wait("t", *kb.last("v"))
            wprj_free = cc.tm.n
            esem = kb.sem("l0f_exp")
            for hh in range(2):
                fb = hh * 64
                for tq in range(0, T, 512):
                    wq = min(512, T - tq)
                    nq = (wq + 127) // 128
                    exp_mark = [0, 0]
                    ctx_mark = [0, 0]
                    for tk in range(TC):
                        pb = tk % 2
                        if tk >= 2:
                            kb.wait("t", kb.self_sem["s"], exp_mark[pb])
                        kb.op("t", "matmul", P[4 + pb][:, 0:wq],
                              KH[fb:fb + 64, tk * 128:(tk + 1) * 128],
                              QH[fb:fb + 64, tq:tq + wq], start=True, stop=True)
                        tv = kb.mark("t", cc.tm)
                        kb.wait("s", cc.tm, tv)
                        if tk >= 2:
                            kb.wait("s", cc.tm, ctx_mark[pb])
                        kb.op("s", "activation", ATT[:, pb * 512:pb * 512 + wq],
                              P[4 + pb][:, 0:wq], AF.Exp, bias=cc.kf(-10.0),
                              scale=cc.kf(0.125))
                        exp_mark[pb] = kb.last("s")[1]
                        kb.wait("t", kb.self_sem["s"], exp_mark[pb])
                        for q in range(nq):
                            kb.op("t", "matmul", P[q][:, 0:65],
                                  ATT[:, pb * 512 + q * 128:pb * 512 + min((q + 1) * 128, wq)],
                                  VT[:, tk * 130 + 64 * hh:tk * 130 + 64 * hh + 65],
                                  start=(tk == 0), stop=(tk == TC - 1))
                        ctx_mark[pb] = kb.mark("t", cc.tm)
                    kb.wait("v", cc.tm, cc.tm.n)
                    rcol, c0 = (64, 0) if hh == 0 else (0, 1)
                    for q in range(nq):
                        kb.op("v", "reciprocal", U1[:, q:q + 1],
                              P[q][:, rcol:rcol + 1])
                        kb.op("v", "tensor_scalar", CTOK[:, q * 64:(q + 1) * 64],
                              P[q][:, c0:c0 + 64], U1[:, q:q + 1], None, OP.mult)
                    kb.wait("t", *kb.last("v"))
                    PT = P[7][:, :].bitcast(BF16)
                    for q in range(nq):
                        kb.op("t", "transpose", PT[0:64, q * 128:(q + 1) * 128],
                              CTOK[:, q * 64:(q + 1) * 64], cc.IDENT[:, :])
                    tv = kb.mark("t", cc.tm)
                    kb.wait("v", cc.tm, tv)
                    kb.op("v", "tensor_copy",
                          CXT[fb:fb + 64, hp * T + tq:hp * T + tq + wq],
                          PT[0:64, 0:wq])
                    kb.wait("t", *kb.last("v"))
        kb.barrier()

        wl = kb.dma("g", WO[:, :].rearrange("p (k o) -> p k o", o=1024),
                    wo_r, cc.dws)
        kb.wait("t", cc.dws, wl)
        for t0 in range(0, T, 512):
            w = min(512, T - t0)
            for oc in range(DT):
                for k in range(DT):
                    kb.op("t", "matmul", P[oc][:, 0:w],
                          WO[:, k * 1024 + oc * 128:k * 1024 + (oc + 1) * 128],
                          CXT[:, k * T + t0:k * T + t0 + w],
                          start=(k == 0), stop=(k == DT - 1))
            tv = kb.mark("t", cc.tm)
            kb.wait("v", cc.tm, tv)
            for oc in range(DT):
                kb.op("v", "tensor_scalar", AOF[:, oc * T + t0:oc * T + t0 + w],
                      P[oc][:, 0:w], bo[:, oc:oc + 1], None, OP.add)
            kb.wait("t", *kb.last("v"))
        kb.barrier()
        def apply_c(t0, w):
            for k in range(DT):
                sl = slice(k * T + t0, k * T + t0 + w)
                kb.op("v", "tensor_mul", U1[:, 0:w], AOF[:, sl], cc.RS[:, 0:w])
                kb.op("v", "tensor_add", U1[:, 0:w], U1[:, 0:w], cc.NMR[:, 0:w])
                kb.op("v", "tensor_scalar", U1[:, 0:w], U1[:, 0:w],
                      g1[:, k:k + 1], be1[:, k:k + 1], OP.mult, OP.add)
                kb.op("v", "tensor_add", U1[:, 0:w], U1[:, 0:w], XN[:, sl])
                kb.op("v", "scalar_tensor_tensor", X[:, sl], U1[:, 0:w], 0.5,
                      X[:, sl], OP.mult, OP.add)
                kb.op("v", "tensor_copy", XN[:, sl], U1[:, 0:w])
        cc.ln_chunked(lambda k, t0, w: AOF[:, k * T + t0:k * T + t0 + w], apply_c)
        kb.barrier()

        kb.wait("t", *kb.last("v"))
        kb.op("t", "matmul", P[6][:, 0:8], cc.ONES_ROW32[0:1, :],
              bg_row[0:1, 0:8], start=True, stop=True)
        tv = kb.mark("t", cc.tm)
        kb.wait("v", cc.tm, tv)
        kb.op("v", "tensor_copy", BGBC[:, 0:8], P[6][:, 0:8])
        kb.wait("t", *kb.last("v"))
        for tc in range(TC):
            t0 = tc * 128
            for k in range(DT):
                kb.op("t", "matmul", P[6][:, 0:8],
                      XN[:, k * T + t0:k * T + t0 + 128], WG[:, k * 8:(k + 1) * 8],
                      start=(k == 0), stop=(k == DT - 1))
            tv = kb.mark("t", cc.tm)
            kb.wait("v", cc.tm, tv)
            kb.op("v", "tensor_tensor", GP[:, 0:8], P[6][:, 0:8], BGBC[:, 0:8],
                  OP.add)
            kb.wait("t", *kb.last("v"))
            kb.op("v", "tensor_reduce", GM[:, 0:1], GP[:, 0:8], AX.X, OP.max)
            kb.op("v", "tensor_scalar", GM[:, 0:1], GM[:, 0:1], -1.0, None,
                  OP.mult)
            kb.op("s", "activation", GP[:, 0:8], GP[:, 0:8], AF.Exp,
                  bias=GM[:, 0:1], waits=[kb.last("v")])
            kb.op("v", "tensor_reduce", GM[:, 1:2], GP[:, 0:8], AX.X, OP.add,
                  waits=[kb.last("s")])
            kb.op("v", "reciprocal", GM[:, 1:2], GM[:, 1:2])
            kb.op("v", "tensor_scalar", GP[:, 0:8], GP[:, 0:8], GM[:, 1:2],
                  None, OP.mult)
            kb.op("v", "tensor_reduce", GM[:, 0:1], GP[:, 0:8], AX.X, OP.max)
            kb.op("v", "tensor_scalar", U1[:, 0:8], GP[:, 0:8], GM[:, 0:1],
                  None, OP.is_equal)
            kb.op("v", "scalar_tensor_tensor", U1[:, 0:8], U1[:, 0:8], -1e30,
                  GP[:, 0:8], OP.mult, OP.add)
            kb.op("v", "tensor_reduce", GM[:, 1:2], U1[:, 0:8], AX.X, OP.max)
            kb.op("v", "scalar_tensor_tensor", U1[:, 0:8], GP[:, 0:8],
                  GM[:, 1:2], GP[:, 0:8], OP.is_ge, OP.mult)
            kb.op("v", "tensor_copy", MSKK[:, tc * 8:(tc + 1) * 8], U1[:, 0:8])
            kb.wait("t", *kb.last("v"))
            PT = P[7][:, :].bitcast(BF16)
            kb.op("t", "transpose", PT[0:8, 0:128], MSKK[:, tc * 8:(tc + 1) * 8],
                  cc.IDENT[:, :])
            tv = kb.mark("t", cc.tm)
            kb.wait("v", cc.tm, tv)
            kb.op("v", "tensor_copy", MSKB[0:8, t0:t0 + 128], PT[0:8, 0:128])
            kb.wait("t", *kb.last("v"))
        kb.barrier()
        mf = kb.dma("y", MSK_DR[:, :], MSKB[0:8, :], cc.dsem)
        kb.wait("y", cc.dsem, mf)
        kb.barrier()

        mbc_free = 0
        for t0 in range(0, T, 512):
            w = min(512, T - t0)
            if mbc_free:
                kb.wait("y", kb.self_sem["v"], mbc_free)
            for e in range(8):
                kb.dma("y", MBC8[:, e * 512:e * 512 + w],
                       MSK_DR[e:e + 1, t0:t0 + w].broadcast_to([128, w]), cc.dsem)
            mbca = cc.dsem.n
            kb.wait("v", cc.dsem, mbca)
            kb.wait("g", cc.dsem, mbca)
            if t0 == 0:
                wea_used = [0, 0]
                wl_done = [0, 0]
                wl_done[0] = kb.dma("g", WEA[:, 0:DT * 1024].rearrange(
                    "p (k o) -> p k o", o=1024), WE_BF.ap()[0], cc.dws)
                wl_done[1] = kb.dma("g", WEA[:, DT * 1024:2 * DT * 1024].rearrange(
                    "p (k o) -> p k o", o=1024), WE_BF.ap()[1], cc.dws)
            prev_mark = 0
            for e in range(8):
                sl = e % 2
                if prev_mark:
                    kb.wait("v", cc.tm, prev_mark)
                for k in range(DT):
                    kb.op("v", "tensor_mul", XAE[:, k * 512:k * 512 + w],
                          XN[:, k * T + t0:k * T + t0 + w],
                          MBC8[:, e * 512:e * 512 + w])
                kb.wait("t", cc.dws, wl_done[sl])
                kb.wait("t", *kb.last("v"))
                for oc in range(DT):
                    for k in range(DT):
                        kb.op("t", "matmul", P[oc][:, 0:w],
                              WEA[:, sl * DT * 1024 + k * 1024 + oc * 128:
                                  sl * DT * 1024 + k * 1024 + (oc + 1) * 128],
                              XAE[:, k * 512:k * 512 + w],
                              start=(e == 0 and k == 0), stop=False)
                prev_mark = kb.mark("t", cc.tm)
                wea_used[sl] = prev_mark
                last_chunk = (t0 + 512 >= T)
                if not (last_chunk and e >= 6):
                    wl_done[sl] = kb.dma(
                        "g",
                        WEA[:, sl * DT * 1024:(sl + 1) * DT * 1024].rearrange(
                            "p (k o) -> p k o", o=1024),
                        WE_BF.ap()[(e + 2) % 8], cc.dws,
                        waits=[(cc.tm, wea_used[sl])])
                if e == 7:
                    mbc_free = kb.last("v")[1]
            for oc in range(DT):
                kb.op("t", "matmul", P[oc][:, 0:w], BEX[0:8, oc * 128:(oc + 1) * 128],
                      MSKB[0:8, t0:t0 + w], start=False, stop=True)
            tv = kb.mark("t", cc.tm)
            kb.wait("s", cc.tm, tv)
            for oc in range(DT):
                kb.op("s", "activation", MO[:, oc * 512:oc * 512 + w], P[oc][:, 0:w],
                      AF.Copy)
            kb.wait("t", *kb.last("s"))
            for k in range(DT):
                kb.op("s", "activation", cc.XQ[:, 0:w], MO[:, k * 512:k * 512 + w],
                      AF.Square)
                kb.wait("t", *kb.last("s"))
                kb.op("t", "matmul", P[0][0:1, 0:w], cc.ONES_COL[:, :],
                      MO[:, k * 512:k * 512 + w], start=(k == 0), stop=(k == DT - 1))
                kb.op("t", "matmul", P[1][0:1, 0:w], cc.ONES_COL[:, :],
                      cc.XQ[:, 0:w], start=(k == 0), stop=(k == DT - 1))
                tv = kb.mark("t", cc.tm)
                kb.wait("s", cc.tm, tv)
            tv = cc.tm.n
            kb.wait("v", cc.tm, tv)
            kb.op("v", "tensor_scalar", cc.R_MU[:, 0:w], P[0][0:1, 0:w],
                  1.0 / D, None, OP.mult)
            kb.op("v", "tensor_scalar", cc.R_MS[:, 0:w], P[1][0:1, 0:w],
                  1.0 / D, None, OP.mult)
            kb.op("v", "scalar_tensor_tensor", cc.R_NMR[:, 0:w],
                  cc.R_MU[:, 0:w], -1.0, cc.R_MU[:, 0:w],
                  OP.mult, OP.mult)
            kb.op("v", "tensor_add", cc.R_MS[:, 0:w],
                  cc.R_MS[:, 0:w], cc.R_NMR[:, 0:w])
            kb.op("s", "activation", cc.R_MS[:, 0:w],
                  cc.R_MS[:, 0:w], AF.Ln, bias=cc.kf(1e-5, 1),
                  waits=[kb.last("v")])
            kb.op("s", "activation", cc.R_RS[:, 0:w],
                  cc.R_MS[:, 0:w], AF.Exp, scale=cc.kf(-0.5, 1))
            kb.op("v", "scalar_tensor_tensor", cc.R_NMR[:, 0:w],
                  cc.R_MU[:, 0:w], -1.0, cc.R_RS[:, 0:w],
                  OP.mult, OP.mult, waits=[kb.last("s")])
            kb.wait("t", *kb.last("v"))
            kb.op("t", "matmul", P[2][:, 0:w], cc.ONES_ROW32[0:1, :],
                  cc.R_RS[:, 0:w], start=True, stop=True)
            kb.op("t", "matmul", P[3][:, 0:w], cc.ONES_ROW32[0:1, :],
                  cc.R_NMR[:, 0:w], start=True, stop=True)
            tv = kb.mark("t", cc.tm)
            kb.wait("v", cc.tm, tv)
            for k in range(DT):
                sl = slice(k * T + t0, k * T + t0 + w)
                kb.op("v", "tensor_mul", U1[:, 0:w], MO[:, k * 512:k * 512 + w],
                      P[2][:, 0:w])
                kb.op("v", "tensor_add", U1[:, 0:w], U1[:, 0:w], P[3][:, 0:w])
                kb.op("v", "tensor_scalar", U1[:, 0:w], U1[:, 0:w],
                      g2[:, k:k + 1], be2[:, k:k + 1], OP.mult, OP.add)
                kb.op("v", "scalar_tensor_tensor", X[:, sl], U1[:, 0:w], 0.5,
                      X[:, sl], OP.mult, OP.add)
            kb.wait("t", *kb.last("v"))
        kb.barrier()

        so = kb.dma("y", HD["xout"][b].rearrange("(k p) t -> p k t", p=128),
                    X[:, :].rearrange("p (k t) -> p k t", k=DT), cc.dout)
        kb.wait("y", cc.dout, so)
    return


def build_l0f(T, nbatch):
    nc = bass.Bass("TRN2", target_bir_lowering=False)
    HD = {}
    HD["xin"] = nc.dram_tensor("xin", [nbatch, D, T], F32, kind="ExternalInput").ap()
    HD["xout"] = nc.dram_tensor("xout", [nbatch, D, T], F32, kind="ExternalOutput").ap()
    for nm, shp in [("L0_ffn_Wq", [D, D]), ("L0_ffn_bq", [D]),
                    ("L0_ffn_Wk", [D, D]), ("L0_ffn_bk", [D]),
                    ("L0_ffn_Wv", [D, D]), ("L0_ffn_bv", [D]),
                    ("L0_ffn_Wo", [D, D]), ("L0_ffn_bo", [D]),
                    ("L0_ffn_g1", [D]), ("L0_ffn_be1", [D]),
                    ("L0_ffn_g2", [D]), ("L0_ffn_be2", [D]),
                    ("L0_ffn_Wg", [D, 8]), ("L0_ffn_bg", [8]),
                    ("L0_ffn_We", [8, D, D]), ("L0_ffn_bexp", [8, D])]:
        HD[nm] = nc.dram_tensor(nm, shp, F32, kind="ExternalInput")
    kb = KB(nc)
    cc = CC(kb, T)
    emit_l0f(kb, cc, HD, nbatch)
    kb.finalize()
    return nc



E_IN = 2048
ET = E_IN // 128
NST = 16
RNK = 64


L0A_SKIP = ()


def emit_l0a(kb, cc, HD, nbatch):
    T, TC = cc.T, cc.TC
    P = cc.P
    nc = kb.nc

    X = kb.sb("l0a_X", [128, DT * T], F32)
    HN = kb.sb("l0a_HN", [128, DT * T], BF16)
    XCR = kb.sb("l0a_XCR", [128, T + 8], BF16)
    XCT = kb.sb("l0a_XCT", [128, 2 * T], BF16)
    ZST = kb.sb("l0a_ZST", [128, T], BF16)
    DTR_PS = None
    DTROW = kb.sb("l0a_DTROW", [64, T], BF16)
    B16 = kb.sb("l0a_B16", [16, T], BF16)
    C16 = kb.sb("l0a_C16", [16, T], BF16)
    BREP = kb.sb("l0a_BREP", [128, T], BF16)
    CREP = kb.sb("l0a_CREP", [128, T], BF16)
    DTB = kb.sb("l0a_DTB", [128, T], BF16)
    UB = kb.sb("l0a_UB", [128, T], BF16)
    DTF = kb.sb("l0a_DTF", [128, 512], F32)
    DTF2 = kb.sb("l0a_DTF2", [128, 512], F32)
    DA = kb.sb("l0a_DA", [128, 2 * 512], BF16)
    DBU = kb.sb("l0a_DBU", [128, 2 * 512], BF16)
    HS = kb.sb("l0a_HS", [128, 2 * 512], F32)
    HSLAST = kb.sb("l0a_HSLAST", [128, 16], F32)
    HCB = kb.sb("l0a_HCB", [128, 2 * 512], BF16)
    ACOL = kb.sb("l0a_ACOL", [128, 2 * NST], F32)
    SEL = kb.sb("l0a_SEL", [128, 16 * 128], BF16)
    SELT = kb.sb("l0a_SELT", [128, 16 * 128], BF16)
    S16 = kb.sb("l0a_S16", [16, 128], BF16)
    WTB = kb.sb("l0a_WTB", [128, 8 * 128], BF16)
    WTB2 = kb.sb("l0a_WTB2", [128, 16 * 128], BF16)
    TSTG = kb.sb("l0a_TSTG", [128, 1024], BF16)
    YF = kb.sb("l0a_YF", [128, T], BF16)
    U1 = kb.sb("l0a_U1", [128, 512], F32)
    XI = kb.sb("l0a_XI", [128, DT * 512], F32)
    LNW = kb.sb("l0a_LNW", [128, 3 * DT], F32)
    LNB = kb.sb("l0a_LNB", [128, 3 * DT], F32)
    DCOL = kb.sb("l0a_DCOL", [128, ET], F32)
    DTPB = kb.sb("l0a_DTPB", [128, ET], F32)
    CVW = kb.sb("l0a_CVW", [128, 4 * ET], F32)
    CVB = kb.sb("l0a_CVB", [128, ET], F32)

    WT_IN = kb.dram("l0a_wt_in", [2, 32, 128, 8 * 128], BF16)
    WT_OUT = kb.dram("l0a_wt_out", [2, DT, 128, 16 * 128], BF16)
    WT_XP = kb.dram("l0a_wt_xp", [2, 16, 128, 96], BF16)
    WT_DTP = kb.dram("l0a_wt_dtp", [2, 16, 64, 128], BF16)
    XC_DR = kb.dram("l0a_xc_dr", [ET, 128, T], BF16)
    Z_DR = kb.dram("l0a_z_dr", [ET, 128, T], BF16)
    YF_DR = kb.dram("l0a_yf_dr", [ET, 128, T], BF16)

    for i in range(2):
        kb.op("g", "dma_start", out=LNW[:, i * DT:(i + 1) * DT],
              in_=HD["L0_attn_ln_w"][i].rearrange("(k p) -> p k", p=128),
              incs=[(cc.dgi, 16)], selfsync=False, allow_slow_non_contiguous=True)
        kb.op("g", "dma_start", out=LNB[:, i * DT:(i + 1) * DT],
              in_=HD["L0_attn_ln_b"][i].rearrange("(k p) -> p k", p=128),
              incs=[(cc.dgi, 16)], selfsync=False, allow_slow_non_contiguous=True)
    kb.op("g", "dma_start", out=LNW[:, 2 * DT:3 * DT],
          in_=HD["L0_attn_norm_f_w"].ap().rearrange("(k p) -> p k", p=128),
          incs=[(cc.dgi, 16)], selfsync=False, allow_slow_non_contiguous=True)
    kb.op("g", "dma_start", out=LNB[:, 2 * DT:3 * DT],
          in_=HD["L0_attn_norm_f_b"].ap().rearrange("(k p) -> p k", p=128),
          incs=[(cc.dgi, 16)], selfsync=False, allow_slow_non_contiguous=True)
    kb.dma("y", SEL[:, :], HD["c_sel"][:, :], cc.dsem)
    kb.dma("y", SELT[:, :], HD["c_selt"][:, :], cc.dsem)
    kb.dma("y", S16[:, :], HD["c_s16"][:, :], cc.dsem)
    kb.barrier()
    for e in ("v", "s", "t", "g"):
        kb.wait(e, cc.dsem, cc.dsem.n)
        kb.wait(e, cc.dgi, cc.dgi.n)

    PT = P[7][:, :].bitcast(BF16)

    def transpose_to_dram(src_rows_ap, n_in_tiles, dst_ap, m_cols):
        pass

    tprev = 0
    for i in range(2) if "pre" not in L0A_SKIP else ():
        inw_r = HD["L0_attn_in_proj_w"][i].rearrange("(ec p) d -> ec p d", p=128)
        for ec in range(32):
            if tprev:
                kb.wait("g", cc.tm, tprev)
            wl = kb.dma("g", TSTG[:, 0:1024], inw_r[ec], cc.dws)
            kb.wait("t", cc.dws, wl)
            for k in range(DT):
                kb.op("t", "transpose", PT[:, k * 128:(k + 1) * 128],
                      TSTG[:, k * 128:(k + 1) * 128], cc.IDENT[:, :])
            tprev = kb.mark("t", cc.tm)
            kb.wait("v", cc.tm, tprev)
            kb.op("v", "tensor_copy", WTB[:, :], PT[:, 0:1024])
            kb.wait("t", *kb.last("v"))
            sv = kb.dma("y", WT_IN[i, ec], WTB[:, :], cc.dout, waits=[kb.last("v")])
            kb.wait("v", cc.dout, sv)
        outw_r = HD["L0_attn_out_proj_w"][i].rearrange("(dc p) e -> dc p e", p=128)
        for dc in range(DT):
            kb.wait("g", cc.tm, tprev)
            wl = kb.dma("g", TSTG[:, 0:1024], outw_r[dc][:, 0:1024], cc.dws)
            wl2 = kb.dma("g", WTB2[:, 0:1024], outw_r[dc][:, 1024:2048], cc.dws)
            kb.wait("t", cc.dws, wl2)
            for k in range(8):
                kb.op("t", "transpose", PT[:, k * 128:(k + 1) * 128],
                      TSTG[:, k * 128:(k + 1) * 128], cc.IDENT[:, :])
            tv = kb.mark("t", cc.tm)
            kb.wait("v", cc.tm, tv)
            kb.op("v", "tensor_copy", TSTG[:, 0:1024], PT[:, 0:1024])
            kb.wait("t", *kb.last("v"))
            for k in range(8):
                kb.op("t", "transpose", PT[:, k * 128:(k + 1) * 128],
                      WTB2[:, k * 128:(k + 1) * 128], cc.IDENT[:, :])
            tprev = kb.mark("t", cc.tm)
            kb.wait("v", cc.tm, tprev)
            kb.op("v", "tensor_copy", WTB2[:, 0:1024], PT[:, 0:1024])
            kb.wait("t", *kb.last("v"))
            sv = kb.dma("y", WT_OUT[i, dc][:, 0:1024], TSTG[:, 0:1024], cc.dout,
                        waits=[kb.last("v")])
            sv = kb.dma("y", WT_OUT[i, dc][:, 1024:2048], WTB2[:, 0:1024],
                        cc.dout)
            kb.wait("g", cc.dout, sv)
            kb.wait("v", cc.dout, sv)
        xp_r = HD["L0_attn_x_proj_w"][i]
        for half in range(2):
            kb.wait("g", cc.tm, tprev)
            wl = kb.dma("g", TSTG[0:96, 0:1024],
                        xp_r[:, half * 1024:(half + 1) * 1024], cc.dws)
            kb.wait("t", cc.dws, wl)
            for j in range(8):
                et = half * 8 + j
                kb.op("t", "transpose", PT[:, j * 96:(j + 1) * 96],
                      TSTG[0:96, j * 128:(j + 1) * 128], cc.IDENT[0:96, 0:96])
                tprev = kb.mark("t", cc.tm)
                kb.wait("v", cc.tm, tprev)
                kb.op("v", "tensor_copy", WTB[:, 0:96], PT[:, j * 96:(j + 1) * 96])
                kb.wait("t", *kb.last("v"))
                sv = kb.dma("y", WT_XP[i, et], WTB[:, 0:96], cc.dout, waits=[kb.last("v")])
                kb.wait("v", cc.dout, sv)
        dtp_r = HD["L0_attn_dt_proj_w"][i].rearrange("(et p) r -> et p r", p=128)
        for et in range(16):
            kb.wait("g", cc.tm, tprev)
            wl = kb.dma("g", TSTG[:, 0:64], dtp_r[et], cc.dws)
            kb.wait("t", cc.dws, wl)
            kb.op("t", "transpose", PT[0:64, 0:128], TSTG[:, 0:64],
                  cc.IDENT[:, :])
            tprev = kb.mark("t", cc.tm)
            kb.wait("v", cc.tm, tprev)
            kb.op("v", "tensor_copy", WTB[0:64, 0:128], PT[0:64, 0:128])
            kb.wait("t", *kb.last("v"))
            sv = kb.dma("y", WT_DTP[i, et], WTB[0:64, 0:128], cc.dout, waits=[kb.last("v")])
            kb.wait("v", cc.dout, sv)
    kb.barrier()

    for b in range(nbatch):
        xld = kb.dma("y", XI[:, 0:DT * 512].rearrange("p (k t) -> p k t", k=DT)
                     if False else X[:, :].rearrange("p (k t) -> p k t", k=DT),
                     HD["xin"][b].rearrange("(k p) t -> p k t", p=128), cc.dsem)
        kb.wait("v", cc.dsem, xld)
        kb.wait("s", cc.dsem, xld)

        def apply_pl(t0, w):
            for k in range(DT):
                sl = slice(k * T + t0, k * T + t0 + w)
                kb.op("v", "tensor_mul", U1[:, 0:w], X[:, sl], cc.RS[:, 0:w])
                kb.op("v", "tensor_add", X[:, sl], U1[:, 0:w], cc.NMR[:, 0:w])
        cc.ln_chunked(cc.cast_tile_cb(X), apply_pl)
        kb.barrier()

        for li in range(2):
            kb.op("g", "dma_start",
                  out=CVW[:, :].rearrange("p (k et) -> p k et", et=ET),
                  in_=HD["L0_attn_conv_w"][li].rearrange("(et p) k -> p k et", p=128),
                  incs=[(cc.dgi, 16)], selfsync=False,
                  allow_slow_non_contiguous=True)
            kb.op("g", "dma_start", out=CVB[:, :],
                  in_=HD["L0_attn_conv_b"][li].rearrange("(et p) -> p et", p=128),
                  incs=[(cc.dgi, 16)], selfsync=False,
                  allow_slow_non_contiguous=True)
            kb.op("g", "dma_start", out=DCOL[:, :],
                  in_=HD["L0_attn_D_skip"][li].rearrange("(et p) -> p et", p=128),
                  incs=[(cc.dgi, 16)], selfsync=False,
                  allow_slow_non_contiguous=True)
            kb.op("g", "dma_start", out=DTPB[:, :],
                  in_=HD["L0_attn_dt_proj_b"][li].rearrange("(et p) -> p et", p=128),
                  incs=[(cc.dgi, 16)], selfsync=False,
                  allow_slow_non_contiguous=True)
            for e in ("v", "s", "t"):
                kb.wait(e, cc.dgi, cc.dgi.n)

            def apply_m1(t0, w, li=li):
                for k in range(DT):
                    sl = slice(k * T + t0, k * T + t0 + w)
                    kb.op("v", "tensor_mul", U1[:, 0:w], X[:, sl], cc.RS[:, 0:w])
                    kb.op("v", "tensor_add", U1[:, 0:w], U1[:, 0:w], cc.NMR[:, 0:w])
                    kb.op("v", "tensor_scalar", HN[:, sl], U1[:, 0:w],
                          LNW[:, li * DT + k:li * DT + k + 1],
                          LNB[:, li * DT + k:li * DT + k + 1], OP.mult, OP.add)
            cc.ln_chunked(cc.cast_tile_cb(X), apply_m1)
            kb.barrier()

            NCH2 = (T + 511) // 512
            if "m2" in L0A_SKIP:
                kb.barrier()
            slot_ev = [None] * 4
            wtb_used = [0, 0]
            wtb_ld = [0, 0]
            bufs = [WTB[:, 0:1024], WTB2[:, 0:1024]]
            wtb_ld[0] = kb.dma("g", bufs[0], WT_IN[li, 0], cc.dws)
            xct_sv = 0
            zst_sv = 0
            for ec in range(32) if "m2" not in L0A_SKIP else ():
                wsl = ec % 2
                buf = bufs[wsl]
                kb.wait("t", cc.dws, wtb_ld[wsl])
                is_z = ec >= 16
                et = ec - 16 if is_z else ec
                for t0 in range(0, T, 512):
                    w = min(512, T - t0)
                    pb = t0 // 512
                    if slot_ev[pb] is not None:
                        kb.wait("t", kb.self_sem[slot_ev[pb][0]], slot_ev[pb][1])
                    for k in range(DT):
                        kb.op("t", "matmul", P[pb][:, 0:w],
                              buf[:, k * 128:(k + 1) * 128],
                              HN[:, k * T + t0:k * T + t0 + w],
                              start=(k == 0), stop=(k == DT - 1))
                    tv = kb.mark("t", cc.tm)
                    if is_z:
                        if t0 == 0 and zst_sv:
                            kb.wait("s", cc.dout, zst_sv)
                        kb.wait("s", cc.tm, tv)
                        kb.op("s", "activation", ZST[:, t0:t0 + w], P[pb][:, 0:w],
                              AF.Silu)
                        slot_ev[pb] = ("s", kb.last("s")[1])
                    else:
                        kb.wait("v", cc.tm, tv)
                        kb.op("v", "tensor_copy", XCR[:, 8 + t0:8 + t0 + w],
                              P[pb][:, 0:w])
                        slot_ev[pb] = ("v", kb.last("v")[1])
                wtb_used[wsl] = cc.tm.n
                if ec + 1 < 32:
                    war = ([(cc.tm, wtb_used[1 - wsl])]
                           if wtb_used[1 - wsl] else [])
                    wtb_ld[1 - wsl] = kb.dma("g", bufs[1 - wsl],
                                             WT_IN[li, ec + 1], cc.dws,
                                             waits=war)
                if is_z:
                    zst_sv = kb.dma("y", Z_DR[et], ZST[:, :], cc.dout,
                                    waits=[kb.last("s")])
                else:
                    kb.op("v", "memset", XCR[:, 5:8], 0.0)
                    for t0 in range(0, T, 512):
                        w = min(512, T - t0)
                        kb.op("v", "tensor_scalar", U1[:, 0:w],
                              XCR[:, 5 + t0:5 + t0 + w],
                              CVW[:, 0 * ET + et:0 * ET + et + 1], None, OP.mult)
                        for kk in range(1, 4):
                            kb.op("v", "scalar_tensor_tensor", U1[:, 0:w],
                                  XCR[:, 5 + kk + t0:5 + kk + t0 + w],
                                  CVW[:, kk * ET + et:kk * ET + et + 1],
                                  U1[:, 0:w], OP.mult, OP.add)
                        kb.op("v", "tensor_scalar", U1[:, 0:w], U1[:, 0:w],
                              CVB[:, et:et + 1], None, OP.add)
                        if t0 == 0 and xct_sv:
                            kb.wait("s", cc.dout, xct_sv)
                        kb.op("s", "activation", XCT[:, t0:t0 + w], U1[:, 0:w],
                              AF.Silu, waits=[kb.last("v")])
                        kb.wait("v", *kb.last("s"))
                    xct_sv = kb.dma("y", XC_DR[et], XCT[:, 0:T], cc.dout,
                                    waits=[kb.last("s")])
            kb.barrier()

            kb.wait("g", cc.dout, cc.dout.n)
            xp_used = [0, 0]
            xp_ld = [0, 0]
            for t0 in range(0, T, 512) if "m2b" not in L0A_SKIP else ():
                w = min(512, T - t0)

                def xp_pre(et, t0=t0, w=w):
                    sl3 = et % 2
                    war = ([(cc.tm, xp_used[sl3])] if xp_used[sl3] else [])
                    kb.dma("g", WTB[:, sl3 * 96:sl3 * 96 + 96],
                           WT_XP[li, et], cc.dws, waits=war)
                    kb.dma("g", XCT[:, sl3 * T + t0:sl3 * T + t0 + w],
                           XC_DR[et][:, t0:t0 + w], cc.dws)
                    xp_ld[sl3] = cc.dws.n
                xp_pre(0)
                xp_pre(1)
                for et in range(ET):
                    sl3 = et % 2
                    wo = sl3 * 96
                    xo = sl3 * T + t0
                    kb.wait("t", cc.dws, xp_ld[sl3])
                    kb.op("t", "matmul", P[2][0:64, 0:w], WTB[:, wo:wo + 64],
                          XCT[:, xo:xo + w], start=(et == 0), stop=(et == ET - 1))
                    kb.op("t", "matmul", P[3][0:16, 0:w], WTB[:, wo + 64:wo + 80],
                          XCT[:, xo:xo + w], start=(et == 0), stop=(et == ET - 1))
                    kb.op("t", "matmul", P[4][0:16, 0:w], WTB[:, wo + 80:wo + 96],
                          XCT[:, xo:xo + w], start=(et == 0), stop=(et == ET - 1))
                    xp_used[sl3] = kb.mark("t", cc.tm)
                    if et + 2 < ET:
                        xp_pre(et + 2)
                kb.wait("v", cc.tm, cc.tm.n)
                kb.op("v", "tensor_copy", DTROW[:, t0:t0 + w], P[2][0:64, 0:w])
                kb.op("v", "tensor_copy", B16[:, t0:t0 + w], P[3][0:16, 0:w])
                kb.op("v", "tensor_copy", C16[:, t0:t0 + w], P[4][0:16, 0:w])
                kb.wait("t", *kb.last("v"))
            for t0 in range(0, T, 512):
                w = min(512, T - t0)
                kb.op("t", "matmul", P[2][:, 0:w], S16[:, :], B16[:, t0:t0 + w],
                      start=True, stop=True)
                kb.op("t", "matmul", P[3][:, 0:w], S16[:, :], C16[:, t0:t0 + w],
                      start=True, stop=True)
                tv = kb.mark("t", cc.tm)
                kb.wait("v", cc.tm, tv)
                kb.op("v", "tensor_copy", BREP[:, t0:t0 + w], P[2][:, 0:w])
                kb.op("v", "tensor_copy", CREP[:, t0:t0 + w], P[3][:, 0:w])
                kb.wait("t", *kb.last("v"))
            kb.barrier()

            NCH = (T + 511) // 512
            xct_free = [0, 0]
            acol_free = [0, 0]
            wtb_free = [0, 0]
            m3ld = {}

            def m3_pre(et):
                sl2 = et % 2
                w1 = kb.dma("g", WTB[0:64, sl2 * 128:sl2 * 128 + 128],
                            WT_DTP[li, et], cc.dws,
                            waits=([(cc.tm, wtb_free[sl2])]
                                   if wtb_free[sl2] else []))
                x1 = kb.dma("g", XCT[:, sl2 * T:sl2 * T + T], XC_DR[et],
                            cc.dws,
                            waits=([(kb.self_sem["v"], xct_free[sl2])]
                                   if xct_free[sl2] else []))
                a1 = kb.dma("g", ACOL[:, sl2 * NST:(sl2 + 1) * NST],
                            HD["L0A_Alog_r"][li, et], cc.dws,
                            waits=([(kb.self_sem["s"], acol_free[sl2])]
                                   if acol_free[sl2] else []))
                m3ld[et] = max(w1, x1, a1)

            prev_exp = [0, 0]
            prev_hc = 0
            prev_tm = 0
            spill_last = 0
            prev_cmb = 0
            if "m3" not in L0A_SKIP:
                m3_pre(0)
            for et in range(ET) if "m3" not in L0A_SKIP else ():
                xsl = et % 2
                kb.wait("t", cc.dws, m3ld[et])
                kb.wait("v", cc.dws, m3ld[et])
                kb.wait("s", cc.dws, m3ld[et])
                if prev_tm:
                    kb.wait("v", cc.tm, prev_tm)
                if prev_hc:
                    kb.wait("v", kb.self_sem["g"], prev_hc)
                kb.op("s", "activation", ACOL[:, xsl * NST:(xsl + 1) * NST],
                      ACOL[:, xsl * NST:(xsl + 1) * NST], AF.Exp)
                kb.op("v", "tensor_scalar", ACOL[:, xsl * NST:(xsl + 1) * NST],
                      ACOL[:, xsl * NST:(xsl + 1) * NST], -1.0, None,
                      OP.mult, waits=[kb.last("s")])
                pend_dt = None
                for c4 in range(NCH):
                    t0 = c4 * 512
                    w = min(512, T - t0)
                    pb = c4 % 2
                    if prev_exp[pb]:
                        kb.wait("t", kb.self_sem["s"], prev_exp[pb])
                        prev_exp[pb] = 0
                    kb.op("t", "matmul", P[4 + pb][:, 0:w],
                          WTB[0:64, xsl * 128:xsl * 128 + 128],
                          DTROW[:, t0:t0 + w], start=True, stop=True)
                    tv = kb.mark("t", cc.tm)
                    kb.wait("v", cc.tm, tv)
                    kb.op("v", "tensor_scalar", HS[:, pb * 512:pb * 512 + w],
                          P[4 + pb][:, 0:w], DTPB[:, et:et + 1], None, OP.add)
                    dtfs = DTF if pb == 0 else DTF2
                    kb.op("s", "activation", dtfs[:, 0:w],
                          HS[:, pb * 512:pb * 512 + w], AF.Abs,
                          waits=[kb.last("v")])
                    kb.op("s", "activation", dtfs[:, 0:w], dtfs[:, 0:w],
                          AF.Exp, scale=cc.kf(-1.0))
                    kb.op("s", "activation", dtfs[:, 0:w], dtfs[:, 0:w],
                          AF.Ln, bias=cc.kf(1.0))
                    sval = kb.last("s")
                    if pend_dt is not None:
                        pt0, pw, ppb, psv, pdt = pend_dt
                        kb.op("v", "tensor_scalar",
                              HS[:, ppb * 512:ppb * 512 + pw],
                              HS[:, ppb * 512:ppb * 512 + pw], 0.0, None,
                              OP.max, waits=[psv])
                        kb.op("v", "tensor_add", DTB[:, pt0:pt0 + pw],
                              HS[:, ppb * 512:ppb * 512 + pw], pdt[:, 0:pw])
                        kb.op("v", "tensor_mul", UB[:, pt0:pt0 + pw],
                              DTB[:, pt0:pt0 + pw],
                              XCT[:, xsl * T + pt0:xsl * T + pt0 + pw])
                    pend_dt = (t0, w, pb, sval, dtfs)
                pt0, pw, ppb, psv, pdt = pend_dt
                kb.op("v", "tensor_scalar", HS[:, ppb * 512:ppb * 512 + pw],
                      HS[:, ppb * 512:ppb * 512 + pw], 0.0, None,
                      OP.max, waits=[psv])
                kb.op("v", "tensor_add", DTB[:, pt0:pt0 + pw],
                      HS[:, ppb * 512:ppb * 512 + pw], pdt[:, 0:pw])
                kb.op("v", "tensor_mul", UB[:, pt0:pt0 + pw],
                      DTB[:, pt0:pt0 + pw],
                      XCT[:, xsl * T + pt0:xsl * T + pt0 + pw])
                wtb_free[xsl] = cc.tm.n
                if et + 1 < ET:
                    m3_pre(et + 1)
                kb.wait("t", *kb.last("v"))
                exp_done = [0, 0]
                dbu_done = [0, 0]
                scan_rd = [0, 0]
                hc_rd = [0, 0]
                y_rd = [0, 0]
                for ci in range(NCH):
                    t0 = ci * 512
                    w = min(512, T - t0)
                    pend_y = None
                    for g in range(16):
                        pb = g % 2
                        if exp_done[pb]:
                            kb.wait("t", kb.self_sem["s"], exp_done[pb])
                        kb.op("t", "matmul", P[4 + pb][:, 0:w],
                              SEL[:, g * 128:(g + 1) * 128], DTB[:, t0:t0 + w],
                              start=True, stop=True)
                        tv1 = kb.mark("t", cc.tm)
                        if dbu_done[pb]:
                            kb.wait("t", kb.self_sem["v"], dbu_done[pb])
                        kb.op("t", "matmul", P[6 + pb][:, 0:w],
                              SEL[:, g * 128:(g + 1) * 128], UB[:, t0:t0 + w],
                              start=True, stop=True)
                        tv2 = kb.mark("t", cc.tm)
                        if pend_y is not None:
                            gy, pby, hcy = pend_y
                            kb.wait("t", kb.self_sem["g"], hcy)
                            kb.op("t", "matmul", P[ci][:, 0:w],
                                  SELT[:, gy * 128:(gy + 1) * 128],
                                  HCB[:, pby * 512:pby * 512 + w],
                                  start=(gy == 0), stop=False)
                            y_rd[pby] = kb.mark("t", cc.tm)
                        kb.wait("s", cc.tm, tv1)
                        if scan_rd[pb]:
                            kb.wait("s", kb.self_sem["v"], scan_rd[pb])
                        kb.op("s", "activation", DA[:, pb * 512:pb * 512 + w],
                              P[4 + pb][:, 0:w], AF.Exp,
                              scale=ACOL[:, xsl * NST + g:xsl * NST + g + 1])
                        exp_done[pb] = kb.last("s")[1]
                        kb.wait("v", cc.tm, tv2)
                        kb.op("v", "tensor_tensor", DBU[:, pb * 512:pb * 512 + w],
                              P[6 + pb][:, 0:w], BREP[:, t0:t0 + w], OP.mult)
                        dbu_done[pb] = kb.last("v")[1]
                        init = 0.0 if ci == 0 else HSLAST[:, g:g + 1]
                        kb.op("v", "tensor_tensor_scan",
                              HS[:, pb * 512:pb * 512 + w],
                              DA[:, pb * 512:pb * 512 + w],
                              DBU[:, pb * 512:pb * 512 + w],
                              init, OP.mult, OP.add,
                              waits=([(kb.self_sem["s"], exp_done[pb])] +
                                     ([(kb.self_sem["g"], hc_rd[pb])]
                                      if hc_rd[pb] else [])))
                        scan_rd[pb] = kb.last("v")[1]
                        if ci < NCH - 1:
                            kb.op("v", "tensor_copy", HSLAST[:, g:g + 1],
                                  HS[:, pb * 512 + w - 1:pb * 512 + w])
                        gw = [(kb.self_sem["v"], kb.last("v")[1])]
                        if y_rd[pb]:
                            gw.append((cc.tm, y_rd[pb]))
                        kb.op("g", "tensor_tensor", HCB[:, pb * 512:pb * 512 + w],
                              HS[:, pb * 512:pb * 512 + w], CREP[:, t0:t0 + w],
                              OP.mult, waits=gw)
                        hc_rd[pb] = kb.last("g")[1]
                        pend_y = (g, pb, hc_rd[pb])
                    gy, pby, hcy = pend_y
                    kb.wait("t", kb.self_sem["g"], hcy)
                    kb.op("t", "matmul", P[ci][:, 0:w],
                          SELT[:, gy * 128:(gy + 1) * 128],
                          HCB[:, pby * 512:pby * 512 + w],
                          start=False, stop=True)
                    y_rd[pby] = kb.mark("t", cc.tm)
                zl = kb.dma("g", ZST[:, :], Z_DR[et], cc.dws,
                            waits=([(kb.self_sem["v"], prev_cmb)]
                                   if prev_cmb else []))
                kb.wait("v", cc.dws, zl)
                kb.wait("v", cc.tm, cc.tm.n)
                if spill_last:
                    kb.wait("v", cc.dout, spill_last)
                for ci in range(NCH):
                    t0 = ci * 512
                    w = min(512, T - t0)
                    kb.op("v", "scalar_tensor_tensor", U1[:, 0:w],
                          XCT[:, xsl * T + t0:xsl * T + t0 + w],
                          DCOL[:, et:et + 1], P[ci][:, 0:w],
                          OP.mult, OP.add)
                    kb.op("v", "tensor_mul", YF[:, t0:t0 + w], U1[:, 0:w],
                          ZST[:, t0:t0 + w])
                    spill_last = kb.dma("y", YF_DR[et][:, t0:t0 + w],
                                        YF[:, t0:t0 + w], cc.dout,
                                        waits=[kb.last("v")])
                prev_cmb = kb.last("v")[1]
                xct_free[xsl] = prev_cmb
                acol_free[xsl] = kb.self_sem["s"].n
                prev_exp = list(exp_done)
                prev_hc = max(hc_rd)
                prev_tm = cc.tm.n
            kb.barrier()
            kb.wait("g", cc.dout, cc.dout.n)

            WOALL = (HN if DT * T >= DT * ET * 128
                     else kb.sb("l0a_WOALL", [128, DT * ET * 128], BF16))
            for dc in range(DT):
                wo_ld = kb.dma("g", WOALL[:, dc * ET * 128:(dc + 1) * ET * 128],
                               WT_OUT[li, dc], cc.dws)
            kb.wait("t", cc.dws, wo_ld)
            NCH4 = (T + 511) // 512
            for h0 in range(0, NCH4, 2) if "m4" not in L0A_SKIP else ():
                t0h = h0 * 512
                wh = min(1024, T - t0h)
                nci = (wh + 511) // 512
                nsl = 2 if 2 * wh <= T else 1
                for dh in range(0, DT, 4):
                    yf_ld = [0, 0]
                    yf_used = [0, 0]
                    war0 = [(cc.tm, cc.tm.n)] if cc.tm.n else []
                    yf_ld[0] = kb.dma("g", YF[:, 0:wh],
                                      YF_DR[0][:, t0h:t0h + wh], cc.dws,
                                      waits=war0)
                    for et in range(ET):
                        sl = et % nsl

                        def pref(et=et):
                            ns = (et + 1) % nsl
                            war = ([(cc.tm, yf_used[ns])]
                                   if yf_used[ns] else [])
                            yf_ld[ns] = kb.dma(
                                "g", YF[:, ns * wh:ns * wh + wh],
                                YF_DR[et + 1][:, t0h:t0h + wh], cc.dws, waits=war)
                        if et + 1 < ET and nsl == 2:
                            pref()
                        kb.wait("t", cc.dws, yf_ld[sl])
                        for dcl in range(4):
                            dc = dh + dcl
                            for ci2 in range(nci):
                                w = min(512, wh - ci2 * 512)
                                kb.op("t", "matmul", P[dcl * 2 + ci2][:, 0:w],
                                      WOALL[:, (dc * ET + et) * 128:
                                             (dc * ET + et + 1) * 128],
                                      YF[:, sl * wh + ci2 * 512:
                                          sl * wh + ci2 * 512 + w],
                                      start=(et == 0), stop=(et == ET - 1))
                        yf_used[sl] = kb.mark("t", cc.tm)
                        if et + 1 < ET and nsl == 1:
                            pref()
                    kb.wait("v", cc.tm, cc.tm.n)
                    for dcl in range(4):
                        for ci2 in range(nci):
                            w = min(512, wh - ci2 * 512)
                            sl2 = slice((dh + dcl) * T + t0h + ci2 * 512,
                                        (dh + dcl) * T + t0h + ci2 * 512 + w)
                            kb.op("v", "tensor_tensor", X[:, sl2], X[:, sl2],
                                  P[dcl * 2 + ci2][:, 0:w], OP.add)
                    kb.wait("t", *kb.last("v"))
            kb.barrier()

        def apply_m5(t0, w):
            xi = kb.dma("y", XI[:, 0:DT * w].rearrange("p (k t) -> p k t", k=DT),
                        HD["xin"][b].rearrange("(k p) t -> p k t", p=128)[:, :, t0:t0 + w],
                        cc.dsem, waits=[(cc.dout, cc.dout.n)])
            kb.wait("v", cc.dsem, xi)
            for k in range(DT):
                sl = slice(k * T + t0, k * T + t0 + w)
                kb.op("v", "tensor_mul", U1[:, 0:w], X[:, sl], cc.RS[:, 0:w])
                kb.op("v", "tensor_add", U1[:, 0:w], U1[:, 0:w], cc.NMR[:, 0:w])
                kb.op("v", "tensor_scalar", U1[:, 0:w], U1[:, 0:w],
                      LNW[:, 2 * DT + k:2 * DT + k + 1],
                      LNB[:, 2 * DT + k:2 * DT + k + 1], OP.mult, OP.add)
                kb.op("v", "tensor_add", XI[:, k * w:(k + 1) * w],
                      XI[:, k * w:(k + 1) * w], U1[:, 0:w])
            so = kb.dma("y", HD["xout"][b].rearrange("(k p) t -> p k t", p=128)[:, :, t0:t0 + w],
                        XI[:, 0:DT * w].rearrange("p (k t) -> p k t", k=DT),
                        cc.dout, waits=[kb.last("v")])
            kb.wait("v", cc.dout, so)
        cc.ln_chunked(cc.cast_tile_cb(X), apply_m5)
        kb.wait("y", cc.dout, cc.dout.n)
        kb.barrier()


def build_l0a(T, nbatch):
    nc = bass.Bass("TRN2", target_bir_lowering=False)
    HD = {}
    HD["xin"] = nc.dram_tensor("xin", [nbatch, D, T], F32, kind="ExternalInput").ap()
    HD["xout"] = nc.dram_tensor("xout", [nbatch, D, T], F32, kind="ExternalOutput").ap()
    for nm, shp in [("L0_attn_ln_w", [2, D]), ("L0_attn_ln_b", [2, D]),
                    ("L0_attn_in_proj_w", [2, 2 * E_IN, D]),
                    ("L0_attn_conv_w", [2, E_IN, 4]), ("L0_attn_conv_b", [2, E_IN]),
                    ("L0_attn_x_proj_w", [2, 96, E_IN]),
                    ("L0_attn_dt_proj_w", [2, E_IN, RNK]),
                    ("L0_attn_dt_proj_b", [2, E_IN]),
                    ("L0_attn_D_skip", [2, E_IN]),
                    ("L0_attn_out_proj_w", [2, D, E_IN]),
                    ("L0_attn_norm_f_w", [D]), ("L0_attn_norm_f_b", [D]),
                    ("L0A_Alog_r", [2, 16, 128, 16]),
                    ("c_sel", [128, 16 * 128]), ("c_selt", [128, 16 * 128]),
                    ("c_s16", [16, 128])]:
        dt = BF16 if nm in ("c_sel", "c_selt", "c_s16") else F32
        HD[nm] = nc.dram_tensor(nm, shp, dt, kind="ExternalInput")
    kb = KB(nc)
    cc = CC(kb, T)
    emit_l0a_vectors(kb, cc, HD)
    emit_l0a(kb, cc, HD, nbatch)
    kb.finalize()
    return nc


def emit_l0a_vectors(kb, cc, HD):
    pass


def l0a_consts():
    sel = np.zeros((128, 16, 128), np.float32)
    selt = np.zeros((128, 16, 128), np.float32)
    for g in range(16):
        for m in range(128):
            sel[8 * g + (m // 16), g, m] = 1.0
        for p in range(128):
            selt[p, g, 8 * g + (p // 16)] = 1.0
    s16 = np.zeros((16, 128), np.float32)
    for p in range(128):
        s16[p % 16, p] = 1.0
    return {
        "c_sel": sel.reshape(128, 16 * 128).astype(ml_dtypes.bfloat16),
        "c_selt": selt.reshape(128, 16 * 128).astype(ml_dtypes.bfloat16),
        "c_s16": s16.astype(ml_dtypes.bfloat16),
    }




def emit_l1f_a(kb, cc, HD, nbatch):
    T, TC = cc.T, cc.TC
    P = cc.P
    nc = kb.nc
    X = kb.sb("f1a_X", [128, DT * T], F32)
    ABF = kb.sb("f1a_ABF", [128, 2 * DT * T], BF16)
    XN = ABF[:, 0:DT * T]
    HSH = ABF[:, DT * T:2 * DT * T]
    WTB = kb.sb("f1a_WTB", [128, DT * 128], BF16)
    WGT = kb.sb("f1a_WGT", [128, DT * 8], BF16)
    TSTG = kb.sb("f1a_TSTG", [128, 1024], BF16)
    U1 = kb.sb("f1a_U1", [128, 512], F32)
    GP = kb.sb("f1a_GP", [128, 8], F32)
    GS = kb.sb("f1a_GS", [128, 4], F32)
    GM = kb.sb("f1a_GM", [128, 2], F32)
    KX = kb.sb("f1a_KX", [128, 16], F32)
    OH = kb.sb("f1a_OH", [128, 20], F32)
    OHB = kb.sb("f1a_OHB", [128, 16], BF16)
    BB = kb.sb("f1a_BB", [128, 8], F32)
    CNT = kb.sb("f1a_CNT", [1, 8], F32)
    EBR = kb.sb("f1a_EBR", [1, 8], F32)

    WT_S1 = kb.dram("f1a_wt_s1", [DT, 128, DT * 128], BF16)
    WT_S3 = kb.dram("f1a_wt_s3", [DT, 128, DT * 128], BF16)
    WT_S2 = kb.dram("f1a_wt_s2", [DT, 128, DT * 128], BF16)

    kb.op("g", "dma_start", out=EBR[:, :],
          in_=HD["L1_ffn_expert_bias"].ap().rearrange("(o n) -> o n", o=1),
          incs=[(cc.dgi, 16)], selfsync=False)
    kb.barrier()
    for e in ("v", "s", "t", "g"):
        kb.wait(e, cc.dsem, cc.dsem.n)
        kb.wait(e, cc.dgi, cc.dgi.n)

    PT = P[7][:, :].bitcast(BF16)
    tprev = 0
    for (wsrc, wdst) in ((HD["L1_ffn_sw1"], WT_S1), (HD["L1_ffn_sw3"], WT_S3),
                         (HD["L1_ffn_sw2"], WT_S2)):
        wr = wsrc.ap().rearrange("(oc p) d -> oc p d", p=128)
        for oc in range(DT):
            if tprev:
                kb.wait("g", cc.tm, tprev)
            wl = kb.dma("g", TSTG[:, 0:1024], wr[oc], cc.dws)
            kb.wait("t", cc.dws, wl)
            for k in range(DT):
                kb.op("t", "transpose", PT[:, k * 128:(k + 1) * 128],
                      TSTG[:, k * 128:(k + 1) * 128], cc.IDENT[:, :])
            tprev = kb.mark("t", cc.tm)
            kb.wait("v", cc.tm, tprev)
            kb.op("v", "tensor_copy", WTB[:, :], PT[:, 0:1024])
            kb.wait("t", *kb.last("v"))
            sv = kb.dma("y", wdst[oc], WTB[:, :], cc.dout, waits=[kb.last("v")])
            kb.wait("v", cc.dout, sv)
    kb.wait("g", cc.tm, tprev)
    wl = kb.dma("g", TSTG[0:8, 0:1024], HD["L1_ffn_w_gate"][:, :], cc.dws)
    kb.wait("t", cc.dws, wl)
    for k in range(DT):
        kb.op("t", "transpose", PT[:, k * 8:(k + 1) * 8],
              TSTG[0:8, k * 128:(k + 1) * 128], cc.IDENT[0:8, 0:8])
        tv = kb.mark("t", cc.tm)
    kb.wait("v", cc.tm, cc.tm.n)
    kb.op("v", "tensor_copy", WGT[:, :], PT[:, 0:DT * 8])
    kb.wait("t", *kb.last("v"))
    kb.op("t", "matmul", P[6][:, 0:8], cc.ONES_ROW32[0:1, :], EBR[0:1, :],
          start=True, stop=True)
    tv = kb.mark("t", cc.tm)
    kb.wait("v", cc.tm, tv)
    kb.op("v", "tensor_copy", BB[:, :], P[6][:, 0:8])
    kb.op("v", "memset", CNT[:, :], 0.0)
    kb.op("v", "memset", OH[:, 18:20], 0.0)
    kb.wait("t", *kb.last("v"))
    kb.barrier()

    cntv = 0
    for b in range(nbatch):
        xld = kb.dma("y", X[:, :].rearrange("p (k t) -> p k t", k=DT),
                     HD["xin"][b].rearrange("(k p) t -> p k t", p=128), cc.dsem)
        kb.wait("v", cc.dsem, xld)
        kb.wait("s", cc.dsem, xld)

        def apply_a(t0, w):
            for k in range(DT):
                sl = slice(k * T + t0, k * T + t0 + w)
                kb.op("v", "tensor_mul", U1[:, 0:w], X[:, sl], cc.RS[:, 0:w])
                kb.op("v", "tensor_add", XN[:, sl], U1[:, 0:w], cc.NMR[:, 0:w])
        cc.ln_chunked(cc.cast_tile_cb(X), apply_a)
        kb.barrier()
        if b == 0:
            x8 = kb.dma("y", HD["xf8"].rearrange("p (k c) -> p k c", c=8),
                        XN[:, :].rearrange("p (k t) -> p k t", t=T)[:, :, 0:8],
                        cc.dout, waits=[kb.last("v")])

        for tc in range(TC):
            t0 = tc * 128
            for k in range(DT):
                kb.op("t", "matmul", P[6][:, 0:8],
                      XN[:, k * T + t0:k * T + t0 + 128], WGT[:, k * 8:(k + 1) * 8],
                      start=(k == 0), stop=(k == DT - 1))
            tv = kb.mark("t", cc.tm)
            kb.wait("v", cc.tm, tv)
            kb.op("v", "tensor_reduce", GM[:, 0:1], P[6][:, 0:8], AX.X, OP.max)
            kb.op("v", "tensor_scalar", GM[:, 0:1], GM[:, 0:1], -1.0, None, OP.mult)
            kb.op("s", "activation", GP[:, 0:8], P[6][:, 0:8], AF.Exp,
                  bias=GM[:, 0:1], waits=[kb.last("v")])
            kb.wait("t", *kb.last("s"))
            kb.op("v", "tensor_reduce", GM[:, 1:2], GP[:, 0:8], AX.X, OP.add,
                  waits=[kb.last("s")])
            kb.op("v", "reciprocal", GM[:, 1:2], GM[:, 1:2])
            kb.op("v", "tensor_scalar", GP[:, 0:8], GP[:, 0:8], GM[:, 1:2],
                  None, OP.mult)
            kb.op("v", "tensor_add", OH[:, 0:8], GP[:, 0:8], BB[:, 0:8])
            sa = OH[:, 0:8].rearrange("p (g two) -> p g two", two=2)
            kb.op("v", "tensor_add", GS[:, 0:4], sa[:, :, 0], sa[:, :, 1])
            kb.op("v", "tensor_reduce", GM[:, 0:1], GS[:, 0:4], AX.X, OP.max)
            kb.op("v", "tensor_scalar", KX[:, 0:4], GS[:, 0:4], GM[:, 0:1],
                  None, OP.is_equal)
            kb.op("v", "scalar_tensor_tensor", KX[:, 0:4], KX[:, 0:4], -1e30,
                  GS[:, 0:4], OP.mult, OP.add)
            kb.op("v", "tensor_reduce", GM[:, 1:2], KX[:, 0:4], AX.X, OP.max)
            kb.op("v", "tensor_scalar", KX[:, 0:4], GS[:, 0:4], GM[:, 1:2],
                  None, OP.is_ge)
            kb.op("v", "tensor_copy",
                  KX[:, 8:16].rearrange("p (g two) -> p g two", two=2),
                  KX[:, 0:4].rearrange("p (g o) -> p g o", o=1).broadcast_to(
                      [128, 4, 2]))
            kb.op("v", "tensor_scalar", KX[:, 8:16], KX[:, 8:16], 1.0, None,
                  OP.subtract)
            kb.op("v", "scalar_tensor_tensor", KX[:, 8:16], KX[:, 8:16], 1e30,
                  OH[:, 0:8], OP.mult, OP.add)
            kb.op("v", "tensor_reduce", GM[:, 0:1], KX[:, 8:16], AX.X, OP.max)
            kb.op("v", "tensor_scalar", OH[:, 0:8], KX[:, 8:16], GM[:, 0:1],
                  None, OP.is_equal)
            kb.op("v", "scalar_tensor_tensor", KX[:, 8:16], OH[:, 0:8], -1e30,
                  KX[:, 8:16], OP.mult, OP.add)
            kb.op("v", "tensor_reduce", GM[:, 1:2], KX[:, 8:16], AX.X, OP.max)
            kb.op("v", "tensor_scalar", OH[:, 8:16], KX[:, 8:16], GM[:, 1:2],
                  None, OP.is_equal)
            kb.op("v", "scalar_tensor_tensor", KX[:, 0:8], OH[:, 0:8], 1.0,
                  GP[:, 0:8], OP.mult, OP.mult, accum_out=OH[:, 16:17])
            kb.op("v", "scalar_tensor_tensor", KX[:, 0:8], OH[:, 8:16], 1.0,
                  GP[:, 0:8], OP.mult, OP.mult, accum_out=OH[:, 17:18])
            kb.op("v", "tensor_add", KX[:, 0:8], OH[:, 0:8], OH[:, 8:16])
            kb.op("v", "tensor_copy", OHB[:, 0:8], KX[:, 0:8])
            kb.wait("t", *kb.last("v"))
            kb.op("t", "matmul", P[5][0:1, 0:8], cc.ONES_COL[:, :], OHB[:, 0:8],
                  start=(b == 0 and tc == 0), stop=(b == nbatch - 1 and tc == TC - 1))
            cntv = kb.mark("t", cc.tm)
            sv = kb.dma("y", HD["ohw"][b, tc], OH[:, :], cc.dout,
                        waits=[kb.last("v")])
            kb.wait("v", cc.dout, sv)

        w1free = 0
        for hc in range(DT):
            if w1free:
                kb.wait("g", cc.tm, w1free)
            l1 = kb.dma("g", WTB[:, :], WT_S1[hc], cc.dws)
            l3 = kb.dma("g", TSTG[:, 0:1024], WT_S3[hc], cc.dws)
            kb.wait("t", cc.dws, l3)
            for t0 in range(0, T, 512):
                w = min(512, T - t0)
                pb = (t0 // 512) % 2
                for k in range(DT):
                    kb.op("t", "matmul", P[pb][:, 0:w], WTB[:, k * 128:(k + 1) * 128],
                          XN[:, k * T + t0:k * T + t0 + w],
                          start=(k == 0), stop=(k == DT - 1))
                for k in range(DT):
                    kb.op("t", "matmul", P[2 + pb][:, 0:w],
                          TSTG[:, k * 128:(k + 1) * 128],
                          XN[:, k * T + t0:k * T + t0 + w],
                          start=(k == 0), stop=(k == DT - 1))
                tv = kb.mark("t", cc.tm)
                kb.wait("s", cc.tm, tv)
                kb.op("s", "activation", U1[:, 0:w], P[pb][:, 0:w], AF.Silu)
                kb.op("v", "tensor_tensor", HSH[:, hc * T + t0:hc * T + t0 + w],
                      U1[:, 0:w], P[2 + pb][:, 0:w], OP.mult,
                      waits=[kb.last("s")])
                kb.wait("t", *kb.last("v"))
            w1free = cc.tm.n
        for oc in range(DT):
            kb.wait("g", cc.tm, w1free)
            wl = kb.dma("g", WTB[:, :], WT_S2[oc], cc.dws)
            kb.wait("t", cc.dws, wl)
            for t0 in range(0, T, 512):
                w = min(512, T - t0)
                pb = (t0 // 512) % 2
                for k in range(DT):
                    kb.op("t", "matmul", P[pb][:, 0:w], WTB[:, k * 128:(k + 1) * 128],
                          HSH[:, k * T + t0:k * T + t0 + w],
                          start=(k == 0), stop=(k == DT - 1))
                tv = kb.mark("t", cc.tm)
                kb.wait("v", cc.tm, tv)
                sl = slice(oc * T + t0, oc * T + t0 + w)
                kb.op("v", "scalar_tensor_tensor", X[:, sl], P[pb][:, 0:w], 0.5,
                      X[:, sl], OP.mult, OP.add)
                kb.wait("t", *kb.last("v"))
            w1free = cc.tm.n
        so = kb.dma("y", HD["xout"][b].rearrange("(k p) t -> p k t", p=128),
                    X[:, :].rearrange("p (k t) -> p k t", k=DT), cc.dout,
                    waits=[kb.last("v")])
        kb.wait("y", cc.dout, so)
        kb.barrier()
    kb.wait("v", cc.tm, cntv)
    kb.op("v", "tensor_copy", CNT[:, :], P[5][0:1, 0:8])
    sv = kb.dma("y", HD["cnt8"][:, :], CNT[:, :], cc.dout, waits=[kb.last("v")])
    kb.wait("y", cc.dout, sv)


def build_l1f_a(T, nbatch):
    nc = bass.Bass("TRN2", target_bir_lowering=False)
    HD = {}
    HD["xin"] = nc.dram_tensor("xin", [nbatch, D, T], F32, kind="ExternalInput").ap()
    HD["xout"] = nc.dram_tensor("xout", [nbatch, D, T], F32, kind="ExternalOutput").ap()
    HD["cnt8"] = nc.dram_tensor("cnt8", [1, 8], F32, kind="ExternalOutput")
    HD["ohw"] = nc.dram_tensor("ohw", [nbatch, T // 128, 128, 20], F32,
                               kind="ExternalOutput").ap()
    HD["xf8"] = nc.dram_tensor("xf8", [128, DT * 8], BF16, kind="ExternalOutput").ap()
    for nm, shp in [("L1_ffn_w_gate", [8, D]), ("L1_ffn_sw1", [D, D]),
                    ("L1_ffn_sw2", [D, D]), ("L1_ffn_sw3", [D, D]),
                    ("L1_ffn_expert_bias", [8])]:
        HD[nm] = nc.dram_tensor(nm, shp, F32, kind="ExternalInput")
    kb = KB(nc)
    cc = CC(kb, T)
    emit_l1f_a(kb, cc, HD, nbatch)
    kb.finalize()
    return nc



def emit_l1f_b(kb, cc, HD, nbatch):
    T, TC = cc.T, cc.TC
    P = cc.P
    nc = kb.nc
    CNT = kb.sb("f1b_CNT", [8, 8], F32)
    GCC = kb.sb("f1b_GCC", [8, 1], F32)
    OFS = kb.sb("f1b_OFS", [1, 8], F32)
    OFB = kb.sb("f1b_OFB", [128, 8], F32)
    ONE8 = kb.sb("f1b_ONE8", [8, 1], F32)
    UT8 = kb.sb("f1b_UT8", [8, 8], F32)
    XF8 = kb.sb("f1b_XF8", [128, DT * 8], BF16)
    TB1 = kb.sb("f1b_TB1", [8, 8 * 512], F32)
    TB3 = kb.sb("f1b_TB3", [8, 8 * 512], F32)
    T64A = kb.sb("f1b_T64A", [64, 512], BF16)
    T64B = kb.sb("f1b_T64B", [64, 512], BF16)
    WE = kb.sb("f1b_WE", [128, 16 * 512], BF16)
    OH = kb.sb("f1b_OH", [128, 20], F32)
    JA = kb.sb("f1b_JA", [128, 2 * TC], F32)
    GA = kb.sb("f1b_GA", [128, 8], F32)
    CMP = kb.sb("f1b_CMP", [128, 8], F32)
    E64 = kb.sb("f1b_E64", [128, 64], BF16)
    E64T = kb.sb("f1b_E64T", [64, 128], BF16)
    HB = kb.sb("f1b_HB", [128, 512], BF16)
    T1 = kb.sb("f1b_T1", [128, 512], BF16)
    T2 = kb.sb("f1b_T2", [128, 512], BF16)
    HP = kb.sb("f1b_HP", [64, 512], F32)
    TDR1 = kb.dram("f1b_tdr1", [8, 8, 512], F32)
    TDR3 = kb.dram("f1b_tdr3", [8, 8, 512], F32)

    if "cnts_global" in HD:
        kb.dma("y", OFS[:, :], HD["cnts_global"][:, :], cc.dsem)
    else:
        kb.dma("y", CNT[:, :], HD["cnts"][:, :], cc.dsem)
    kb.dma("g", XF8[:, :], HD["xf8"][:, :], cc.dgi)
    kb.dma("y", ONE8[:, :], HD["c_one8"][:, :], cc.dsem)
    kb.dma("y", UT8[:, :], HD["c_ut8"][:, :], cc.dsem)
    kb.dma("y", JA[:, 0:nbatch * TC], HD["c_ja"][:, 0:nbatch * TC], cc.dsem)
    kb.barrier()
    for e in ("v", "s", "t", "g"):
        kb.wait(e, cc.dsem, cc.dsem.n)
        kb.wait(e, cc.dgi, cc.dgi.n)

    if "cnts_global" in HD:
        kb.op("t", "transpose", P[0][0:8, 0:1], OFS[0:1, 0:8],
              cc.ONES_ROW32[0:1, 0:1])
    else:
        kb.op("t", "matmul", P[0][0:8, 0:1], CNT[:, :], ONE8[:, :], start=True,
              stop=True)
    tv = kb.mark("t", cc.tm)
    kb.wait("v", cc.tm, tv)
    kb.op("v", "tensor_copy", GCC[:, :], P[0][0:8, 0:1])
    kb.wait("t", *kb.last("v"))
    kb.op("t", "matmul", P[0][0:1, 0:8], GCC[:, :], UT8[:, :], start=True,
          stop=True)
    tv = kb.mark("t", cc.tm)
    kb.wait("v", cc.tm, tv)
    kb.op("v", "tensor_copy", OFS[:, :], P[0][0:1, 0:8])
    kb.wait("t", *kb.last("v"))
    kb.op("t", "matmul", P[0][:, 0:8], cc.ONES_ROW32[0:1, :], OFS[0:1, :],
          start=True, stop=True)
    tv = kb.mark("t", cc.tm)
    kb.wait("v", cc.tm, tv)
    kb.op("v", "tensor_copy", OFB[:, :], P[0][:, 0:8])
    kb.wait("t", *kb.last("v"))

    we_free = 0
    for g in range(8):
        if we_free:
            kb.wait("g", cc.tm, we_free)
        wl = kb.dma("g", WE[:, 0:8 * 512].rearrange("p (k h) -> p k h", h=512),
                    HD["L1_ffn_w1"][g].rearrange("(k p) h -> p k h", p=128),
                    cc.dws)
        wl3 = kb.dma("g", WE[:, 8 * 512:16 * 512].rearrange("p (k h) -> p k h", h=512),
                     HD["L1_ffn_w3"][g].rearrange("(k p) h -> p k h", p=128),
                     cc.dws)
        kb.wait("t", cc.dws, wl3)
        if g > 0:
            kb.wait("t", *kb.last("v"))
        for k in range(DT):
            kb.op("t", "matmul", P[0][0:8, 0:512], XF8[:, k * 8:(k + 1) * 8],
                  WE[:, k * 512:(k + 1) * 512], start=(k == 0), stop=(k == DT - 1))
        for k in range(DT):
            kb.op("t", "matmul", P[1][0:8, 0:512], XF8[:, k * 8:(k + 1) * 8],
                  WE[:, (8 + k) * 512:(9 + k) * 512], start=(k == 0),
                  stop=(k == DT - 1))
        tv = kb.mark("t", cc.tm)
        we_free = tv
        kb.wait("v", cc.tm, tv)
        kb.op("v", "tensor_copy", TB1[0:8, g * 512:(g + 1) * 512], P[0][0:8, 0:512])
        kb.op("v", "tensor_copy", TB3[0:8, g * 512:(g + 1) * 512], P[1][0:8, 0:512])
    sv = kb.dma("y", TDR1.ap(), TB1[0:8, :].rearrange("e (g h) -> e g h", h=512),
                cc.dout, waits=[kb.last("v")])
    sv = kb.dma("y", TDR3.ap(), TB3[0:8, :].rearrange("e (g h) -> e g h", h=512),
                cc.dout)
    l1 = kb.dma("g", T64A[:, :], TDR1.ap().rearrange("e g h -> (e g) h"), cc.dgi,
                waits=[(cc.dout, sv)])
    l2 = kb.dma("g", T64B[:, :], TDR3.ap().rearrange("e g h -> (e g) h"), cc.dgi)
    kb.barrier()
    for e in ("v", "s", "t", "g"):
        kb.wait(e, cc.dsem, cc.dsem.n)
        kb.wait(e, cc.dgi, cc.dgi.n)

    PTB = P[7][:, :].bitcast(BF16)
    first = True
    for b in range(nbatch):
        for tc in range(TC):
            ol = kb.dma("y", OH[:, :], HD["ohw"][b, tc], cc.dsem,
                        waits=[kb.last("v")] if not first else ())
            kb.wait("v", cc.dsem, ol)
            jcol = JA[:, b * TC + tc:b * TC + tc + 1]
            for slot in range(2):
                if not first:
                    kb.wait("v", cc.tm, cc.tm.n)
                if slot == 0:
                    kb.op("v", "tensor_scalar", CMP[:, 0:8], OFB[:, :], jcol,
                          None, OP.is_gt)
                else:
                    kb.op("v", "tensor_scalar", CMP[:, 0:8], OFB[:, :], jcol,
                          1.0, OP.subtract, OP.is_gt)
                kb.op("v", "tensor_copy", GA[:, 0:1], CMP[:, 0:1])
                kb.op("v", "tensor_sub", GA[:, 1:8], CMP[:, 1:8], CMP[:, 0:7])
                oh = OH[:, 0:8] if slot == 0 else OH[:, 8:16]
                wa = OH[:, 16:17] if slot == 0 else OH[:, 17:18]
                kb.op("v", "tensor_mul", E64[:, :].rearrange("p (e g) -> p e g", g=8),
                      oh.rearrange("p (e o) -> p e o", o=1).broadcast_to([128, 8, 8]),
                      GA[:, 0:8].rearrange("p (o g) -> p o g", o=1).broadcast_to([128, 8, 8]))
                kb.wait("t", *kb.last("v"))
                kb.op("t", "transpose", PTB[:, 0:128][0:64, :] if False else PTB[0:64, 0:128],
                      E64[:, :], cc.IDENT[:, :])
                tv = kb.mark("t", cc.tm)
                kb.wait("v", cc.tm, tv)
                kb.op("v", "tensor_copy", E64T[:, :], PTB[0:64, 0:128])
                kb.wait("t", *kb.last("v"))
                kb.op("t", "matmul", P[0][:, 0:512], E64T[:, :], T64A[:, :],
                      start=True, stop=True)
                kb.op("t", "matmul", P[1][:, 0:512], E64T[:, :], T64B[:, :],
                      start=True, stop=True)
                tv = kb.mark("t", cc.tm)
                kb.wait("s", cc.tm, tv)
                kb.op("s", "activation", T1[:, :], P[0][:, 0:512], AF.Silu,
                      scale=wa)
                kb.wait("v", cc.tm, tv)
                kb.op("v", "tensor_scalar", T2[:, :], P[1][:, 0:512], wa, None,
                      OP.mult)
                kb.op("v", "tensor_tensor", HB[:, :], T1[:, :], T2[:, :],
                      OP.mult, waits=[kb.last("s")])
                kb.wait("t", *kb.last("v"))
                kb.op("t", "matmul", P[2][0:64, 0:512], E64[:, :], HB[:, :],
                      start=first, stop=(b == nbatch - 1 and tc == TC - 1 and
                                         slot == 1))
                kb.mark("t", cc.tm)
                first = False
    kb.wait("v", cc.tm, cc.tm.n)
    kb.op("v", "tensor_copy", HP[:, :], P[2][0:64, 0:512])
    sv = kb.dma("y", HD["hpart"][:, :], HP[:, :], cc.dout, waits=[kb.last("v")])
    kb.wait("y", cc.dout, sv)


def build_l1f_b(T, nbatch):
    nc = bass.Bass("TRN2", target_bir_lowering=False)
    HD = {}
    TC = T // 128
    HD["cnts"] = nc.dram_tensor("cnts", [8, 8], F32, kind="ExternalInput")
    HD["xf8"] = nc.dram_tensor("xf8", [128, DT * 8], BF16, kind="ExternalInput")
    HD["ohw"] = nc.dram_tensor("ohw", [nbatch, TC, 128, 20], F32,
                               kind="ExternalInput").ap()
    HD["c_one8"] = nc.dram_tensor("c_one8", [8, 1], F32, kind="ExternalInput")
    HD["c_ut8"] = nc.dram_tensor("c_ut8", [8, 8], F32, kind="ExternalInput")
    HD["c_ja"] = nc.dram_tensor("c_ja", [128, nbatch * TC], F32,
                                kind="ExternalInput")
    HD["L1_ffn_w1"] = nc.dram_tensor("L1_ffn_w1", [8, D, 512], F32,
                                     kind="ExternalInput").ap()
    HD["L1_ffn_w3"] = nc.dram_tensor("L1_ffn_w3", [8, D, 512], F32,
                                     kind="ExternalInput").ap()
    HD["hpart"] = nc.dram_tensor("hpart", [64, 512], F32, kind="ExternalOutput")
    kb = KB(nc)
    cc = CC(kb, T)
    emit_l1f_b(kb, cc, HD, nbatch)
    kb.finalize()
    return nc


def emit_l1f_c(kb, cc, HD, nbatch):
    T = cc.T
    P = cc.P
    nc = kb.nc
    HPA = kb.sb("f1c_HPA", [64, 512], F32)
    HPS = kb.sb("f1c_HPS", [64, 512], F32)
    HGT = kb.sb("f1c_HGT", [128, 4 * 8], BF16)
    W2 = kb.sb("f1c_W2", [128, 4 * 1024], BF16)
    RR = kb.sb("f1c_RR", [8, 1024], F32)
    RRB = kb.sb("f1c_RRB", [8, 1024], BF16)
    UPT = kb.sb("f1c_UPT", [128, DT * 8], F32)
    X8 = kb.sb("f1c_X8", [128, DT * 8], F32)
    MSK = kb.sb("f1c_MSK", [8, 1], F32)
    HDR = kb.dram("f1c_hdr", [64, 512], F32)

    kb.dma("y", MSK[:, :], HD["maskval"][:, :], cc.dsem)
    kb.barrier()
    for e in ("v", "s", "t", "g"):
        kb.wait(e, cc.dsem, cc.dsem.n)

    if "hpart_global" in HD:
        l = kb.dma("y", HPS[:, :], HD["hpart_global"][:, :], cc.dsem,
                   waits=[kb.last("v")])
        kb.wait("v", cc.dsem, l)
        kb.op("v", "tensor_copy", HPS[:, :], HPS[:, :])
    else:
        for c in range(8):
            l = kb.dma("y", HPA[:, :], HD["hparts"][c], cc.dsem,
                       waits=[kb.last("v")] if c else ())
            kb.wait("v", cc.dsem, l)
            if c == 0:
                kb.op("v", "tensor_copy", HPS[:, :], HPA[:, :])
            else:
                kb.op("v", "tensor_add", HPS[:, :], HPS[:, :], HPA[:, :])
    sv_hdr = kb.dma("y", HDR.ap(), HPS[:, :], cc.dout, waits=[kb.last("v")])
    for g in range(8):
        hsrc = HDR.ap().rearrange("(e g) (q p) -> g p q e", g=8, p=128)[g]
        for q in range(4):
            l = kb.dma("g", HGT[:, q * 8:(q + 1) * 8], hsrc[:, q, :],
                       cc.dws, allow_slow_non_contiguous=True,
                       waits=([(cc.tm, cc.tm.n)] if (g and q == 0) else []) +
                             ([(cc.dout, sv_hdr)] if (g == 0 and q == 0) else []))
        wl = kb.dma("g", W2[:, :].rearrange("p (q n) -> p q n", n=1024),
                    HD["L1_ffn_w2"][g].rearrange("(q p) n -> p q n", p=128),
                    cc.dws)
        kb.wait("t", cc.dws, wl)
        for q in range(4):
            for half in range(2):
                kb.op("t", "matmul", P[half][0:8, 0:512],
                      HGT[:, q * 8:(q + 1) * 8],
                      W2[:, q * 1024 + half * 512:q * 1024 + (half + 1) * 512],
                      start=(g == 0 and q == 0), stop=(g == 7 and q == 3))
        kb.mark("t", cc.tm)
    kb.wait("v", cc.tm, cc.tm.n)
    kb.op("v", "tensor_scalar", RR[:, 0:512], P[0][0:8, 0:512],
          MSK[:, 0:1], None, OP.mult)
    kb.op("v", "tensor_scalar", RR[:, 512:1024], P[1][0:8, 0:512],
          MSK[:, 0:1], None, OP.mult)
    kb.op("v", "tensor_copy", RRB[:, :], RR[:, :])
    kb.wait("t", *kb.last("v"))
    PTB = P[7][:, :].bitcast(BF16)
    for k in range(DT):
        kb.op("t", "transpose", PTB[:, k * 8:(k + 1) * 8],
              RRB[:, k * 128:(k + 1) * 128], cc.IDENT[0:8, 0:8])
    tv = kb.mark("t", cc.tm)
    kb.wait("v", cc.tm, tv)
    kb.op("v", "tensor_copy", UPT[:, :], PTB[:, 0:DT * 8])
    if not HD.get("inplace"):
        for b in range(nbatch):
            kb.dma("y", HD["xout"][b], HD["xa"][b], cc.dout)
    x8l = kb.dma("y", X8[:, :].rearrange("p (k c) -> p k c", c=8),
                 HD["xa"][0].rearrange("(k p) t -> p k t", p=128)[:, :, 0:8],
                 cc.dsem)
    kb.wait("v", cc.dsem, x8l)
    kb.op("v", "tensor_add", X8[:, :], X8[:, :], UPT[:, :])
    sv = kb.dma("y", HD["xout"][0].rearrange("(k p) t -> p k t", p=128)[:, :, 0:8],
                X8[:, :].rearrange("p (k c) -> p k c", c=8), cc.dout,
                waits=[kb.last("v")])
    kb.wait("y", cc.dout, sv)


def build_l1f_c(T, nbatch):
    nc = bass.Bass("TRN2", target_bir_lowering=False)
    HD = {}
    HD["xa"] = nc.dram_tensor("xa", [nbatch, D, T], F32, kind="ExternalInput").ap()
    HD["xout"] = nc.dram_tensor("xout", [nbatch, D, T], F32,
                                kind="ExternalOutput").ap()
    HD["hparts"] = nc.dram_tensor("hparts", [8, 64, 512], F32,
                                  kind="ExternalInput").ap()
    HD["maskval"] = nc.dram_tensor("maskval", [8, 1], F32, kind="ExternalInput")
    HD["L1_ffn_w2"] = nc.dram_tensor("L1_ffn_w2", [8, 512, D], F32,
                                     kind="ExternalInput").ap()
    kb = KB(nc)
    cc = CC(kb, T)
    emit_l1f_c(kb, cc, HD, nbatch)
    kb.finalize()
    return nc



def emit_l1a(kb, cc, HD, nbatch):
    T, TC = cc.T, cc.TC
    P = cc.P
    nc = kb.nc
    NH, DK, DV = 8, 128, 128
    NCH = T // 16
    NG = NCH // 8

    XNO = kb.sb("l1a_XNO", [128, DT * T], BF16)
    XN = XNO
    OF = XNO
    OALL = kb.sb("l1a_OALL", [128, DT * T], BF16)
    if DT * T * 2 >= DT * 512 * 4:
        XCHK = OALL[:, :].bitcast(F32)[:, 0:DT * 512]
    else:
        XCHK = kb.sb("l1a_XCHK", [128, DT * 512], F32)[:, :]
    WPRJ = kb.sb("l1a_WPRJ", [128, 4 * DT * 128], BF16)
    WOC = WPRJ[:, 0:2 * DT * 128]
    WA = kb.sb("l1a_WA", [128, DT * 8], BF16)

    QT = kb.sb("l1a_QT", [128, T], BF16)
    KT = kb.sb("l1a_KT", [128, T], BF16)
    VTK = kb.sb("l1a_VTK", [128, T], BF16)
    SCO = kb.sb("l1a_SCO", [128, T], F32)
    SMH = kb.sb("l1a_SMH", [128, T], BF16)
    FHALL = kb.sb("l1a_FHALL", [128, TC * 8], F32)
    TH = kb.sb("l1a_TH", [128, 4 * TC], F32)
    RST = kb.sb("l1a_RST", [128, 2 * TC], F32)
    NWB = kb.sb("l1a_NWB", [128, 128], F32)
    U1 = kb.sb("l1a_U1", [128, 1024], F32)
    U2 = kb.sb("l1a_U2", [128, max(T, 1024)], F32)
    NCQ = NCH // 4
    QF = kb.sb("l1a_QF", [128, T], BF16)
    SFM = kb.sb("l1a_SFM", [128, T], BF16)
    KPFX = kb.sb("l1a_KPFX", [128, T], BF16)
    FF2 = kb.sb("l1a_FF2", [128, T], F32)
    FF2X = None
    EF = kb.sb("l1a_EF", [128, T], BF16)
    QD = kb.sb("l1a_QD", [128, T], BF16)
    DPR = kb.sb("l1a_DPR", [128, 8 * 256], F32)
    EPB = kb.sb("l1a_EPB", [128, 8 * 256], BF16)
    PP = kb.sb("l1a_PP", [128, 8 * 256], BF16)
    ASR = kb.sb("l1a_ASR", [1, 2 * 8 * 256], BF16)
    MASKP = kb.sb("l1a_MASKP", [128, 256], F32)
    KVP = kb.sb("l1a_KVP", [128, 2 * (NCH // 4) * 128], BF16)
    A16 = kb.sb("l1a_A16", [128, NCH * 16], BF16)
    OBP = kb.sb("l1a_OBP", [128, (NCH // 4) * 128], BF16)
    FF = SCO
    SST = kb.sb("l1a_SST", [128, 128], F32)
    SSB = kb.sb("l1a_SSB", [128, 128], BF16)
    DCC = kb.sb("l1a_DCC", [128, NCH], F32)
    GNW = kb.sb("l1a_GNW", [128, 1024], BF16)
    NEA = kb.sb("l1a_NEA", [128, 8], F32)
    NEA8 = kb.sb("l1a_NEA8", [8, 1], F32)
    DTB8 = kb.sb("l1a_DTB8", [128, 8], F32)
    DTBR = kb.sb("l1a_DTBR", [8, 1], F32)
    DTROW = kb.sb("l1a_DTROW", [1, 8], F32)
    A8 = kb.sb("l1a_A8", [1, 8], F32)
    FHB = FF2
    FH8 = FF2[0:8, 0:T]
    U8 = U2[0:8, 0:T]
    EK = SMH
    EKT = EF

    TM_DR = kb.dram("l1a_tm_dr", [T, 128], BF16)
    TM2_DR = kb.dram("l1a_tm2_dr", [T, 128], BF16)
    A_DR = kb.dram("l1a_a_dr", [NCH, 16, 16], BF16)
    O_DR = kb.dram("l1a_o_dr", [16, NCH * 128], BF16)
    FH_DR = kb.dram("l1a_fh_dr", [8, T], F32)

    kb.dma("y", MASKP[:, :], HD["c_maskp"][:, :], cc.dsem)
    kb.dma("y", GNW[:, :], HD["c_gnw"][:, :], cc.dsem)
    kb.op("g", "dma_start", out=A8[:, :],
          in_=HD["L1_attn_A_log"].ap().rearrange("(o n) -> o n", o=1),
          incs=[(cc.dgi, 16)], selfsync=False)
    kb.op("g", "dma_start", out=DTBR[:, :],
          in_=HD["L1_attn_dt_bias"].ap().rearrange("(n o) -> n o", o=1),
          incs=[(cc.dgi, 16)], selfsync=False)
    kb.op("g", "dma_start", out=DTROW[:, :],
          in_=HD["L1_attn_dt_bias"].ap().rearrange("(o n) -> o n", o=1),
          incs=[(cc.dgi, 16)], selfsync=False)
    kb.barrier()
    for e in ("v", "s", "t", "g"):
        kb.wait(e, cc.dsem, cc.dsem.n)
        kb.wait(e, cc.dgi, cc.dgi.n)
    kb.op("t", "matmul", P[0][:, 0:8], cc.ONES_ROW32[0:1, :], A8[0:1, :],
          start=True, stop=True)
    tv = kb.mark("t", cc.tm)
    kb.wait("s", cc.tm, tv)
    kb.op("s", "activation", NEA[:, :], P[0][:, 0:8], AF.Exp)
    kb.op("v", "tensor_scalar", NEA[:, :], NEA[:, :], -1.0, None, OP.mult,
          waits=[kb.last("s")])
    kb.wait("t", *kb.last("v"))
    kb.op("t", "transpose", P[1][0:8, 0:1], A8[0:1, 0:8], cc.ONES_ROW32[0:1, 0:1])
    tv = kb.mark("t", cc.tm)
    kb.wait("s", cc.tm, tv)
    kb.op("s", "activation", NEA8[:, :], P[1][0:8, 0:1], AF.Exp)
    kb.op("v", "tensor_scalar", NEA8[:, :], NEA8[:, :], -1.0, None, OP.mult,
          waits=[kb.last("s")])
    kb.wait("t", *kb.last("v"))
    kb.op("t", "matmul", P[0][:, 0:8], cc.ONES_ROW32[0:1, :], DTROW[0:1, :],
          start=True, stop=True)
    tv = kb.mark("t", cc.tm)
    kb.wait("v", cc.tm, tv)
    kb.op("v", "tensor_copy", DTB8[:, :], P[0][:, 0:8])
    kb.wait("t", *kb.last("v"))
    kb.barrier()

    wq_r = HD["L1_attn_Wq"].ap().rearrange("(k p) (h c) -> p k h c", p=128, c=128)
    wk_r = HD["L1_attn_Wk"].ap().rearrange("(k p) (h c) -> p k h c", p=128, c=128)
    wv_r = HD["L1_attn_Wv"].ap().rearrange("(k p) (h c) -> p k h c", p=128, c=128)
    wr_r = HD["L1_attn_Wr"].ap().rearrange("(k p) (h c) -> p k h c", p=128, c=128)
    wa_r = HD["L1_attn_Wa"].ap().rearrange("(k p) h -> p k h", p=128)
    wo_r = HD["L1_attn_Wo"].ap().rearrange("(k p) (oc c) -> p k oc c", p=128, c=128)
    PTB = P[7][:, :].bitcast(BF16)
    PTB6 = P[6][:, :].bitcast(BF16)

    def tbatch(items):
        kb.wait("t", kb.self_sem["s"], kb.self_sem["s"].n)
        done = [0, 0]
        for i, (s_, d_) in enumerate(items):
            slb = [PTB6, PTB][i % 2]
            if done[i % 2]:
                kb.wait("t", kb.self_sem["v"], done[i % 2])
            kb.op("t", "transpose", slb[:, 0:128], s_, cc.IDENT[:, :])
            tv = kb.mark("t", cc.tm)
            kb.wait("v", cc.tm, tv)
            kb.op("v", "tensor_copy", d_, slb[:, 0:128])
            done[i % 2] = kb.last("v")[1]
        kb.wait("t", *kb.last("v"))

    for b in range(nbatch):
        def load_a(t0, w):
            xld = kb.dma("y", XCHK[:, 0:DT * w].rearrange("p (k t) -> p k t", k=DT),
                         HD["xin"][b].rearrange("(k p) t -> p k t", p=128)[:, :, t0:t0 + w],
                         cc.dsem, waits=[kb.last("v")])
            kb.wait("v", cc.dsem, xld)

            def get(k, w):
                return XCHK[:, k * w:(k + 1) * w]
            return get

        def apply_a(t0, w):
            for k in range(DT):
                sl = slice(k * T + t0, k * T + t0 + w)
                kb.op("v", "tensor_mul", U1[:, 0:w], XCHK[:, k * w:(k + 1) * w],
                      cc.RS[:, 0:w])
                kb.op("v", "tensor_add", XN[:, sl], U1[:, 0:w], cc.NMR[:, 0:w])
        cc.ln_stream(load_a, apply_a)
        kb.barrier()

        wal = kb.dma("g", WA[:, :].rearrange("p (k h) -> p k h", h=8), wa_r,
                     cc.dws)
        kb.wait("t", cc.dws, wal)
        for tc in range(TC):
            t0 = tc * 128
            for k in range(DT):
                kb.op("t", "matmul", P[4][:, 0:8],
                      XN[:, k * T + t0:k * T + t0 + 128], WA[:, k * 8:(k + 1) * 8],
                      start=(k == 0), stop=(k == DT - 1))
            tv = kb.mark("t", cc.tm)
            kb.wait("v", cc.tm, tv)
            kb.op("v", "tensor_add", U1[:, 0:8], P[4][:, 0:8], DTB8[:, :])
            kb.op("s", "activation", U2[:, 0:8], U1[:, 0:8], AF.Abs,
                  waits=[kb.last("v")])
            kb.op("s", "activation", U2[:, 0:8], U2[:, 0:8], AF.Exp,
                  scale=cc.kf(-1.0))
            kb.op("s", "activation", U2[:, 0:8], U2[:, 0:8], AF.Ln,
                  bias=cc.kf(1.0))
            kb.op("v", "tensor_scalar", U1[:, 0:8], U1[:, 0:8], 0.0, None,
                  OP.max, waits=[kb.last("s")])
            kb.op("v", "tensor_add", U1[:, 0:8], U1[:, 0:8], U2[:, 0:8])
            kb.op("v", "tensor_mul", FHALL[:, tc * 8:(tc + 1) * 8], U1[:, 0:8],
                  NEA[:, :])
            kb.wait("t", *kb.last("v"))
        for t0 in range(0, T, 512):
            w = min(512, T - t0)
            for k in range(DT):
                kb.op("t", "matmul", P[5][0:8, 0:w], WA[:, k * 8:(k + 1) * 8],
                      XN[:, k * T + t0:k * T + t0 + w],
                      start=(k == 0), stop=(k == DT - 1))
            tv = kb.mark("t", cc.tm)
            kb.wait("v", cc.tm, tv)
            kb.op("v", "tensor_scalar", FH8[:, t0:t0 + w], P[5][0:8, 0:w],
                  DTBR[:, 0:1], None, OP.add)
            kb.op("s", "activation", U8[:, t0:t0 + w], FH8[:, t0:t0 + w], AF.Abs,
                  waits=[kb.last("v")])
            kb.op("s", "activation", U8[:, t0:t0 + w], U8[:, t0:t0 + w], AF.Exp,
                  scale=cc.kf(-1.0, 8))
            kb.op("s", "activation", U8[:, t0:t0 + w], U8[:, t0:t0 + w], AF.Ln,
                  bias=cc.kf(1.0, 8))
            kb.op("v", "tensor_scalar", FH8[:, t0:t0 + w], FH8[:, t0:t0 + w],
                  0.0, None, OP.max, waits=[kb.last("s")])
            kb.op("v", "tensor_add", FH8[:, t0:t0 + w], FH8[:, t0:t0 + w],
                  U8[:, t0:t0 + w])
            kb.op("v", "tensor_scalar", FH8[:, t0:t0 + w], FH8[:, t0:t0 + w],
                  NEA8[:, 0:1], None, OP.mult)
            kb.wait("t", *kb.last("v"))
        fhs = kb.dma("y", FH_DR.ap(), FH8[:, :], cc.dout, waits=[kb.last("v")])
        kb.wait("v", cc.dout, fhs)
        kb.wait("s", cc.dout, fhs)

        ogather_v = 0
        for h in range(NH):
            for wi, wr_ in enumerate((wq_r, wk_r, wv_r, wr_r)):
                kb.dma("g", WPRJ[:, wi * DT * 128:(wi + 1) * DT * 128].rearrange(
                    "p (k c) -> p k c", c=128), wr_[:, :, h, :], cc.dws,
                    waits=[(cc.tm, cc.tm.n)] if wi == 0 else ())
            wl = cc.dws.n
            kb.wait("t", cc.dws, wl)
            pev = {}
            for tc in range(TC):
                t0 = tc * 128
                base = (tc % 2) * 4
                for wi in range(4):
                    bk = base + wi
                    if bk in pev:
                        kb.wait("t", kb.self_sem["s"], pev[bk])
                    for k in range(DT):
                        kb.op("t", "matmul", P[bk][:, 0:128],
                              XN[:, k * T + t0:k * T + t0 + 128],
                              WPRJ[:, wi * DT * 128 + k * 128:
                                   wi * DT * 128 + (k + 1) * 128],
                              start=(k == 0), stop=(k == DT - 1))
                tv = kb.mark("t", cc.tm)
                kb.wait("s", cc.tm, tv)
                for wi, (dst, af) in enumerate(((QT, AF.Silu), (KT, AF.Silu),
                                                (VTK, AF.Silu), (SCO, AF.Sigmoid))):
                    kb.op("s", "activation", dst[:, t0:t0 + 128],
                          P[base + wi][:, 0:128], af)
                    pev[base + wi] = kb.last("s")[1]
            kb.wait("t", *kb.last("s"))
            kb.wait("v", *kb.last("s"))
            nw = kb.dma("y", NWB[:, :], HD["c_qnw"][:, :], cc.dsem,
                        waits=[kb.last("v")])
            kb.wait("v", cc.dsem, nw)
            for wi, TT_ in enumerate((QT, KT)):
                if wi == 1:
                    nw = kb.dma("y", NWB[:, :], HD["c_knw"][:, :], cc.dsem,
                                waits=[kb.last("v")])
                    kb.wait("v", cc.dsem, nw)
                h1 = T // 2
                nh1 = h1 // 128
                kb.op("s", "activation", U1[:, 0:h1], TT_[:, 0:h1],
                      AF.Square, waits=[kb.last("v")])
                kb.op("v", "tensor_reduce", RST[:, 0:nh1],
                      U1[:, 0:h1].rearrange("p (tc m) -> p tc m", m=128),
                      AX.X, OP.add, waits=[kb.last("s")])
                kb.op("s", "activation", U2[:, 0:h1], TT_[:, h1:T], AF.Square)
                kb.op("v", "tensor_reduce", RST[:, nh1:TC],
                      U2[:, 0:h1].rearrange("p (tc m) -> p tc m", m=128),
                      AX.X, OP.add, waits=[kb.last("s")])
                kb.op("s", "activation", RST[:, TC:2 * TC], RST[:, 0:TC],
                      AF.Ln, bias=cc.kf(1e-5), scale=cc.kf(1.0 / 128),
                      waits=[kb.last("v")])
                kb.op("s", "activation", RST[:, TC:2 * TC], RST[:, TC:2 * TC],
                      AF.Exp, scale=cc.kf(-0.5))
                t3 = TT_[:, :].rearrange("p (tc m) -> p tc m", m=128)
                kb.op("v", "tensor_tensor", t3, t3,
                      RST[:, TC:2 * TC].rearrange("p (tc o) -> p tc o", o=1
                                                  ).broadcast_to([128, TC, 128]),
                      OP.mult, waits=[kb.last("s")])
                kb.op("v", "tensor_tensor", t3, t3,
                      NWB[:, :].rearrange("p (o m) -> p o m", o=1
                                          ).broadcast_to([128, TC, 128]),
                      OP.mult)
            sco3 = SCO[:, :].rearrange("p (tc m) -> p tc m", m=128)
            u13 = SMH[:, :].rearrange("p (tc m) -> p tc m", m=128)
            kb.op("v", "memset", TH[:, 0:TC], 0.0)
            kb.op("v", "memset", TH[:, TC:2 * TC], 1.0)
            for it in range(26):
                kb.op("v", "tensor_add", TH[:, 2 * TC:3 * TC], TH[:, 0:TC],
                      TH[:, TC:2 * TC])
                kb.op("v", "tensor_scalar", TH[:, 2 * TC:3 * TC],
                      TH[:, 2 * TC:3 * TC], 0.5, None, OP.mult)
                kb.op("v", "tensor_tensor", u13, sco3,
                      TH[:, 2 * TC:3 * TC].rearrange("p (tc o) -> p tc o", o=1
                                                     ).broadcast_to([128, TC, 128]),
                      OP.is_ge)
                kb.op("v", "tensor_reduce", TH[:, 3 * TC:4 * TC], u13, AX.X,
                      OP.add)
                kb.op("v", "tensor_scalar", TH[:, 3 * TC:4 * TC],
                      TH[:, 3 * TC:4 * TC], 32.0, None, OP.is_ge)
                kb.op("v", "tensor_mul", U2[:, 0:TC], TH[:, 3 * TC:4 * TC],
                      TH[:, 2 * TC:3 * TC])
                kb.op("v", "tensor_max", TH[:, 0:TC], TH[:, 0:TC], U2[:, 0:TC])
                kb.op("v", "scalar_tensor_tensor", U2[:, 0:TC],
                      TH[:, 3 * TC:4 * TC], 1e30, TH[:, 2 * TC:3 * TC],
                      OP.mult, OP.add)
                kb.op("v", "tensor_tensor", TH[:, TC:2 * TC], TH[:, TC:2 * TC],
                      U2[:, 0:TC], OP.min)
            kb.op("v", "tensor_tensor", u13, sco3,
                  TH[:, 0:TC].rearrange("p (tc o) -> p tc o", o=1
                                        ).broadcast_to([128, TC, 128]), OP.is_ge)
            kb.op("v", "tensor_mul", U2[:, 0:T], SMH[:, :], SCO[:, :])
            kb.op("v", "tensor_reduce", TH[:, 2 * TC:3 * TC],
                  U2[:, 0:T].rearrange("p (tc m) -> p tc m", m=128), AX.X, OP.add)
            kb.op("v", "tensor_scalar", TH[:, 2 * TC:3 * TC],
                  TH[:, 2 * TC:3 * TC], 1e-9, None, OP.add)
            kb.op("v", "reciprocal", TH[:, 3 * TC:4 * TC], TH[:, 2 * TC:3 * TC])
            kb.op("v", "tensor_tensor", SMH[:, :].rearrange("p (tc m) -> p tc m", m=128),
                  U2[:, 0:T].rearrange("p (tc m) -> p tc m", m=128),
                  TH[:, 3 * TC:4 * TC].rearrange("p (tc o) -> p tc o", o=1
                                                 ).broadcast_to([128, TC, 128]),
                  OP.mult)
            for tc in range(TC):
                t0 = tc * 128
                kb.op("v", "tensor_scalar", FF2[:, t0:t0 + 128],
                      SMH[:, t0:t0 + 128], FHALL[:, tc * 8 + h:tc * 8 + h + 1],
                      None, OP.mult)
            kb.op("s", "activation", FF2[:, 0:T], FF2[:, 0:T], AF.Exp,
                  waits=[kb.last("v")])
            kb.op("v", "tensor_scalar", FF2[:, 0:T], FF2[:, 0:T], -1.0, 1.0,
                  OP.mult, OP.add, waits=[kb.last("s")])
            kb.op("v", "tensor_mul", KT[:, :], FF2[:, 0:T], KT[:, :])
            kb.wait("t", *kb.last("v"))
            items = []
            for tc in range(TC):
                items.append((QT[:, tc * 128:(tc + 1) * 128],
                              QF[:, tc * 128:(tc + 1) * 128]))
                items.append((SMH[:, tc * 128:(tc + 1) * 128],
                              SFM[:, tc * 128:(tc + 1) * 128]))
                items.append((KT[:, tc * 128:(tc + 1) * 128],
                              KPFX[:, tc * 128:(tc + 1) * 128]))
            tbatch(items)
            fb = kb.dma("y", FHB[:, :],
                        FH_DR[h:h + 1, :].broadcast_to([128, T]), cc.dsem,
                        waits=[(cc.dout, fhs), kb.last("v")])
            kb.wait("v", cc.dsem, fb)
            kb.op("v", "tensor_mul", FF[:, :], SFM[:, :], FHB[:, :])
            cur, nxt = FF, FF2
            for sh in (1, 2, 4, 8):
                c3 = cur[:, :].rearrange("p (ch s) -> p ch s", s=16)
                n3 = nxt[:, :].rearrange("p (ch s) -> p ch s", s=16)
                kb.op("v", "tensor_add", n3[:, :, sh:16], c3[:, :, sh:16],
                      c3[:, :, 0:16 - sh])
                kb.op("v", "tensor_copy", n3[:, :, 0:sh], c3[:, :, 0:sh])
                cur, nxt = nxt, cur
            FCUM = cur
            kb.op("s", "activation", EF[:, :], FCUM[:, :], AF.Exp,
                  waits=[kb.last("v")])
            f3 = FCUM[:, :].rearrange("p (ch s) -> p ch s", s=16)
            kb.op("v", "tensor_tensor", FF2[:, :].rearrange("p (ch s) -> p ch s", s=16),
                  f3[:, :, 15:16].broadcast_to([128, NCH, 16]), f3, OP.subtract)
            kb.op("s", "activation", EK[:, :], FF2[:, :], AF.Exp,
                  waits=[kb.last("v")])
            kb.op("s", "activation", DCC[:, :],
                  FCUM[:, :].rearrange("p (ch s) -> p ch s", s=16)[:, :, 15],
                  AF.Exp)
            kb.op("v", "tensor_mul", QD[:, :], QF[:, :], EF[:, :],
                  waits=[kb.last("s")])
            pa_done = 0
            ppg_done = 0
            ppmm_done = 0
            exp_done = 0
            asr_sv = [0, 0]
            for g in range(NG):
                fg = FCUM[:, g * 128:(g + 1) * 128].rearrange(
                    "p (ch s) -> p ch s", s=16)
                d4 = DPR[:, :].rearrange("p (ch s t) -> p ch s t", s=16, t=16)
                if exp_done:
                    kb.wait("v", kb.self_sem["s"], exp_done)
                kb.op("v", "tensor_tensor", d4,
                      fg.rearrange("p ch (o t) -> p ch o t", o=1).broadcast_to(
                          [128, 8, 16, 16]),
                      fg.rearrange("p ch (s o) -> p ch s o", o=1).broadcast_to(
                          [128, 8, 16, 16]),
                      OP.subtract)
                kb.op("v", "tensor_tensor", DPR[:, :].rearrange(
                    "p (ch q) -> p ch q", q=256),
                    DPR[:, :].rearrange("p (ch q) -> p ch q", q=256),
                    MASKP[:, :].rearrange("p (o q) -> p o q", o=1
                                          ).broadcast_to([128, 8, 256]), OP.add)
                if ppg_done:
                    kb.wait("s", kb.self_sem["g"], ppg_done)
                kb.op("s", "activation", EPB[:, :], DPR[:, :], AF.Exp,
                      waits=[kb.last("v")])
                exp_done = kb.last("s")[1]
                qg = QF[:, g * 128:(g + 1) * 128].rearrange(
                    "p (ch t) -> p ch t", t=16)
                kgf = KPFX[:, g * 128:(g + 1) * 128].rearrange(
                    "p (ch s) -> p ch s", s=16)
                if ppmm_done:
                    kb.wait("v", cc.tm, ppmm_done)
                kb.op("v", "tensor_tensor", PP[:, :].rearrange(
                    "p (ch s t) -> p ch s t", s=16, t=16),
                    qg.rearrange("p ch (o t) -> p ch o t", o=1).broadcast_to(
                        [128, 8, 16, 16]),
                    kgf.rearrange("p ch (s o) -> p ch s o", o=1).broadcast_to(
                        [128, 8, 16, 16]),
                    OP.mult)
                kb.op("g", "tensor_tensor", PP[:, :], PP[:, :], EPB[:, :],
                      OP.mult, waits=[kb.last("v"), (kb.self_sem["s"], exp_done)])
                ppg_done = kb.last("g")[1]
                kb.wait("t", kb.self_sem["g"], ppg_done)
                asl = g % 2
                for q in range(4):
                    pb = 5 + q % 2
                    if pa_done:
                        kb.wait("t", kb.self_sem["s"], pa_done)
                    kb.op("t", "matmul", P[pb][0:1, 0:512], cc.ONES_COL[:, :],
                          PP[:, q * 512:(q + 1) * 512], start=True, stop=True)
                    tv = kb.mark("t", cc.tm)
                    if q == 3:
                        ppmm_done = tv
                    kb.wait("s", cc.tm, tv)
                    if q == 0 and asr_sv[asl]:
                        kb.wait("s", cc.dout, asr_sv[asl])
                    kb.op("s", "activation",
                          ASR[0:1, asl * 2048 + q * 512:asl * 2048 + (q + 1) * 512],
                          P[pb][0:1, 0:512], AF.Copy)
                    pa_done = kb.last("s")[1]
                asr_sv[asl] = kb.dma(
                    "y", A_DR.ap().rearrange("ch s t -> (ch s t)"
                        ).rearrange("(g n) -> g n", g=NG)[g].rearrange(
                        "(o n) -> o n", o=1),
                    ASR[0:1, asl * 2048:asl * 2048 + 2048], cc.dout,
                    waits=[kb.last("s")])
            tbatch([(EK[:, tc * 128:(tc + 1) * 128],
                     EKT[:, tc * 128:(tc + 1) * 128]) for tc in range(TC)])
            kb.op("v", "tensor_mul", EKT[:, :], KT[:, :], EKT[:, :])
            s1 = kb.dma("y", TM_DR.ap().rearrange("(tc p) c -> p tc c", p=128),
                        EKT[:, :].rearrange("p (tc c) -> p tc c", c=128),
                        cc.dout, waits=[kb.last("v")])
            s2 = kb.dma("y", TM2_DR.ap().rearrange("(tc p) c -> p tc c", p=128),
                        VTK[:, :].rearrange("p (tc c) -> p tc c", c=128),
                        cc.dout)
            for bb in (0, 32):
                for qq in range(4):
                    chs = slice(qq * NCQ, (qq + 1) * NCQ)
                    kb.dma("g", A16[bb:bb + 16,
                                    qq * NCQ * 16:(qq + 1) * NCQ * 16].rearrange(
                        "s (ch t) -> s ch t", t=16),
                        A_DR.ap().rearrange("ch s t -> s ch t")[:, chs, :],
                        cc.dws, allow_slow_non_contiguous=True,
                        waits=[(cc.dout, s2)] if (qq == 0 and bb == 0) else ())
            kb.op("v", "memset", SST[:, :], 0.0)
            kb.op("v", "memset", SSB[:, :], 0.0)
            kb.wait("t", *kb.last("v"))
            ob_done = [0, 0]
            kv_done = [0, 0]
            ssb_done = 0
            q_used = [0, 0]
            kv_ld = [0, 0]
            ob_sv = [0, 0]
            QW = NCQ * 128

            def kv_pre(qq):
                sl2 = qq % 2
                bb = 32 * sl2
                war = [(cc.tm, q_used[sl2])] if q_used[sl2] else []
                kb.dma("g", KVP[bb:bb + 16, 0:QW].rearrange(
                    "s (ch d) -> s ch d", d=128),
                    TM_DR.ap().rearrange("(ch s) d -> s ch d", s=16)[
                        :, qq * NCQ:(qq + 1) * NCQ, :],
                    cc.dws, allow_slow_non_contiguous=True, waits=war)
                kb.dma("g", KVP[bb:bb + 16, QW:2 * QW].rearrange(
                    "s (ch d) -> s ch d", d=128),
                    TM2_DR.ap().rearrange("(ch s) d -> s ch d", s=16)[
                        :, qq * NCQ:(qq + 1) * NCQ, :],
                    cc.dws, allow_slow_non_contiguous=True)
                kv_ld[sl2] = cc.dws.n
            kv_pre(0)
            kv_pre(1)
            for qq in range(4):
                sl2 = qq % 2
                bb = 32 * sl2
                K16 = KVP[bb:bb + 16, 0:QW]
                V16 = KVP[bb:bb + 16, QW:2 * QW]
                A16s = A16[bb:bb + 16, :]
                OB16 = OBP[bb:bb + 16, :]
                kb.wait("t", cc.dws, kv_ld[sl2])
                if qq == 0:
                    kb.wait("v", cc.dws, kv_ld[sl2])
                for ci in range(NCQ):
                    ch = qq * NCQ + ci
                    pb = ci % 2
                    if ssb_done:
                        kb.wait("t", kb.self_sem["v"], ssb_done)
                    if ob_done[pb]:
                        kb.wait("t", kb.self_sem["v"], ob_done[pb])
                    if ch > 0:
                        kb.op("t", "matmul", P[pb][0:16, 0:128],
                              QD[:, ch * 16:(ch + 1) * 16], SSB[:, :],
                              start=True, stop=False)
                    kb.op("t", "matmul", P[pb][0:16, 0:128],
                          A16s[:, ch * 16:(ch + 1) * 16],
                          V16[:, ci * 128:(ci + 1) * 128],
                          start=(ch == 0), stop=True)
                    tvo = kb.mark("t", cc.tm)
                    if kv_done[pb]:
                        kb.wait("t", kb.self_sem["v"], kv_done[pb])
                    kb.op("t", "matmul", P[2 + pb][:, 0:128],
                          K16[:, ci * 128:(ci + 1) * 128],
                          V16[:, ci * 128:(ci + 1) * 128], start=True, stop=True)
                    tvk = kb.mark("t", cc.tm)
                    kb.wait("v", cc.tm, tvo)
                    if ci == 0 and ob_sv[sl2]:
                        kb.wait("v", cc.dout, ob_sv[sl2])
                    kb.op("v", "tensor_copy", OB16[:, ci * 128:(ci + 1) * 128],
                          P[pb][0:16, 0:128])
                    ob_done[pb] = kb.last("v")[1]
                    kb.wait("v", cc.tm, tvk)
                    kb.op("v", "scalar_tensor_tensor", SST[:, :], SST[:, :],
                          DCC[:, ch:ch + 1], P[2 + pb][:, 0:128], OP.mult, OP.add)
                    kv_done[pb] = kb.last("v")[1]
                    kb.op("v", "tensor_copy", SSB[:, :], SST[:, :])
                    ssb_done = kb.last("v")[1]
                q_used[sl2] = cc.tm.n
                if qq + 2 < 4:
                    kv_pre(qq + 2)
                ow = [kb.last("v")]
                if ogather_v:
                    ow.append((cc.dgi, ogather_v))
                ob_sv[sl2] = kb.dma("y",
                                    O_DR[:, qq * NCQ * 128:(qq + 1) * NCQ * 128],
                                    OB16[:, :], cc.dout, waits=ow)
            so = cc.dout.n
            od = OALL[:, :].rearrange("p (tc hh c) -> p tc hh c", hh=8, c=128)
            os_ = O_DR.ap().rearrange("s (tc chp d) -> s tc chp d",
                                      tc=T // 128, d=128)
            for chp in range(8):
                kb.dma("g", od[16 * chp:16 * chp + 16, :, h, :],
                       os_[:, :, chp, :], cc.dgi,
                       waits=[(cc.dout, so)] if chp == 0 else ())
            ogather_v = cc.dgi.n
        for e in ("v", "s", "t"):
            kb.wait(e, cc.dgi, cc.dgi.n)
        for tc in range(TC):
            sl = slice(tc * 1024, (tc + 1) * 1024)
            kb.op("s", "activation", U1[:, 0:1024], OALL[:, sl], AF.Square,
                  waits=[kb.last("v")])
            kb.op("v", "tensor_scalar", U2[:, 0:1024], U1[:, 0:1024], 1.0, 0.0,
                  OP.mult, OP.add, accum_out=RST[:, tc:tc + 1],
                  waits=[kb.last("s")])
            kb.op("s", "activation", RST[:, TC + tc:TC + tc + 1],
                  RST[:, tc:tc + 1], AF.Ln, bias=cc.kf(1e-5),
                  scale=cc.kf(1.0 / 1024), waits=[kb.last("v")])
            kb.op("s", "activation", RST[:, TC + tc:TC + tc + 1],
                  RST[:, TC + tc:TC + tc + 1], AF.Exp, scale=cc.kf(-0.5))
            kb.op("v", "tensor_scalar", U1[:, 0:1024], OALL[:, sl],
                  RST[:, TC + tc:TC + tc + 1], None, OP.mult,
                  waits=[kb.last("s")])
            kb.op("v", "tensor_mul", OALL[:, sl], U1[:, 0:1024], GNW[:, :])
            kb.wait("t", *kb.last("v"))
            tbatch([(OALL[:, tc * 1024 + k * 128:tc * 1024 + (k + 1) * 128],
                     OF[:, k * T + tc * 128:k * T + (tc + 1) * 128])
                    for k in range(DT)])
        woq = [0, 0]
        for t0 in range(0, T, 512):
            w = min(512, T - t0)
            xi = kb.dma("y", XCHK[:, 0:DT * w].rearrange("p (k t) -> p k t", k=DT),
                        HD["xin"][b].rearrange("(k p) t -> p k t", p=128)[:, :, t0:t0 + w],
                        cc.dsem,
                        waits=[(cc.dout, cc.dout.n), kb.last("v")])
            kb.wait("v", cc.dsem, xi)
            for oc in range(DT):
                dbuf = oc % 2
                if woq[dbuf]:
                    kb.wait("g", cc.tm, woq[dbuf])
                wl = kb.dma("g", WOC[:, dbuf * DT * 128:(dbuf + 1) * DT * 128].rearrange(
                    "p (k c) -> p k c", c=128), wo_r[:, :, oc, :], cc.dws)
                kb.wait("t", cc.dws, wl)
                pb = 4 + oc % 2
                for k in range(DT):
                    kb.op("t", "matmul", P[pb][:, 0:w],
                          WOC[:, dbuf * DT * 128 + k * 128:dbuf * DT * 128 + (k + 1) * 128],
                          OF[:, k * T + t0:k * T + t0 + w],
                          start=(k == 0), stop=(k == DT - 1))
                tv = kb.mark("t", cc.tm)
                woq[dbuf] = tv
                kb.wait("v", cc.tm, tv)
                kb.op("v", "tensor_tensor", XCHK[:, oc * w:(oc + 1) * w],
                      XCHK[:, oc * w:(oc + 1) * w], P[pb][:, 0:w], OP.add)
                kb.wait("t", *kb.last("v"))
            so = kb.dma("y", HD["xout"][b].rearrange("(k p) t -> p k t", p=128)[:, :, t0:t0 + w],
                        XCHK[:, 0:DT * w].rearrange("p (k t) -> p k t", k=DT),
                        cc.dout, waits=[kb.last("v")])
        kb.wait("y", cc.dout, cc.dout.n)
        kb.barrier()


def build_l1a(T, nbatch):
    nc = bass.Bass("TRN2", target_bir_lowering=False)
    HD = {}
    HD["xin"] = nc.dram_tensor("xin", [nbatch, D, T], F32, kind="ExternalInput").ap()
    HD["xout"] = nc.dram_tensor("xout", [nbatch, D, T], F32, kind="ExternalOutput").ap()
    for nm, shp, dt in [("L1_attn_Wq", [D, D], F32), ("L1_attn_Wk", [D, D], F32),
                        ("L1_attn_Wv", [D, D], F32), ("L1_attn_Wr", [D, D], F32),
                        ("L1_attn_Wa", [D, 8], F32), ("L1_attn_A_log", [8], F32),
                        ("L1_attn_dt_bias", [8], F32), ("L1_attn_Wo", [D, D], F32),
                        ("c_maskp", [128, 256], F32), ("c_qnw", [128, 128], F32),
                        ("c_knw", [128, 128], F32), ("c_gnw", [128, 1024], BF16)]:
        HD[nm] = nc.dram_tensor(nm, shp, dt, kind="ExternalInput")
    kb = KB(nc)
    cc = CC(kb, T)
    emit_l1a(kb, cc, HD, nbatch)
    kb.finalize()
    return nc


def l1a_consts(P):
    mask = np.zeros((16, 16), np.float32)
    for s in range(16):
        for t in range(16):
            if t < s:
                mask[s, t] = -1e30
    return {
        "c_maskp": np.tile(mask.reshape(1, 256), (128, 1)).astype(np.float32),
        "c_qnw": np.tile(P["L1_attn_q_norm_w"].reshape(1, 128), (128, 1)),
        "c_knw": np.tile(P["L1_attn_k_norm_w"].reshape(1, 128), (128, 1)),
        "c_gnw": np.tile(P["L1_attn_g_norm_w"].reshape(1, 1024), (128, 1)).astype(ml_dtypes.bfloat16),
    }




def emit_l1f_fused(kb, cc, HD, nbatch, tag):
    from concourse.replica_groups import maybe_share_collective_output_space
    nc = kb.nc
    T = cc.T
    TC = cc.TC
    rg = [list(range(8))]
    sp = maybe_share_collective_output_space("AllReduce", rg)
    cnt_in = kb.dram("ccf_cnt_in", [1, 8], F32)
    cnt_out = kb.dram("ccf_cnt_out", [1, 8], F32)
    xf8_in = kb.dram("ccf_xf8_in", [128, DT * 8], F32)
    xf8_out = kb.dram("ccf_xf8_out", [128, DT * 8], F32)
    hp_in = kb.dram("ccf_hp_in", [64, 512], F32)
    hp_out = kb.dram("ccf_hp_out", [64, 512], F32)
    ohw_dr = kb.dram("ccf_ohw", [nbatch, TC, 128, 20], F32)
    cnt8_dr = kb.dram("ccf_cnt8", [1, 8], F32)
    xf8_dr = kb.dram("ccf_xf8b", [128, DT * 8], BF16)
    ccs = kb.sem("ccs")

    HDa = dict(HD)
    HDa["cnt8"] = cnt8_dr
    HDa["ohw"] = ohw_dr.ap()
    HDa["xf8"] = xf8_dr.ap()
    emit_l1f_a(kb, cc, HDa, nbatch)
    kb.new_phase(cc)

    XFT = kb.sb("ccf_XFT", [128, DT * 8], F32)
    XFB = kb.sb("ccf_XFB", [128, DT * 8], BF16)
    C0 = kb.sb("ccf_C0", [128, 1], F32)
    l1 = kb.dma("y", XFB[:, :], xf8_dr.ap(), cc.dsem)
    l2 = kb.dma("y", C0[:, :], HD["c_core0"][:, :], cc.dsem)
    kb.wait("v", cc.dsem, l2)
    kb.op("v", "tensor_scalar", XFT[:, :], XFB[:, :], C0[:, 0:1], None, OP.mult)
    s1 = kb.dma("y", xf8_in.ap(), XFT[:, :], cc.dout, waits=[kb.last("v")])
    s2 = kb.dma("y", cnt_in.ap(), cnt8_dr.ap(), cc.dout)
    kb.wait("g", cc.dout, s2)
    kb.op("g", "collective_compute", "AllReduce", OP.add,
          replica_groups=rg, ins=[cnt_in.ap().opt()], outs=[cnt_out.ap().opt()],
          incs=[(ccs, "cc")], selfsync=False)
    kb.op("g", "collective_compute", "AllReduce", OP.add,
          replica_groups=rg, ins=[xf8_in.ap().opt()], outs=[xf8_out.ap().opt()],
          incs=[(ccs, "cc")], selfsync=False)
    kb.wait("g", ccs, ccs.n)
    gl = kb.dma("g", xf8_dr.ap(), xf8_out.ap(), cc.dgi)
    kb.wait("y", cc.dgi, gl)
    kb.new_phase(cc)

    HDb = dict(HD)
    HDb["cnts_global"] = cnt_out
    HDb["xf8"] = xf8_dr.ap()
    HDb["ohw"] = ohw_dr.ap()
    HDb["hpart"] = kb.dram("ccf_hpart_loc", [64, 512], F32)
    emit_l1f_b(kb, cc, HDb, nbatch)
    kb.new_phase(cc)

    s3 = kb.dma("y", hp_in.ap(), HDb["hpart"].ap(), cc.dout)
    kb.wait("g", cc.dout, s3)
    kb.op("g", "collective_compute", "AllReduce", OP.add,
          replica_groups=rg, ins=[hp_in.ap().opt()], outs=[hp_out.ap().opt()],
          incs=[(ccs, "cc")], selfsync=False)
    kb.wait("g", ccs, ccs.n)
    kb.wait("y", ccs, ccs.n)
    kb.new_phase(cc)

    HDc = dict(HD)
    HDc["hpart_global"] = hp_out
    HDc["xa"] = HD["xout"]
    HDc["inplace"] = True
    emit_l1f_c(kb, cc, HDc, nbatch)



def _declare_all_inputs(nc, nbatch, T):
    TC = T // 128
    HD = {}
    specs = [
        ("L0_ffn_Wq", [D, D], F32), ("L0_ffn_bq", [D], F32),
        ("L0_ffn_Wk", [D, D], F32), ("L0_ffn_bk", [D], F32),
        ("L0_ffn_Wv", [D, D], F32), ("L0_ffn_bv", [D], F32),
        ("L0_ffn_Wo", [D, D], F32), ("L0_ffn_bo", [D], F32),
        ("L0_ffn_g1", [D], F32), ("L0_ffn_be1", [D], F32),
        ("L0_ffn_g2", [D], F32), ("L0_ffn_be2", [D], F32),
        ("L0_ffn_Wg", [D, 8], F32), ("L0_ffn_bg", [8], F32),
        ("L0_ffn_We", [8, D, D], F32), ("L0_ffn_bexp", [8, D], F32),
        ("L0_attn_ln_w", [2, D], F32), ("L0_attn_ln_b", [2, D], F32),
        ("L0_attn_in_proj_w", [2, 2 * E_IN, D], F32),
        ("L0_attn_conv_w", [2, E_IN, 4], F32), ("L0_attn_conv_b", [2, E_IN], F32),
        ("L0_attn_x_proj_w", [2, 96, E_IN], F32),
        ("L0_attn_dt_proj_w", [2, E_IN, RNK], F32),
        ("L0_attn_dt_proj_b", [2, E_IN], F32),
        ("L0_attn_D_skip", [2, E_IN], F32),
        ("L0_attn_out_proj_w", [2, D, E_IN], F32),
        ("L0_attn_norm_f_w", [D], F32), ("L0_attn_norm_f_b", [D], F32),
        ("L0A_Alog_r", [2, 16, 128, 16], F32),
        ("c_sel", [128, 16 * 128], BF16), ("c_selt", [128, 16 * 128], BF16),
        ("c_s16", [16, 128], BF16),
        ("L1_ffn_w_gate", [8, D], F32), ("L1_ffn_sw1", [D, D], F32),
        ("L1_ffn_sw2", [D, D], F32), ("L1_ffn_sw3", [D, D], F32),
        ("L1_ffn_expert_bias", [8], F32),
        ("L1_ffn_w1", [8, D, 512], F32), ("L1_ffn_w3", [8, D, 512], F32),
        ("L1_ffn_w2", [8, 512, D], F32),
        ("c_one8", [8, 1], F32), ("c_ut8", [8, 8], F32),
        ("c_ja", [128, nbatch * TC], F32), ("maskval", [8, 1], F32),
        ("c_core0", [128, 1], F32),
        ("L1_attn_Wq", [D, D], F32), ("L1_attn_Wk", [D, D], F32),
        ("L1_attn_Wv", [D, D], F32), ("L1_attn_Wr", [D, D], F32),
        ("L1_attn_Wa", [D, 8], F32), ("L1_attn_A_log", [8], F32),
        ("L1_attn_dt_bias", [8], F32), ("L1_attn_Wo", [D, D], F32),
        ("c_maskp", [128, 256], F32), ("c_qnw", [128, 128], F32),
        ("c_knw", [128, 128], F32), ("c_gnw", [128, 1024], BF16),
    ]
    for nm, shp, dt in specs:
        HD[nm] = nc.dram_tensor(nm, shp, dt, kind="ExternalInput")
    for nm in ("L1_ffn_w1", "L1_ffn_w3", "L1_ffn_w2"):
        HD[nm] = HD[nm].ap()
    return HD


ARENA_COLS = 99456


def build_fused(T, nbatch, upto=None):
    nc = bass.Bass("TRN2", target_bir_lowering=False)
    xin = nc.dram_tensor("xin", [nbatch, D, T], F32, kind="ExternalInput")
    xout = nc.dram_tensor("xout", [nbatch, D, T], F32, kind="ExternalOutput")
    HDW = _declare_all_inputs(nc, nbatch, T)
    kb = KB(nc)
    cc = CC(kb, T)
    xa = kb.dram("xmid_a", [nbatch, D, T], F32)
    xb = kb.dram("xmid_b", [nbatch, D, T], F32)
    kb.enable_arena(ARENA_COLS)
    offs = []
    nph = [0]

    def phase(emit, xi, xo, *extra):
        if upto is not None and nph[0] >= upto:
            return
        nph[0] += 1
        HDp = dict(HDW)
        HDp["xin"] = xi.ap()
        HDp["xout"] = xo.ap()
        emit(kb, cc, HDp, nbatch, *extra)
        offs.append(kb._ar_off)
        kb.new_phase(cc)

    phase(emit_l0f, xin, xa)
    phase(emit_l0a, xa, xb)
    phase(emit_l0f, xb, xa)
    phase(emit_l1f_fused, xa, xb, 0)
    phase(emit_l1a, xb, xa)
    if upto is None or nph[0] < upto:
        HDp = dict(HDW)
        HDp["xin"] = xa.ap()
        HDp["xout"] = xout.ap()
        emit_l1f_fused(kb, cc, HDp, nbatch, 1)
        offs.append(kb._ar_off)
    kb.finalize()
    build_fused.arena_offs = offs
    return nc


def const_inputs():
    return {
        "c_ones_col_bf": np.ones((128, 1), ml_dtypes.bfloat16),
        "c_ones_row_bf": np.ones((1, 128), ml_dtypes.bfloat16),
        "c_ones_row_f32": np.ones((1, 128), np.float32),
        "c_ident_bf": np.eye(128, dtype=ml_dtypes.bfloat16),
        "c_scal": np.tile(np.array([1e-5, -10.0, -0.5, 0.125, 1.0, 0.5,
                                    -1.0, 1.0 / 1024, 1.0 / 128, 2.0],
                                   np.float32), (128, 1)),
        "c_ones_r65": np.ones((65, 128), np.float32),
    }


USE_FUSED = True
_BUILD_CACHE = {}


def _get(build_fn, key, *a):
    if key not in _BUILD_CACHE:
        _BUILD_CACHE[key] = build_fn(*a)
    return _BUILD_CACHE[key]


def kernel(**inputs):
    T = 2048
    NB = 2
    NC = 8
    TC = T // 128

    x = np.ascontiguousarray(
        np.asarray(inputs["L0_attn_x"], np.float32).transpose(0, 2, 1))
    consts = dict(const_inputs())
    l0ac = l0a_consts()
    Pw = {k: np.ascontiguousarray(np.asarray(v)) for k, v in inputs.items()}
    l1ac = l1a_consts(Pw)
    al = Pw["L0_attn_A_log"].reshape(2, 16, 16, 8, 16)
    alog_r = np.ascontiguousarray(al.transpose(0, 1, 3, 4, 2)).reshape(2, 16, 128, 16)

    def shard(xf):
        return [np.ascontiguousarray(xf[c * NB:(c + 1) * NB]) for c in range(NC)]

    def launch(nc, per_core, outs):
        res = run_bass_kernel_spmd(nc, per_core, core_ids=list(range(NC)))
        return [{k: np.asarray(res.results[c][k]) for k in outs}
                for c in range(NC)]

    def base_ins(keys):
        d = dict(consts)
        for k in keys:
            d[k] = Pw[k]
        return d

    L0F_KEYS = [k for k in Pw if k.startswith("L0_ffn")]
    L0A_KEYS = [k for k in Pw if k.startswith("L0_attn") and
                k not in ("L0_attn_x", "L0_attn_A_log")]
    L1A_KEYS = ["L1_attn_Wq", "L1_attn_Wk", "L1_attn_Wv", "L1_attn_Wr",
                "L1_attn_Wa", "L1_attn_A_log", "L1_attn_dt_bias", "L1_attn_Wo"]

    cur = shard(x)

    def run_l0f(cur):
        nc = _get(build_l0f, "l0f", T, NB)
        ins = base_ins(L0F_KEYS)
        per = [dict(ins, xin=cur[c]) for c in range(NC)]
        r = launch(nc, per, ["xout"])
        return [r[c]["xout"].reshape(NB, D, T) for c in range(NC)]

    def run_l0a(cur):
        nc = _get(build_l0a, "l0a", T, NB)
        ins = base_ins(L0A_KEYS)
        ins.update(l0ac)
        ins["L0A_Alog_r"] = alog_r
        per = [dict(ins, xin=cur[c]) for c in range(NC)]
        r = launch(nc, per, ["xout"])
        return [r[c]["xout"].reshape(NB, D, T) for c in range(NC)]

    def run_l1f(cur):
        nca = _get(build_l1f_a, "l1fa", T, NB)
        insa = base_ins(["L1_ffn_w_gate", "L1_ffn_sw1", "L1_ffn_sw2",
                         "L1_ffn_sw3", "L1_ffn_expert_bias"])
        per = [dict(insa, xin=cur[c]) for c in range(NC)]
        ra = launch(nca, per, ["xout", "cnt8", "ohw", "xf8"])
        cnts = np.stack([ra[c]["cnt8"].reshape(8) for c in range(NC)])
        ncb = _get(build_l1f_b, "l1fb", T, NB)
        insb = dict(consts)
        insb["cnts"] = cnts.astype(np.float32)
        insb["xf8"] = ra[0]["xf8"]
        insb["c_one8"] = np.ones((8, 1), np.float32)
        insb["c_ut8"] = np.triu(np.ones((8, 8), np.float32))
        insb["L1_ffn_w1"] = Pw["L1_ffn_w1"]
        insb["L1_ffn_w3"] = Pw["L1_ffn_w3"]
        perb = []
        for c in range(NC):
            ja = np.zeros((128, NB * TC), np.float32)
            for b in range(NB):
                for tc in range(TC):
                    base = 2.0 * (c * NB * T + b * T + tc * 128)
                    ja[:, b * TC + tc] = base + 2.0 * np.arange(128)
            perb.append(dict(insb, ohw=ra[c]["ohw"].reshape(NB, TC, 128, 20),
                             c_ja=ja))
        rb = launch(ncb, perb, ["hpart"])
        hparts = np.stack([rb[c]["hpart"].reshape(64, 512) for c in range(NC)])
        ncc = _get(build_l1f_c, "l1fc", T, NB)
        insc = dict(consts)
        insc["hparts"] = hparts.astype(np.float32)
        insc["L1_ffn_w2"] = Pw["L1_ffn_w2"]
        perc = []
        for c in range(NC):
            mv = np.full((8, 1), 0.5 if c == 0 else 0.0, np.float32)
            perc.append(dict(insc, xa=ra[c]["xout"].reshape(NB, D, T),
                             maskval=mv))
        rc = launch(ncc, perc, ["xout"])
        return [rc[c]["xout"].reshape(NB, D, T) for c in range(NC)]

    def run_l1a(cur):
        nc = _get(build_l1a, "l1a", T, NB)
        ins = base_ins(L1A_KEYS)
        ins.update(l1ac)
        per = [dict(ins, xin=cur[c]) for c in range(NC)]
        r = launch(nc, per, ["xout"])
        return [r[c]["xout"].reshape(NB, D, T) for c in range(NC)]

    if not USE_FUSED:
        cur = run_l0f(cur)
        cur = run_l0a(cur)
        cur = run_l0f(cur)
        cur = run_l1f(cur)
        cur = run_l1a(cur)
        cur = run_l1f(cur)
        out = np.concatenate(cur, axis=0)
        return np.ascontiguousarray(out.transpose(0, 2, 1)).astype(np.float32)

    fc = dict(consts)
    fc.update(l0ac)
    fc.update(l1ac)
    fc["L0A_Alog_r"] = alog_r
    fc["c_one8"] = np.ones((8, 1), np.float32)
    fc["c_ut8"] = np.triu(np.ones((8, 8), np.float32))
    for k in Pw:
        if k not in ("L0_attn_x", "L0_attn_A_log"):
            fc[k] = Pw[k]
    nc = _get(build_fused, "fused", T, NB)
    per = []
    for c in range(NC):
        d = dict(fc)
        d["xin"] = cur[c]
        ja = np.zeros((128, NB * TC), np.float32)
        for b in range(NB):
            for tc in range(TC):
                ja[:, b * TC + tc] = (2.0 * (c * NB * T + b * T + tc * 128) +
                                      2.0 * np.arange(128))
        d["c_ja"] = ja
        d["maskval"] = np.full((8, 1), 0.5 if c == 0 else 0.0, np.float32)
        d["c_core0"] = np.full((128, 1), 1.0 if c == 0 else 0.0, np.float32)
        per.append(d)
    res = run_bass_kernel_spmd(nc, per, core_ids=list(range(NC)))
    out = np.concatenate([np.asarray(res.results[c]["xout"]).reshape(NB, D, T)
                          for c in range(NC)], 0)
    return np.ascontiguousarray(out.transpose(0, 2, 1)).astype(np.float32)


# version 09
# speedup vs baseline: 1.3303x; 1.0040x over previous
import contextlib
import sys
import ml_dtypes
import numpy as np
import concourse.bass as bass
import concourse.mybir as mybir
from concourse.bass_utils import run_bass_kernel_spmd

F32 = mybir.dt.float32
BF16 = mybir.dt.bfloat16
AF = mybir.ActivationFunctionType
OP = mybir.AluOpType
AX = mybir.AxisListType

D = 1024
DT = D // 128
EPS = 1e-5
ENGS = ("v", "s", "g", "t", "y")


class Sem:
    __slots__ = ("name", "h", "n")

    def __init__(self, name):
        self.name = name
        self.h = None
        self.n = 0


class KB:
    def __init__(self, nc):
        self.nc = nc
        self.q = {e: [] for e in ENGS}
        self.sems = {}
        self.self_sem = {e: self.sem("self_" + e) for e in ("v", "s", "g")}
        self._st = contextlib.ExitStack()
        self._sb_real = {}
        self._dram_cache = {}
        self.arena = None
        self._ar_off = 0
        self._ar_cache = {}
        self._phase = 0

    def enable_arena(self, nbytes_cols_bf16):
        self.arena = self._st.enter_context(
            self.nc.sbuf_tensor("ARENA", [128, nbytes_cols_bf16], BF16))

    def arena_reset(self):
        self._ar_off = 0
        self._ar_cache = {}

    def dram(self, name, shape, dt):
        if name not in self._dram_cache:
            self._dram_cache[name] = self.nc.dram_tensor(name, shape, dt)
        return self._dram_cache[name]

    def new_phase(self, cc):
        self._phase += 1
        i = self._phase
        for e in ("v", "s", "g", "t", "y"):
            for sm in (cc.dsem, cc.dgi, cc.dws, cc.dout):
                self.wait(e, sm, sm.n)
        self.barrier()
        LIM = 30000
        if cc.dsem.n > LIM:
            cc.dsem = self.sem(f"dma_in_{i}")
        if cc.dgi.n > LIM:
            cc.dgi = self.sem(f"dma_in_g_{i}")
        if cc.dws.n > LIM:
            cc.dws = self.sem(f"dma_w_{i}")
        if cc.dout.n > LIM:
            cc.dout = self.sem(f"dma_out_{i}")
        if cc.tm.n > LIM:
            cc.tm = self.sem(f"tm_{i}")
        for e in ("v", "s", "g"):
            if self.self_sem[e].n > LIM:
                self.self_sem[e] = self.sem(f"self_{e}_{i}")
        self.arena_reset()

    def sem(self, name):
        if name not in self.sems:
            self.sems[name] = Sem(name)
        return self.sems[name]

    def sb(self, name, shape, dt):
        if self.arena is None:
            if name not in self._sb_real:
                self._sb_real[name] = self._st.enter_context(
                    self.nc.sbuf_tensor(name, shape, dt))
            return self._sb_real[name]
        if name in self._sb_real:
            return self._sb_real[name]
        if name in self._ar_cache:
            return self._ar_cache[name]
        rows, cols = shape
        bf_cols = cols if dt == BF16 else 2 * cols
        bf_cols = (bf_cols + 15) & ~15
        off = self._ar_off
        assert off + bf_cols <= self.arena.shape[1], \
            f"arena overflow at {name}: {off}+{bf_cols}"
        self._ar_off = off + bf_cols
        view = self.arena[:, off:off + bf_cols]
        if dt != BF16:
            view = view.bitcast(dt)
        ap = view[0:rows, 0:cols]
        self._ar_cache[name] = ap
        return ap

    def psum(self, name, shape, dt):
        return self._st.enter_context(self.nc.psum_tensor(name, shape, dt))

    def op(self, e, method, *a, waits=(), incs=(), selfsync=True, **kw):
        waits = list(waits)
        incs = list(incs)
        if e in ("v", "s", "g") and selfsync:
            ss = self.self_sem[e]
            if ss.n > 0:
                waits.append((ss, ss.n))
            incs.append((ss, 1))
        incs2 = []
        for (sm, n) in incs:
            if n == "cc":
                sm.n += 1
                incs2.append((sm, "cc", sm.n))
            else:
                sm.n += n
                incs2.append((sm, n, sm.n))
        incs = incs2
        fr = sys._getframe(1)
        while fr and fr.f_code.co_filename.endswith("kernel.py") and \
                fr.f_code.co_name in ("op", "dma", "mark"):
            fr = fr.f_back
        self.q[e].append([method, a, kw, waits, incs,
                          f"{fr.f_code.co_name}:{fr.f_lineno}" if fr else "?"])

    def mark(self, e, sem, n=1):
        sem.n += n
        self.q[e][-1][4].append((sem, n, sem.n))
        return sem.n

    def last(self, e):
        ss = self.self_sem[e]
        return (ss, ss.n)

    def wait(self, e, sem, val=None):
        self.q[e].append(["__wait__", (sem, sem.n if val is None else val), {}, (), ()])

    def dma(self, e, out, in_, sem, waits=(), **kw):
        w = list(waits) + ([(sem, sem.n)] if sem.n > 0 else [])
        self.op(e, "dma_start", out=out, in_=in_, waits=w, incs=[(sem, 16)],
                selfsync=False, **kw)
        return sem.n

    def _prod_point(self, e):
        if e in ("v", "s", "g"):
            ss = self.self_sem[e]
            return (ss, ss.n) if ss.n > 0 else None
        q = self.q[e]
        idx = None
        for i in range(len(q) - 1, -1, -1):
            if q[i][0] not in ("__wait__", "__inc__"):
                idx = i
                break
        if idx is None:
            return None
        if q[idx][4]:
            sm, n, after = q[idx][4][-1]
            return (sm, after)
        bm = self.sem("__bm_" + e)
        key = getattr(self, "_bar_last", {})
        if key.get(e, (None,))[0] == id(q[idx]):
            return (bm, key[e][1])
        bm.n += 1
        q[idx][4].append((bm, 1, bm.n))
        key[e] = (id(q[idx]), bm.n)
        self._bar_last = key
        return (bm, bm.n)

    def barrier(self, engines=("v", "s", "g", "t", "y")):
        pts = {e: self._prod_point(e) for e in engines}
        for e in engines:
            for e2 in engines:
                if e2 == e or pts[e2] is None:
                    continue
                self.wait(e, pts[e2][0], pts[e2][1])

    def finalize(self):
        nc = self.nc
        for s in self.sems.values():
            s.h = self._st.enter_context(nc.semaphore(s.name))
        eng_map = {"v": "vector", "s": "scalar", "g": "gpsimd", "t": "tensor",
                   "y": "sync"}
        with self._st:
            with nc.Block() as block:
                for e in ENGS:
                    if not self.q[e]:
                        continue

                    def make(run_q):
                        def _(eng):
                            for rec in run_q:
                                method, a, kw, waits, incs = rec[:5]
                                src_info = rec[5] if len(rec) > 5 else "?"

                                if method == "__wait__":
                                    sm, v = a
                                    eng.wait_ge(sm.h, v)
                                    continue
                                if method == "__inc__":
                                    sm, n = a
                                    eng.nop().then_inc(sm.h, n)
                                    continue
                                for (sm, v) in waits:
                                    eng.wait_ge(sm.h, v)
                                try:
                                    ins = getattr(eng, method)(*a, **kw)
                                except Exception as ex:
                                    raise RuntimeError(
                                        f"emit fail {method} from {src_info}: {ex}")
                                for inc in incs:
                                    if inc[1] == "cc":
                                        ins = ins.then_inc(inc[0].h)
                                    else:
                                        ins = ins.then_inc(inc[0].h, inc[1])
                        return _
                    getattr(block, eng_map[e])(make(self.q[e]))
        return nc


class CC:

    def __init__(self, kb, T):
        self.kb = kb
        self.T = T
        self.TC = T // 128
        self.N5 = T // 512 if T % 512 == 0 else 1
        self.C5 = 512 if T % 512 == 0 else T
        nc = kb.nc
        self.dsem = kb.sem("dma_in")
        self.dgi = kb.sem("dma_in_g")
        self.dws = kb.sem("dma_w")
        self.dout = kb.sem("dma_out")
        self.tm = kb.sem("tm")

        self.c_ones_col_bf = nc.dram_tensor("c_ones_col_bf", [128, 1], BF16,
                                            kind="ExternalInput")
        self.c_ones_row_bf = nc.dram_tensor("c_ones_row_bf", [1, 128], BF16,
                                            kind="ExternalInput")
        self.c_ones_row_f32 = nc.dram_tensor("c_ones_row_f32", [1, 128], F32,
                                             kind="ExternalInput")
        self.c_ident_bf = nc.dram_tensor("c_ident_bf", [128, 128], BF16,
                                         kind="ExternalInput")
        self.c_scal = nc.dram_tensor("c_scal", [128, 10], F32,
                                     kind="ExternalInput")
        self.c_ones_r65 = nc.dram_tensor("c_ones_r65", [65, 128], F32,
                                         kind="ExternalInput")
        self.ONES_COL = kb.sb("ONES_COL", [128, 1], BF16)
        self.ONES_ROWB = kb.sb("ONES_ROWB", [1, 128], BF16)
        self.ONES_ROW32 = kb.sb("ONES_ROW32", [1, 128], F32)
        self.IDENT = kb.sb("IDENT", [128, 128], BF16)
        self.SCAL = kb.sb("SCAL", [128, 10], F32)
        self.ONES65 = kb.sb("ONES65", [65, 128], F32)
        kb.dma("y", self.ONES_COL[:, :], self.c_ones_col_bf[:, :], self.dsem)
        kb.dma("y", self.ONES_ROWB[:, :], self.c_ones_row_bf[:, :], self.dsem)
        kb.dma("y", self.ONES_ROW32[:, :], self.c_ones_row_f32[:, :], self.dsem)
        kb.dma("y", self.IDENT[:, :], self.c_ident_bf[:, :], self.dsem)
        kb.dma("y", self.SCAL[:, :], self.c_scal[:, :], self.dsem)
        kb.dma("y", self.ONES65[:, :], self.c_ones_r65[:, :], self.dsem)

        self.P = [kb.psum(f"P{i}", [128, 512], F32) for i in range(8)]

        self.R_MU = kb.sb("R_MU", [1, 512], F32)[:, :]
        self.R_MS = kb.sb("R_MS", [1, 512], F32)[:, :]
        self.R_RS = self.R_MS
        self.R_NMR = kb.sb("R_NMR", [1, 512], F32)[:, :]
        self.RS = None
        self.NMR = None
        self.XB = kb.sb("XBSC", [128, 2 * 512], BF16)
        self.XQ = kb.sb("XQSC", [128, 2 * 512], BF16)

    SCAL_VALS = {1e-5: 0, -10.0: 1, -0.5: 2, 0.125: 3, 1.0: 4, 0.5: 5,
                 -1.0: 6, 1.0 / 1024: 7, 1.0 / 128: 8, 2.0: 9}

    def kf(self, val, p=128):
        c = self.SCAL_VALS[val]
        return self.SCAL[0:p, c:c + 1]

    def _ln_pipe(self, tile_fn, apply_cb, n_feat, ktiles, cast_X=None,
                 load_cb=None):
        kb, T = self.kb, self.T
        P = self.P
        RS_sb, NMR_sb = self.RS, self.NMR
        stats = [(P[0], P[1]), (P[2], P[3])]
        bcb = [(P[4], P[5]), (P[6], P[7])]
        rowmath_done = [0, 0]
        apply_gd = [0, 0]
        row_free = 0
        for ci, t0 in enumerate(range(0, T, 512)):
            w = min(512, T - t0)
            sl = ci % 2
            PA, PB = stats[sl]
            PC, PD = bcb[sl]
            if load_cb is not None:
                get = load_cb(t0, w)
            xmm = [0, 0]
            for k in range(ktiles):
                xsl = k % 2
                xo = xsl * 512
                if cast_X is not None or load_cb is not None:
                    srcap = (cast_X[:, k * T + t0:k * T + t0 + w]
                             if cast_X is not None else get(k, w))
                    vw = [(self.tm, xmm[xsl])] if xmm[xsl] else []
                    kb.op("v", "tensor_copy", self.XB[:, xo:xo + w],
                          srcap, waits=vw)
                    cast_v = kb.last("v")
                    kb.op("s", "activation", self.XQ[:, xo:xo + w],
                          self.XB[:, xo:xo + w], AF.Square, waits=[cast_v])
                    xb = self.XB[:, xo:xo + w]
                    kb.wait("t", *cast_v)
                else:
                    xb = tile_fn(k, t0, w)
                    if xmm[xsl]:
                        kb.wait("s", self.tm, xmm[xsl])
                    kb.op("s", "activation", self.XQ[:, xo:xo + w], xb,
                          AF.Square)
                sq = kb.last("s")
                if k == 0 and rowmath_done[sl]:
                    kb.wait("t", kb.self_sem["v"], rowmath_done[sl])
                kb.op("t", "matmul", PA[0:1, 0:w], self.ONES_COL[:, :], xb,
                      start=(k == 0), stop=(k == ktiles - 1))
                kb.wait("t", *sq)
                kb.op("t", "matmul", PB[0:1, 0:w], self.ONES_COL[:, :],
                      self.XQ[:, xo:xo + w], start=(k == 0),
                      stop=(k == ktiles - 1))
                xmm[xsl] = kb.mark("t", self.tm)
            smm = xmm[(ktiles - 1) % 2]
            mu = self.R_MU[:, 0:w]
            ms = self.R_MS[:, 0:w]
            nm = self.R_NMR[:, 0:w]
            if row_free:
                kb.wait("v", self.tm, row_free)
            kb.wait("v", self.tm, smm)
            kb.op("v", "tensor_scalar", mu, PA[0:1, 0:w], 1.0 / n_feat, None,
                  OP.mult)
            kb.op("v", "scalar_tensor_tensor", nm, PA[0:1, 0:w],
                  -1.0 / n_feat, mu, OP.mult, OP.mult)
            kb.op("v", "scalar_tensor_tensor", ms, PB[0:1, 0:w],
                  1.0 / n_feat, nm, OP.mult, OP.add)
            kb.op("s", "activation", ms, ms, AF.Ln, bias=self.kf(1e-5, 1),
                  waits=[kb.last("v")])
            kb.op("s", "activation", ms, ms, AF.Exp, scale=self.kf(-0.5, 1))
            kb.op("v", "scalar_tensor_tensor", nm, PA[0:1, 0:w],
                  -1.0 / n_feat, ms, OP.mult, OP.mult, waits=[kb.last("s")])
            rowmath_done[sl] = kb.last("v")[1]
            kb.wait("t", kb.self_sem["v"], rowmath_done[sl])
            if apply_gd[sl]:
                kb.wait("t", kb.self_sem["v"], apply_gd[sl])
            kb.op("t", "matmul", PC[:, 0:w], self.ONES_ROW32[0:1, :], ms,
                  start=True, stop=True)
            kb.op("t", "matmul", PD[:, 0:w], self.ONES_ROW32[0:1, :], nm,
                  start=True, stop=True)
            row_free = kb.mark("t", self.tm)
            kb.wait("v", self.tm, row_free)
            self.RS, self.NMR = PC, PD
            apply_cb(t0, w)
            self.RS, self.NMR = RS_sb, NMR_sb
            apply_gd[sl] = kb.self_sem["v"].n

    def ln_chunked(self, get_tile, apply_cb, n_feat=D, ktiles=DT):
        if isinstance(get_tile, _CastSrc):
            self._ln_pipe(None, apply_cb, n_feat, ktiles, cast_X=get_tile.X)
        else:
            self._ln_pipe(get_tile, apply_cb, n_feat, ktiles)

    def ln_stream(self, load_cb, apply_cb, n_feat=D, ktiles=DT):
        self._ln_pipe(None, apply_cb, n_feat, ktiles, load_cb=load_cb)

    def cast_tile_cb(self, X):
        return _CastSrc(X)


class _CastSrc:
    def __init__(self, X):
        self.X = X


def vcol(kb, name, dram, n=D, cc=None):
    t = kb.sb(name, [128, n // 128], F32)
    sem = cc.dgi if cc is not None else kb.sem("dma_in_g")
    kb.op("g", "dma_start", out=t[:, :], in_=dram.ap().rearrange("(k p) -> p k", p=128),
          incs=[(sem, 16)], selfsync=False,
          allow_slow_non_contiguous=True)
    return t


def vrow(kb, name, dram, n, dt=BF16, cc=None):
    t = kb.sb(name, [1, n], dt)
    sem = cc.dgi if cc is not None else kb.sem("dma_in_g")
    kb.op("g", "dma_start", out=t[:, :], in_=dram.ap().rearrange("(o n) -> o n", o=1),
          incs=[(sem, 16)], selfsync=False)
    return t


def emit_l0f(kb, cc, HD, nbatch):
    T, TC = cc.T, cc.TC
    P = cc.P
    NHP = 8

    X = kb.sb("l0f_X", [128, DT * T], F32)
    ABF = kb.sb("l0f_ABF", [128, 2 * DT * T], BF16)
    XN = ABF[:, 0 * DT * T:1 * DT * T]
    CXT = ABF[:, 1 * DT * T:2 * DT * T]
    AOF = CXT
    if DT * T >= 2 * DT * 1024:
        WEA = ABF[:, 1 * DT * T:1 * DT * T + 2 * DT * 1024]
    else:
        WEA = kb.sb("l0f_WEA", [128, 2 * DT * 1024], BF16)
    QK = kb.sb("l0f_QK", [128, max(2 * T, DT * 512)], BF16)
    QH = QK[:, 0:T]
    KH = QK[:, T:2 * T]
    VT = kb.sb("l0f_VT", [128, TC * 130], BF16)
    ATT = kb.sb("l0f_ATT", [128, 2 * 512], BF16)
    CTOK = kb.sb("l0f_CTOK", [128, 4 * 64], BF16)
    WPRJ = kb.sb("l0f_WPRJ", [128, 3 * DT * 128], BF16)
    WO = kb.sb("l0f_WO", [128, DT * 1024], BF16)
    WG = kb.sb("l0f_WG", [128, DT * 8], BF16)
    U1 = kb.sb("l0f_U1", [128, 512], F32)
    GP = kb.sb("l0f_GP", [128, 8], F32)
    GM = kb.sb("l0f_GM", [128, 2], F32)
    MSKK = kb.sb("l0f_MSKK", [128, TC * 8], BF16)
    MSKB = kb.sb("l0f_MSKB", [8, T], BF16)
    MSK_DR = kb.dram("l0f_mskdr", [8, T], BF16)
    MBC8 = kb.sb("l0f_MBC8", [128, 8 * 512], BF16)
    BGBC = kb.sb("l0f_BGBC", [128, 8], F32)
    XAE = kb.sb("l0f_XAE", [128, DT * 512], BF16)
    MO = QK[:, 0:DT * 512]
    BEX = kb.sb("l0f_BEX", [8, D], BF16)

    g1 = vcol(kb, "l0f_g1", HD["L0_ffn_g1"], cc=cc)
    be1 = vcol(kb, "l0f_be1", HD["L0_ffn_be1"], cc=cc)
    g2 = vcol(kb, "l0f_g2", HD["L0_ffn_g2"], cc=cc)
    be2 = vcol(kb, "l0f_be2", HD["L0_ffn_be2"], cc=cc)
    bq = vcol(kb, "l0f_bq", HD["L0_ffn_bq"], cc=cc)
    bk = vcol(kb, "l0f_bk", HD["L0_ffn_bk"], cc=cc)
    bo = vcol(kb, "l0f_bo", HD["L0_ffn_bo"], cc=cc)
    bv_row = vrow(kb, "l0f_bv", HD["L0_ffn_bv"], D, BF16, cc=cc)
    bg_row = vrow(kb, "l0f_bg", HD["L0_ffn_bg"], 8, F32, cc=cc)
    kb.dma("g", BEX[:, :], HD["L0_ffn_bexp"][:, :], cc.dgi)
    kb.dma("g", WG[:, :].rearrange("p (k e) -> p k e", e=8),
           HD["L0_ffn_Wg"].ap().rearrange("(k p) e -> p k e", p=128), cc.dgi)
    wq_r = HD["L0_ffn_Wq"].ap().rearrange("(k p) (hp c) -> p k hp c", p=128, c=128)
    wk_r = HD["L0_ffn_Wk"].ap().rearrange("(k p) (hp c) -> p k hp c", p=128, c=128)
    wv_r = HD["L0_ffn_Wv"].ap().rearrange("(k p) (hp c) -> p k hp c", p=128, c=128)
    wo_r = HD["L0_ffn_Wo"].ap().rearrange("(k p) o -> p k o", p=128)
    we_r = HD["L0_ffn_We"].ap().rearrange("e (k p) o -> e p k o", p=128)
    WE_BF = kb.dram("l0f_we_bf", [8, 128, DT * 1024], BF16)

    kb.barrier()
    for e in ("v", "s", "t", "g"):
        kb.wait(e, cc.dsem, cc.dsem.n)
        kb.wait(e, cc.dgi, cc.dgi.n)

    if not getattr(kb, "_l0f_we_cast", False):
        kb._l0f_we_cast = True
        sv_slot = [0, 0]
        for e8 in range(8):
            sl = e8 % 2
            buf = WEA[:, sl * DT * 512:sl * DT * 512 + DT * 512]
            for half in range(2):
                war = [(cc.dout, sv_slot[sl])] if sv_slot[sl] else []
                ld = kb.dma("g", buf.rearrange("p (k o) -> p k o", o=512),
                            we_r[e8][:, :, half * 512:(half + 1) * 512],
                            cc.dws, waits=war)
                sv_slot[sl] = kb.dma(
                    "y", WE_BF.ap()[e8].rearrange("p (k o) -> p k o", o=1024
                                                  )[:, :, half * 512:(half + 1) * 512],
                    buf.rearrange("p (k o) -> p k o", o=512), cc.dout,
                    waits=[(cc.dws, ld)])
        kb.wait("g", cc.dout, cc.dout.n)
        kb.wait("y", cc.dout, cc.dout.n)

    for b in range(nbatch):
        xld = kb.dma("y", X[:, :].rearrange("p (k t) -> p k t", k=DT),
                     HD["xin"][b].rearrange("(k p) t -> p k t", p=128), cc.dsem)
        kb.wait("v", cc.dsem, xld)
        def apply_a(t0, w):
            for k in range(DT):
                sl = slice(k * T + t0, k * T + t0 + w)
                kb.op("v", "tensor_mul", U1[:, 0:w], X[:, sl], cc.RS[:, 0:w])
                kb.op("v", "tensor_add", XN[:, sl], U1[:, 0:w], cc.NMR[:, 0:w])
        cc.ln_chunked(cc.cast_tile_cb(X), apply_a)
        kb.barrier()

        wprj_free = 0
        for hp in range(NHP):
            if wprj_free:
                kb.wait("g", cc.tm, wprj_free)
            kb.dma("g", WPRJ[:, 0 * DT * 128:1 * DT * 128].rearrange("p (k c) -> p k c", c=128), wq_r[:, :, hp, :], cc.dws)
            kb.dma("g", WPRJ[:, 1 * DT * 128:2 * DT * 128].rearrange("p (k c) -> p k c", c=128), wk_r[:, :, hp, :], cc.dws)
            wl = kb.dma("g", WPRJ[:, 2 * DT * 128:3 * DT * 128].rearrange(
                "p (k c) -> p k c", c=128), wv_r[:, :, hp, :], cc.dws)
            kb.wait("t", cc.dws, wl)
            kb.op("v", "memset",
                  VT[:, :].rearrange("p (tc c) -> p tc c", c=130)[:, :, 64:65], 1.0)
            kb.wait("t", *kb.last("v"))
            for t0 in range(0, T, 512):
                w = min(512, T - t0)
                for k in range(DT):
                    kb.op("t", "matmul", P[4][:, 0:w],
                          WPRJ[:, k * 128:(k + 1) * 128],
                          XN[:, k * T + t0:k * T + t0 + w],
                          start=(k == 0), stop=(k == DT - 1))
                for k in range(DT):
                    kb.op("t", "matmul", P[5][:, 0:w],
                          WPRJ[:, DT * 128 + k * 128:DT * 128 + (k + 1) * 128],
                          XN[:, k * T + t0:k * T + t0 + w],
                          start=(k == 0), stop=(k == DT - 1))
                tv = kb.mark("t", cc.tm)
                kb.wait("v", cc.tm, tv)
                kb.op("v", "tensor_scalar", QH[:, t0:t0 + w], P[4][:, 0:w],
                      bq[:, hp:hp + 1], None, OP.add)
                kb.op("v", "tensor_scalar", KH[:, t0:t0 + w], P[5][:, 0:w],
                      bk[:, hp:hp + 1], None, OP.add)
                kb.wait("t", *kb.last("v"))
            for tc in range(TC):
                t0 = tc * 128
                for k in range(DT):
                    kb.op("t", "matmul", P[6][:, 0:128],
                          XN[:, k * T + t0:k * T + t0 + 128],
                          WPRJ[:, 2 * DT * 128 + k * 128:2 * DT * 128 + (k + 1) * 128],
                          start=(k == 0), stop=False)
                kb.op("t", "matmul", P[6][:, 0:128], cc.ONES_ROWB[0:1, :],
                      bv_row[0:1, hp * 128:(hp + 1) * 128], start=False, stop=True)
                tv = kb.mark("t", cc.tm)
                kb.wait("v", cc.tm, tv)
                kb.op("v", "tensor_copy", VT[:, tc * 130:tc * 130 + 64],
                      P[6][:, 0:64])
                kb.op("v", "tensor_copy", VT[:, tc * 130 + 65:tc * 130 + 129],
                      P[6][:, 64:128])
                kb.wait("t", *kb.last("v"))
            wprj_free = cc.tm.n
            esem = kb.sem("l0f_exp")
            for hh in range(2):
                fb = hh * 64
                for tq in range(0, T, 512):
                    wq = min(512, T - tq)
                    nq = (wq + 127) // 128
                    exp_mark = [0, 0]
                    ctx_mark = [0, 0]
                    for tk in range(TC):
                        pb = tk % 2
                        if tk >= 2:
                            kb.wait("t", kb.self_sem["s"], exp_mark[pb])
                        kb.op("t", "matmul", P[4 + pb][:, 0:wq],
                              KH[fb:fb + 64, tk * 128:(tk + 1) * 128],
                              QH[fb:fb + 64, tq:tq + wq], start=True, stop=True)
                        tv = kb.mark("t", cc.tm)
                        kb.wait("s", cc.tm, tv)
                        if tk >= 2:
                            kb.wait("s", cc.tm, ctx_mark[pb])
                        kb.op("s", "activation", ATT[:, pb * 512:pb * 512 + wq],
                              P[4 + pb][:, 0:wq], AF.Exp, bias=cc.kf(-10.0),
                              scale=cc.kf(0.125))
                        exp_mark[pb] = kb.last("s")[1]
                        kb.wait("t", kb.self_sem["s"], exp_mark[pb])
                        for q in range(nq):
                            kb.op("t", "matmul", P[q][:, 0:65],
                                  ATT[:, pb * 512 + q * 128:pb * 512 + min((q + 1) * 128, wq)],
                                  VT[:, tk * 130 + 64 * hh:tk * 130 + 64 * hh + 65],
                                  start=(tk == 0), stop=(tk == TC - 1))
                        ctx_mark[pb] = kb.mark("t", cc.tm)
                    kb.wait("v", cc.tm, cc.tm.n)
                    rcol, c0 = (64, 0) if hh == 0 else (0, 1)
                    for q in range(nq):
                        kb.op("v", "reciprocal", U1[:, q:q + 1],
                              P[q][:, rcol:rcol + 1])
                        kb.op("v", "tensor_scalar", CTOK[:, q * 64:(q + 1) * 64],
                              P[q][:, c0:c0 + 64], U1[:, q:q + 1], None, OP.mult)
                    kb.wait("t", *kb.last("v"))
                    PT = P[7][:, :].bitcast(BF16)
                    for q in range(nq):
                        kb.op("t", "transpose", PT[0:64, q * 128:(q + 1) * 128],
                              CTOK[:, q * 64:(q + 1) * 64], cc.IDENT[:, :])
                    tv = kb.mark("t", cc.tm)
                    kb.wait("v", cc.tm, tv)
                    kb.op("v", "tensor_copy",
                          CXT[fb:fb + 64, hp * T + tq:hp * T + tq + wq],
                          PT[0:64, 0:wq])
                    kb.wait("t", *kb.last("v"))
        kb.barrier()

        wl = kb.dma("g", WO[:, :].rearrange("p (k o) -> p k o", o=1024),
                    wo_r, cc.dws)
        kb.wait("t", cc.dws, wl)
        for t0 in range(0, T, 512):
            w = min(512, T - t0)
            for oc in range(DT):
                for k in range(DT):
                    kb.op("t", "matmul", P[oc][:, 0:w],
                          WO[:, k * 1024 + oc * 128:k * 1024 + (oc + 1) * 128],
                          CXT[:, k * T + t0:k * T + t0 + w],
                          start=(k == 0), stop=(k == DT - 1))
            tv = kb.mark("t", cc.tm)
            kb.wait("v", cc.tm, tv)
            for oc in range(DT):
                kb.op("v", "tensor_scalar", AOF[:, oc * T + t0:oc * T + t0 + w],
                      P[oc][:, 0:w], bo[:, oc:oc + 1], None, OP.add)
            kb.wait("t", *kb.last("v"))
        kb.barrier()
        def apply_c(t0, w):
            for k in range(DT):
                sl = slice(k * T + t0, k * T + t0 + w)
                kb.op("v", "tensor_mul", U1[:, 0:w], AOF[:, sl], cc.RS[:, 0:w])
                kb.op("v", "tensor_add", U1[:, 0:w], U1[:, 0:w], cc.NMR[:, 0:w])
                kb.op("v", "tensor_scalar", U1[:, 0:w], U1[:, 0:w],
                      g1[:, k:k + 1], be1[:, k:k + 1], OP.mult, OP.add)
                kb.op("v", "tensor_add", U1[:, 0:w], U1[:, 0:w], XN[:, sl])
                kb.op("v", "scalar_tensor_tensor", X[:, sl], U1[:, 0:w], 0.5,
                      X[:, sl], OP.mult, OP.add)
                kb.op("v", "tensor_copy", XN[:, sl], U1[:, 0:w])
        cc.ln_chunked(lambda k, t0, w: AOF[:, k * T + t0:k * T + t0 + w], apply_c)
        kb.barrier()

        kb.wait("t", *kb.last("v"))
        kb.op("t", "matmul", P[6][:, 0:8], cc.ONES_ROW32[0:1, :],
              bg_row[0:1, 0:8], start=True, stop=True)
        tv = kb.mark("t", cc.tm)
        kb.wait("v", cc.tm, tv)
        kb.op("v", "tensor_copy", BGBC[:, 0:8], P[6][:, 0:8])
        kb.wait("t", *kb.last("v"))
        NT8 = TC * 8
        for tc in range(TC):
            t0 = tc * 128
            for k in range(DT):
                kb.op("t", "matmul", P[6][:, tc * 8:(tc + 1) * 8],
                      XN[:, k * T + t0:k * T + t0 + 128], WG[:, k * 8:(k + 1) * 8],
                      start=(k == 0), stop=(k == DT - 1))
        tv = kb.mark("t", cc.tm)
        kb.wait("v", cc.tm, tv)
        GPA = U1[:, 0:NT8]
        UA = U1[:, 128:128 + NT8]
        GMA = U1[:, 256:256 + TC]
        GMB = U1[:, 288:288 + TC]
        g3 = GPA.rearrange("p (tc e) -> p tc e", e=8)
        u3 = UA.rearrange("p (tc e) -> p tc e", e=8)
        def bc3(ap):
            return ap.rearrange("p (tc o) -> p tc o", o=1).broadcast_to(
                [128, TC, 8])
        kb.op("v", "tensor_tensor", g3, P[6][:, 0:NT8].rearrange(
            "p (tc e) -> p tc e", e=8),
            BGBC[:, 0:8].rearrange("p (o e) -> p o e", o=1).broadcast_to(
                [128, TC, 8]), OP.add)
        kb.wait("t", *kb.last("v"))
        kb.op("v", "tensor_reduce", GMA, g3, AX.X, OP.max)
        kb.op("v", "tensor_tensor", g3, g3, bc3(GMA), OP.subtract)
        kb.op("s", "activation", GPA, GPA, AF.Exp, waits=[kb.last("v")])
        kb.op("v", "tensor_reduce", GMB, g3, AX.X, OP.add,
              waits=[kb.last("s")])
        kb.op("v", "reciprocal", GMB, GMB)
        kb.op("v", "tensor_tensor", g3, g3, bc3(GMB), OP.mult)
        kb.op("v", "tensor_reduce", GMA, g3, AX.X, OP.max)
        kb.op("v", "tensor_tensor", u3, g3, bc3(GMA), OP.is_equal)
        kb.op("v", "scalar_tensor_tensor", u3, u3, -1e30, g3,
              OP.mult, OP.add)
        kb.op("v", "tensor_reduce", GMB, u3, AX.X, OP.max)
        kb.op("v", "tensor_tensor", u3, g3, bc3(GMB), OP.is_ge)
        kb.op("v", "tensor_tensor", u3, u3, g3, OP.mult)
        kb.op("v", "tensor_copy", MSKK[:, 0:NT8], UA)
        kb.wait("t", *kb.last("v"))
        PT = P[7][:, :].bitcast(BF16)
        PT6D = P[6][:, :].bitcast(BF16)
        dmsk = [0, 0]
        for tc in range(TC):
            t0 = tc * 128
            slb = [PT6D, PT][tc % 2]
            if dmsk[tc % 2]:
                kb.wait("t", kb.self_sem["v"], dmsk[tc % 2])
            kb.op("t", "transpose", slb[0:8, 0:128],
                  MSKK[:, tc * 8:(tc + 1) * 8], cc.IDENT[:, :])
            tv = kb.mark("t", cc.tm)
            kb.wait("v", cc.tm, tv)
            kb.op("v", "tensor_copy", MSKB[0:8, t0:t0 + 128], slb[0:8, 0:128])
            dmsk[tc % 2] = kb.last("v")[1]
        kb.wait("t", *kb.last("v"))
        kb.barrier()
        mf = kb.dma("y", MSK_DR[:, :], MSKB[0:8, :], cc.dsem)
        kb.wait("y", cc.dsem, mf)
        kb.barrier()

        mbc_free = 0
        for t0 in range(0, T, 512):
            w = min(512, T - t0)
            if mbc_free:
                kb.wait("y", kb.self_sem["v"], mbc_free)
            for e in range(8):
                kb.dma("y", MBC8[:, e * 512:e * 512 + w],
                       MSK_DR[e:e + 1, t0:t0 + w].broadcast_to([128, w]), cc.dsem)
            mbca = cc.dsem.n
            kb.wait("v", cc.dsem, mbca)
            kb.wait("g", cc.dsem, mbca)
            if t0 == 0:
                wea_used = [0, 0]
                wl_done = [0, 0]
                wl_done[0] = kb.dma("g", WEA[:, 0:DT * 1024].rearrange(
                    "p (k o) -> p k o", o=1024), WE_BF.ap()[0], cc.dws)
                wl_done[1] = kb.dma("g", WEA[:, DT * 1024:2 * DT * 1024].rearrange(
                    "p (k o) -> p k o", o=1024), WE_BF.ap()[1], cc.dws)
            prev_mark = 0
            for e in range(8):
                sl = e % 2
                if prev_mark:
                    kb.wait("v", cc.tm, prev_mark)
                for k in range(DT):
                    kb.op("v", "tensor_mul", XAE[:, k * 512:k * 512 + w],
                          XN[:, k * T + t0:k * T + t0 + w],
                          MBC8[:, e * 512:e * 512 + w])
                kb.wait("t", cc.dws, wl_done[sl])
                kb.wait("t", *kb.last("v"))
                for oc in range(DT):
                    for k in range(DT):
                        kb.op("t", "matmul", P[oc][:, 0:w],
                              WEA[:, sl * DT * 1024 + k * 1024 + oc * 128:
                                  sl * DT * 1024 + k * 1024 + (oc + 1) * 128],
                              XAE[:, k * 512:k * 512 + w],
                              start=(e == 0 and k == 0), stop=False)
                prev_mark = kb.mark("t", cc.tm)
                wea_used[sl] = prev_mark
                last_chunk = (t0 + 512 >= T)
                if not (last_chunk and e >= 6):
                    wl_done[sl] = kb.dma(
                        "g",
                        WEA[:, sl * DT * 1024:(sl + 1) * DT * 1024].rearrange(
                            "p (k o) -> p k o", o=1024),
                        WE_BF.ap()[(e + 2) % 8], cc.dws,
                        waits=[(cc.tm, wea_used[sl])])
                if e == 7:
                    mbc_free = kb.last("v")[1]
            for oc in range(DT):
                kb.op("t", "matmul", P[oc][:, 0:w], BEX[0:8, oc * 128:(oc + 1) * 128],
                      MSKB[0:8, t0:t0 + w], start=False, stop=True)
            tv = kb.mark("t", cc.tm)
            kb.wait("s", cc.tm, tv)
            for oc in range(DT):
                kb.op("s", "activation", MO[:, oc * 512:oc * 512 + w], P[oc][:, 0:w],
                      AF.Copy)
            kb.wait("t", *kb.last("s"))
            for k in range(DT):
                kb.op("s", "activation", cc.XQ[:, 0:w], MO[:, k * 512:k * 512 + w],
                      AF.Square)
                kb.wait("t", *kb.last("s"))
                kb.op("t", "matmul", P[0][0:1, 0:w], cc.ONES_COL[:, :],
                      MO[:, k * 512:k * 512 + w], start=(k == 0), stop=(k == DT - 1))
                kb.op("t", "matmul", P[1][0:1, 0:w], cc.ONES_COL[:, :],
                      cc.XQ[:, 0:w], start=(k == 0), stop=(k == DT - 1))
                tv = kb.mark("t", cc.tm)
                kb.wait("s", cc.tm, tv)
            tv = cc.tm.n
            kb.wait("v", cc.tm, tv)
            kb.op("v", "tensor_scalar", cc.R_MU[:, 0:w], P[0][0:1, 0:w],
                  1.0 / D, None, OP.mult)
            kb.op("v", "tensor_scalar", cc.R_MS[:, 0:w], P[1][0:1, 0:w],
                  1.0 / D, None, OP.mult)
            kb.op("v", "scalar_tensor_tensor", cc.R_NMR[:, 0:w],
                  cc.R_MU[:, 0:w], -1.0, cc.R_MU[:, 0:w],
                  OP.mult, OP.mult)
            kb.op("v", "tensor_add", cc.R_MS[:, 0:w],
                  cc.R_MS[:, 0:w], cc.R_NMR[:, 0:w])
            kb.op("s", "activation", cc.R_MS[:, 0:w],
                  cc.R_MS[:, 0:w], AF.Ln, bias=cc.kf(1e-5, 1),
                  waits=[kb.last("v")])
            kb.op("s", "activation", cc.R_RS[:, 0:w],
                  cc.R_MS[:, 0:w], AF.Exp, scale=cc.kf(-0.5, 1))
            kb.op("v", "scalar_tensor_tensor", cc.R_NMR[:, 0:w],
                  cc.R_MU[:, 0:w], -1.0, cc.R_RS[:, 0:w],
                  OP.mult, OP.mult, waits=[kb.last("s")])
            kb.wait("t", *kb.last("v"))
            kb.op("t", "matmul", P[2][:, 0:w], cc.ONES_ROW32[0:1, :],
                  cc.R_RS[:, 0:w], start=True, stop=True)
            kb.op("t", "matmul", P[3][:, 0:w], cc.ONES_ROW32[0:1, :],
                  cc.R_NMR[:, 0:w], start=True, stop=True)
            tv = kb.mark("t", cc.tm)
            kb.wait("v", cc.tm, tv)
            for k in range(DT):
                sl = slice(k * T + t0, k * T + t0 + w)
                kb.op("v", "tensor_mul", U1[:, 0:w], MO[:, k * 512:k * 512 + w],
                      P[2][:, 0:w])
                kb.op("v", "tensor_add", U1[:, 0:w], U1[:, 0:w], P[3][:, 0:w])
                kb.op("v", "tensor_scalar", U1[:, 0:w], U1[:, 0:w],
                      g2[:, k:k + 1], be2[:, k:k + 1], OP.mult, OP.add)
                kb.op("v", "scalar_tensor_tensor", X[:, sl], U1[:, 0:w], 0.5,
                      X[:, sl], OP.mult, OP.add)
            kb.wait("t", *kb.last("v"))
        kb.barrier()

        so = kb.dma("y", HD["xout"][b].rearrange("(k p) t -> p k t", p=128),
                    X[:, :].rearrange("p (k t) -> p k t", k=DT), cc.dout)
        kb.wait("y", cc.dout, so)
    return


def build_l0f(T, nbatch):
    nc = bass.Bass("TRN2", target_bir_lowering=False)
    HD = {}
    HD["xin"] = nc.dram_tensor("xin", [nbatch, D, T], F32, kind="ExternalInput").ap()
    HD["xout"] = nc.dram_tensor("xout", [nbatch, D, T], F32, kind="ExternalOutput").ap()
    for nm, shp in [("L0_ffn_Wq", [D, D]), ("L0_ffn_bq", [D]),
                    ("L0_ffn_Wk", [D, D]), ("L0_ffn_bk", [D]),
                    ("L0_ffn_Wv", [D, D]), ("L0_ffn_bv", [D]),
                    ("L0_ffn_Wo", [D, D]), ("L0_ffn_bo", [D]),
                    ("L0_ffn_g1", [D]), ("L0_ffn_be1", [D]),
                    ("L0_ffn_g2", [D]), ("L0_ffn_be2", [D]),
                    ("L0_ffn_Wg", [D, 8]), ("L0_ffn_bg", [8]),
                    ("L0_ffn_We", [8, D, D]), ("L0_ffn_bexp", [8, D])]:
        HD[nm] = nc.dram_tensor(nm, shp, F32, kind="ExternalInput")
    kb = KB(nc)
    cc = CC(kb, T)
    emit_l0f(kb, cc, HD, nbatch)
    kb.finalize()
    return nc



E_IN = 2048
ET = E_IN // 128
NST = 16
RNK = 64


L0A_SKIP = ()


def emit_l0a(kb, cc, HD, nbatch):
    T, TC = cc.T, cc.TC
    P = cc.P
    nc = kb.nc

    X = kb.sb("l0a_X", [128, DT * T], F32)
    HN = kb.sb("l0a_HN", [128, DT * T], BF16)
    XCR = kb.sb("l0a_XCR", [128, T + 8], BF16)
    XCT = kb.sb("l0a_XCT", [128, 2 * T], BF16)
    ZST = kb.sb("l0a_ZST", [128, T], BF16)
    DTR_PS = None
    DTROW = kb.sb("l0a_DTROW", [64, T], BF16)
    B16 = kb.sb("l0a_B16", [16, T], BF16)
    C16 = kb.sb("l0a_C16", [16, T], BF16)
    BREP = kb.sb("l0a_BREP", [128, T], BF16)
    CREP = kb.sb("l0a_CREP", [128, T], BF16)
    DTB = kb.sb("l0a_DTB", [128, T], BF16)
    UB = kb.sb("l0a_UB", [128, T], BF16)
    DTF = kb.sb("l0a_DTF", [128, 512], F32)
    DTF2 = kb.sb("l0a_DTF2", [128, 512], F32)
    DA = kb.sb("l0a_DA", [128, 2 * 512], BF16)
    DBU = kb.sb("l0a_DBU", [128, 2 * 512], BF16)
    HS = kb.sb("l0a_HS", [128, 2 * 512], F32)
    HSLAST = kb.sb("l0a_HSLAST", [128, 16], F32)
    HCB = kb.sb("l0a_HCB", [128, 2 * 512], BF16)
    ACOL = kb.sb("l0a_ACOL", [128, 2 * NST], F32)
    SEL = kb.sb("l0a_SEL", [128, 16 * 128], BF16)
    SELT = kb.sb("l0a_SELT", [128, 16 * 128], BF16)
    S16 = kb.sb("l0a_S16", [16, 128], BF16)
    WTB = kb.sb("l0a_WTB", [128, 8 * 128], BF16)
    WTB2 = kb.sb("l0a_WTB2", [128, 16 * 128], BF16)
    TSTG = kb.sb("l0a_TSTG", [128, 1024], BF16)
    YF = kb.sb("l0a_YF", [128, T], BF16)
    U1 = kb.sb("l0a_U1", [128, 512], F32)
    XI = kb.sb("l0a_XI", [128, DT * 512], F32)
    LNW = kb.sb("l0a_LNW", [128, 3 * DT], F32)
    LNB = kb.sb("l0a_LNB", [128, 3 * DT], F32)
    DCOL = kb.sb("l0a_DCOL", [128, ET], F32)
    DTPB = kb.sb("l0a_DTPB", [128, ET], F32)
    CVW = kb.sb("l0a_CVW", [128, 4 * ET], F32)
    CVB = kb.sb("l0a_CVB", [128, ET], F32)

    WT_IN = kb.dram("l0a_wt_in", [2, 32, 128, 8 * 128], BF16)
    WT_OUT = kb.dram("l0a_wt_out", [2, DT, 128, 16 * 128], BF16)
    WT_XP = kb.dram("l0a_wt_xp", [2, 16, 128, 96], BF16)
    WT_DTP = kb.dram("l0a_wt_dtp", [2, 16, 64, 128], BF16)
    XC_DR = kb.dram("l0a_xc_dr", [ET, 128, T], BF16)
    Z_DR = kb.dram("l0a_z_dr", [ET, 128, T], BF16)
    YF_DR = kb.dram("l0a_yf_dr", [ET, 128, T], BF16)

    for i in range(2):
        kb.op("g", "dma_start", out=LNW[:, i * DT:(i + 1) * DT],
              in_=HD["L0_attn_ln_w"][i].rearrange("(k p) -> p k", p=128),
              incs=[(cc.dgi, 16)], selfsync=False, allow_slow_non_contiguous=True)
        kb.op("g", "dma_start", out=LNB[:, i * DT:(i + 1) * DT],
              in_=HD["L0_attn_ln_b"][i].rearrange("(k p) -> p k", p=128),
              incs=[(cc.dgi, 16)], selfsync=False, allow_slow_non_contiguous=True)
    kb.op("g", "dma_start", out=LNW[:, 2 * DT:3 * DT],
          in_=HD["L0_attn_norm_f_w"].ap().rearrange("(k p) -> p k", p=128),
          incs=[(cc.dgi, 16)], selfsync=False, allow_slow_non_contiguous=True)
    kb.op("g", "dma_start", out=LNB[:, 2 * DT:3 * DT],
          in_=HD["L0_attn_norm_f_b"].ap().rearrange("(k p) -> p k", p=128),
          incs=[(cc.dgi, 16)], selfsync=False, allow_slow_non_contiguous=True)
    kb.dma("y", SEL[:, :], HD["c_sel"][:, :], cc.dsem)
    kb.dma("y", SELT[:, :], HD["c_selt"][:, :], cc.dsem)
    kb.dma("y", S16[:, :], HD["c_s16"][:, :], cc.dsem)
    kb.barrier()
    for e in ("v", "s", "t", "g"):
        kb.wait(e, cc.dsem, cc.dsem.n)
        kb.wait(e, cc.dgi, cc.dgi.n)

    PT = P[7][:, :].bitcast(BF16)

    def transpose_to_dram(src_rows_ap, n_in_tiles, dst_ap, m_cols):
        pass

    tprev = 0
    for i in range(2) if "pre" not in L0A_SKIP else ():
        inw_r = HD["L0_attn_in_proj_w"][i].rearrange("(ec p) d -> ec p d", p=128)
        for ec in range(32):
            if tprev:
                kb.wait("g", cc.tm, tprev)
            wl = kb.dma("g", TSTG[:, 0:1024], inw_r[ec], cc.dws)
            kb.wait("t", cc.dws, wl)
            for k in range(DT):
                kb.op("t", "transpose", PT[:, k * 128:(k + 1) * 128],
                      TSTG[:, k * 128:(k + 1) * 128], cc.IDENT[:, :])
            tprev = kb.mark("t", cc.tm)
            kb.wait("v", cc.tm, tprev)
            kb.op("v", "tensor_copy", WTB[:, :], PT[:, 0:1024])
            kb.wait("t", *kb.last("v"))
            sv = kb.dma("y", WT_IN[i, ec], WTB[:, :], cc.dout, waits=[kb.last("v")])
            kb.wait("v", cc.dout, sv)
        outw_r = HD["L0_attn_out_proj_w"][i].rearrange("(dc p) e -> dc p e", p=128)
        for dc in range(DT):
            kb.wait("g", cc.tm, tprev)
            wl = kb.dma("g", TSTG[:, 0:1024], outw_r[dc][:, 0:1024], cc.dws)
            wl2 = kb.dma("g", WTB2[:, 0:1024], outw_r[dc][:, 1024:2048], cc.dws)
            kb.wait("t", cc.dws, wl2)
            for k in range(8):
                kb.op("t", "transpose", PT[:, k * 128:(k + 1) * 128],
                      TSTG[:, k * 128:(k + 1) * 128], cc.IDENT[:, :])
            tv = kb.mark("t", cc.tm)
            kb.wait("v", cc.tm, tv)
            kb.op("v", "tensor_copy", TSTG[:, 0:1024], PT[:, 0:1024])
            kb.wait("t", *kb.last("v"))
            for k in range(8):
                kb.op("t", "transpose", PT[:, k * 128:(k + 1) * 128],
                      WTB2[:, k * 128:(k + 1) * 128], cc.IDENT[:, :])
            tprev = kb.mark("t", cc.tm)
            kb.wait("v", cc.tm, tprev)
            kb.op("v", "tensor_copy", WTB2[:, 0:1024], PT[:, 0:1024])
            kb.wait("t", *kb.last("v"))
            sv = kb.dma("y", WT_OUT[i, dc][:, 0:1024], TSTG[:, 0:1024], cc.dout,
                        waits=[kb.last("v")])
            sv = kb.dma("y", WT_OUT[i, dc][:, 1024:2048], WTB2[:, 0:1024],
                        cc.dout)
            kb.wait("g", cc.dout, sv)
            kb.wait("v", cc.dout, sv)
        xp_r = HD["L0_attn_x_proj_w"][i]
        for half in range(2):
            kb.wait("g", cc.tm, tprev)
            wl = kb.dma("g", TSTG[0:96, 0:1024],
                        xp_r[:, half * 1024:(half + 1) * 1024], cc.dws)
            kb.wait("t", cc.dws, wl)
            for j in range(8):
                et = half * 8 + j
                kb.op("t", "transpose", PT[:, j * 96:(j + 1) * 96],
                      TSTG[0:96, j * 128:(j + 1) * 128], cc.IDENT[0:96, 0:96])
                tprev = kb.mark("t", cc.tm)
                kb.wait("v", cc.tm, tprev)
                kb.op("v", "tensor_copy", WTB[:, 0:96], PT[:, j * 96:(j + 1) * 96])
                kb.wait("t", *kb.last("v"))
                sv = kb.dma("y", WT_XP[i, et], WTB[:, 0:96], cc.dout, waits=[kb.last("v")])
                kb.wait("v", cc.dout, sv)
        dtp_r = HD["L0_attn_dt_proj_w"][i].rearrange("(et p) r -> et p r", p=128)
        for et in range(16):
            kb.wait("g", cc.tm, tprev)
            wl = kb.dma("g", TSTG[:, 0:64], dtp_r[et], cc.dws)
            kb.wait("t", cc.dws, wl)
            kb.op("t", "transpose", PT[0:64, 0:128], TSTG[:, 0:64],
                  cc.IDENT[:, :])
            tprev = kb.mark("t", cc.tm)
            kb.wait("v", cc.tm, tprev)
            kb.op("v", "tensor_copy", WTB[0:64, 0:128], PT[0:64, 0:128])
            kb.wait("t", *kb.last("v"))
            sv = kb.dma("y", WT_DTP[i, et], WTB[0:64, 0:128], cc.dout, waits=[kb.last("v")])
            kb.wait("v", cc.dout, sv)
    kb.barrier()

    for b in range(nbatch):
        xld = kb.dma("y", XI[:, 0:DT * 512].rearrange("p (k t) -> p k t", k=DT)
                     if False else X[:, :].rearrange("p (k t) -> p k t", k=DT),
                     HD["xin"][b].rearrange("(k p) t -> p k t", p=128), cc.dsem)
        kb.wait("v", cc.dsem, xld)
        kb.wait("s", cc.dsem, xld)

        def apply_pl(t0, w):
            for k in range(DT):
                sl = slice(k * T + t0, k * T + t0 + w)
                kb.op("v", "tensor_mul", U1[:, 0:w], X[:, sl], cc.RS[:, 0:w])
                kb.op("v", "tensor_add", X[:, sl], U1[:, 0:w], cc.NMR[:, 0:w])
        cc.ln_chunked(cc.cast_tile_cb(X), apply_pl)
        kb.barrier()

        for li in range(2):
            kb.op("g", "dma_start",
                  out=CVW[:, :].rearrange("p (k et) -> p k et", et=ET),
                  in_=HD["L0_attn_conv_w"][li].rearrange("(et p) k -> p k et", p=128),
                  incs=[(cc.dgi, 16)], selfsync=False,
                  allow_slow_non_contiguous=True)
            kb.op("g", "dma_start", out=CVB[:, :],
                  in_=HD["L0_attn_conv_b"][li].rearrange("(et p) -> p et", p=128),
                  incs=[(cc.dgi, 16)], selfsync=False,
                  allow_slow_non_contiguous=True)
            kb.op("g", "dma_start", out=DCOL[:, :],
                  in_=HD["L0_attn_D_skip"][li].rearrange("(et p) -> p et", p=128),
                  incs=[(cc.dgi, 16)], selfsync=False,
                  allow_slow_non_contiguous=True)
            kb.op("g", "dma_start", out=DTPB[:, :],
                  in_=HD["L0_attn_dt_proj_b"][li].rearrange("(et p) -> p et", p=128),
                  incs=[(cc.dgi, 16)], selfsync=False,
                  allow_slow_non_contiguous=True)
            for e in ("v", "s", "t"):
                kb.wait(e, cc.dgi, cc.dgi.n)

            def apply_m1(t0, w, li=li):
                for k in range(DT):
                    sl = slice(k * T + t0, k * T + t0 + w)
                    kb.op("v", "tensor_mul", U1[:, 0:w], X[:, sl], cc.RS[:, 0:w])
                    kb.op("v", "tensor_add", U1[:, 0:w], U1[:, 0:w], cc.NMR[:, 0:w])
                    kb.op("v", "tensor_scalar", HN[:, sl], U1[:, 0:w],
                          LNW[:, li * DT + k:li * DT + k + 1],
                          LNB[:, li * DT + k:li * DT + k + 1], OP.mult, OP.add)
            cc.ln_chunked(cc.cast_tile_cb(X), apply_m1)
            kb.barrier()

            NCH2 = (T + 511) // 512
            if "m2" in L0A_SKIP:
                kb.barrier()
            slot_ev = [None] * 4
            wtb_used = [0, 0]
            wtb_ld = [0, 0]
            bufs = [WTB[:, 0:1024], WTB2[:, 0:1024]]
            wtb_ld[0] = kb.dma("g", bufs[0], WT_IN[li, 0], cc.dws)
            xct_sv = 0
            zst_sv = 0
            conv_g = 0
            for ec in range(32) if "m2" not in L0A_SKIP else ():
                wsl = ec % 2
                buf = bufs[wsl]
                kb.wait("t", cc.dws, wtb_ld[wsl])
                is_z = ec >= 16
                et = ec - 16 if is_z else ec
                for t0 in range(0, T, 512):
                    w = min(512, T - t0)
                    pb = t0 // 512
                    if slot_ev[pb] is not None:
                        kb.wait("t", kb.self_sem[slot_ev[pb][0]], slot_ev[pb][1])
                    for k in range(DT):
                        kb.op("t", "matmul", P[pb][:, 0:w],
                              buf[:, k * 128:(k + 1) * 128],
                              HN[:, k * T + t0:k * T + t0 + w],
                              start=(k == 0), stop=(k == DT - 1))
                    tv = kb.mark("t", cc.tm)
                    if is_z:
                        if t0 == 0 and zst_sv:
                            kb.wait("s", cc.dout, zst_sv)
                        kb.wait("s", cc.tm, tv)
                        kb.op("s", "activation", ZST[:, t0:t0 + w], P[pb][:, 0:w],
                              AF.Silu)
                        slot_ev[pb] = ("s", kb.last("s")[1])
                    else:
                        kb.wait("v", cc.tm, tv)
                        if t0 == 0 and conv_g:
                            kb.wait("v", kb.self_sem["g"], conv_g)
                        kb.op("v", "tensor_copy", XCR[:, 8 + t0:8 + t0 + w],
                              P[pb][:, 0:w])
                        slot_ev[pb] = ("v", kb.last("v")[1])
                wtb_used[wsl] = cc.tm.n
                if ec + 1 < 32:
                    war = ([(cc.tm, wtb_used[1 - wsl])]
                           if wtb_used[1 - wsl] else [])
                    wtb_ld[1 - wsl] = kb.dma("g", bufs[1 - wsl],
                                             WT_IN[li, ec + 1], cc.dws,
                                             waits=war)
                if is_z:
                    zst_sv = kb.dma("y", Z_DR[et], ZST[:, :], cc.dout,
                                    waits=[kb.last("s")])
                else:
                    kb.op("v", "memset", XCR[:, 5:8], 0.0)
                    kb.wait("g", *kb.last("v"))
                    for t0 in range(0, T, 512):
                        w = min(512, T - t0)
                        cve = "v"
                        cbuf = U1 if (t0 // 512) % 2 == 0 else DTF
                        kb.op(cve, "tensor_scalar", cbuf[:, 0:w],
                              XCR[:, 5 + t0:5 + t0 + w],
                              CVW[:, 0 * ET + et:0 * ET + et + 1], None, OP.mult)
                        for kk in range(1, 4):
                            kb.op(cve, "scalar_tensor_tensor", cbuf[:, 0:w],
                                  XCR[:, 5 + kk + t0:5 + kk + t0 + w],
                                  CVW[:, kk * ET + et:kk * ET + et + 1],
                                  cbuf[:, 0:w], OP.mult, OP.add)
                        kb.op(cve, "tensor_scalar", cbuf[:, 0:w], cbuf[:, 0:w],
                              CVB[:, et:et + 1], None, OP.add)
                        if t0 == 0 and xct_sv:
                            kb.wait("s", cc.dout, xct_sv)
                        kb.op("s", "activation", XCT[:, t0:t0 + w], cbuf[:, 0:w],
                              AF.Silu, waits=[kb.last(cve)])
                        kb.wait(cve, *kb.last("s"))
                    conv_g = kb.self_sem["g"].n
                    xct_sv = kb.dma("y", XC_DR[et], XCT[:, 0:T], cc.dout,
                                    waits=[kb.last("s")])
            kb.barrier()

            kb.wait("g", cc.dout, cc.dout.n)
            xp_used = [0, 0]
            xp_ld = [0, 0]
            for t0 in range(0, T, 512) if "m2b" not in L0A_SKIP else ():
                w = min(512, T - t0)

                def xp_pre(et, t0=t0, w=w):
                    sl3 = et % 2
                    war = ([(cc.tm, xp_used[sl3])] if xp_used[sl3] else [])
                    kb.dma("g", WTB[:, sl3 * 96:sl3 * 96 + 96],
                           WT_XP[li, et], cc.dws, waits=war)
                    kb.dma("g", XCT[:, sl3 * T + t0:sl3 * T + t0 + w],
                           XC_DR[et][:, t0:t0 + w], cc.dws)
                    xp_ld[sl3] = cc.dws.n
                xp_pre(0)
                xp_pre(1)
                for et in range(ET):
                    sl3 = et % 2
                    wo = sl3 * 96
                    xo = sl3 * T + t0
                    kb.wait("t", cc.dws, xp_ld[sl3])
                    kb.op("t", "matmul", P[2][0:64, 0:w], WTB[:, wo:wo + 64],
                          XCT[:, xo:xo + w], start=(et == 0), stop=(et == ET - 1))
                    kb.op("t", "matmul", P[3][0:16, 0:w], WTB[:, wo + 64:wo + 80],
                          XCT[:, xo:xo + w], start=(et == 0), stop=(et == ET - 1))
                    kb.op("t", "matmul", P[4][0:16, 0:w], WTB[:, wo + 80:wo + 96],
                          XCT[:, xo:xo + w], start=(et == 0), stop=(et == ET - 1))
                    xp_used[sl3] = kb.mark("t", cc.tm)
                    if et + 2 < ET:
                        xp_pre(et + 2)
                kb.wait("v", cc.tm, cc.tm.n)
                kb.op("v", "tensor_copy", DTROW[:, t0:t0 + w], P[2][0:64, 0:w])
                kb.op("v", "tensor_copy", B16[:, t0:t0 + w], P[3][0:16, 0:w])
                kb.op("v", "tensor_copy", C16[:, t0:t0 + w], P[4][0:16, 0:w])
                kb.wait("t", *kb.last("v"))
            for t0 in range(0, T, 512):
                w = min(512, T - t0)
                kb.op("t", "matmul", P[2][:, 0:w], S16[:, :], B16[:, t0:t0 + w],
                      start=True, stop=True)
                kb.op("t", "matmul", P[3][:, 0:w], S16[:, :], C16[:, t0:t0 + w],
                      start=True, stop=True)
                tv = kb.mark("t", cc.tm)
                kb.wait("v", cc.tm, tv)
                kb.op("v", "tensor_copy", BREP[:, t0:t0 + w], P[2][:, 0:w])
                kb.op("v", "tensor_copy", CREP[:, t0:t0 + w], P[3][:, 0:w])
                kb.wait("t", *kb.last("v"))
            kb.barrier()

            NCH = (T + 511) // 512
            xct_free = [0, 0]
            acol_free = [0, 0]
            wtb_free = [0, 0]
            m3ld = {}

            def m3_pre(et):
                sl2 = et % 2
                w1 = kb.dma("g", WTB[0:64, sl2 * 128:sl2 * 128 + 128],
                            WT_DTP[li, et], cc.dws,
                            waits=([(cc.tm, wtb_free[sl2])]
                                   if wtb_free[sl2] else []))
                x1 = kb.dma("g", XCT[:, sl2 * T:sl2 * T + T], XC_DR[et],
                            cc.dws,
                            waits=([(kb.self_sem["v"], xct_free[sl2])]
                                   if xct_free[sl2] else []))
                a1 = kb.dma("g", ACOL[:, sl2 * NST:(sl2 + 1) * NST],
                            HD["L0A_Alog_r"][li, et], cc.dws,
                            waits=([(kb.self_sem["s"], acol_free[sl2])]
                                   if acol_free[sl2] else []))
                m3ld[et] = max(w1, x1, a1)

            prev_exp = [0, 0]
            prev_hc = 0
            prev_tm = 0
            spill_last = 0
            prev_cmb = 0
            if "m3" not in L0A_SKIP:
                m3_pre(0)
            for et in range(ET) if "m3" not in L0A_SKIP else ():
                xsl = et % 2
                kb.wait("t", cc.dws, m3ld[et])
                kb.wait("v", cc.dws, m3ld[et])
                kb.wait("s", cc.dws, m3ld[et])
                if prev_tm:
                    kb.wait("v", cc.tm, prev_tm)
                if prev_hc:
                    kb.wait("v", kb.self_sem["g"], prev_hc)
                kb.op("s", "activation", ACOL[:, xsl * NST:(xsl + 1) * NST],
                      ACOL[:, xsl * NST:(xsl + 1) * NST], AF.Exp)
                kb.op("v", "tensor_scalar", ACOL[:, xsl * NST:(xsl + 1) * NST],
                      ACOL[:, xsl * NST:(xsl + 1) * NST], -1.0, None,
                      OP.mult, waits=[kb.last("s")])
                pend_dt = None
                for c4 in range(NCH):
                    t0 = c4 * 512
                    w = min(512, T - t0)
                    pb = c4 % 2
                    if prev_exp[pb]:
                        kb.wait("t", kb.self_sem["s"], prev_exp[pb])
                        prev_exp[pb] = 0
                    kb.op("t", "matmul", P[4 + pb][:, 0:w],
                          WTB[0:64, xsl * 128:xsl * 128 + 128],
                          DTROW[:, t0:t0 + w], start=True, stop=True)
                    tv = kb.mark("t", cc.tm)
                    kb.wait("v", cc.tm, tv)
                    kb.op("v", "tensor_scalar", HS[:, pb * 512:pb * 512 + w],
                          P[4 + pb][:, 0:w], DTPB[:, et:et + 1], None, OP.add)
                    dtfs = DTF if pb == 0 else DTF2
                    kb.op("s", "activation", dtfs[:, 0:w],
                          HS[:, pb * 512:pb * 512 + w], AF.Abs,
                          waits=[kb.last("v")])
                    kb.op("s", "activation", dtfs[:, 0:w], dtfs[:, 0:w],
                          AF.Exp, scale=cc.kf(-1.0))
                    kb.op("s", "activation", dtfs[:, 0:w], dtfs[:, 0:w],
                          AF.Ln, bias=cc.kf(1.0))
                    sval = kb.last("s")
                    if pend_dt is not None:
                        pt0, pw, ppb, psv, pdt = pend_dt
                        kb.op("v", "tensor_scalar",
                              HS[:, ppb * 512:ppb * 512 + pw],
                              HS[:, ppb * 512:ppb * 512 + pw], 0.0, None,
                              OP.max, waits=[psv])
                        kb.op("v", "tensor_add", DTB[:, pt0:pt0 + pw],
                              HS[:, ppb * 512:ppb * 512 + pw], pdt[:, 0:pw])
                        kb.op("v", "tensor_mul", UB[:, pt0:pt0 + pw],
                              DTB[:, pt0:pt0 + pw],
                              XCT[:, xsl * T + pt0:xsl * T + pt0 + pw])
                    pend_dt = (t0, w, pb, sval, dtfs)
                pt0, pw, ppb, psv, pdt = pend_dt
                kb.op("v", "tensor_scalar", HS[:, ppb * 512:ppb * 512 + pw],
                      HS[:, ppb * 512:ppb * 512 + pw], 0.0, None,
                      OP.max, waits=[psv])
                kb.op("v", "tensor_add", DTB[:, pt0:pt0 + pw],
                      HS[:, ppb * 512:ppb * 512 + pw], pdt[:, 0:pw])
                kb.op("v", "tensor_mul", UB[:, pt0:pt0 + pw],
                      DTB[:, pt0:pt0 + pw],
                      XCT[:, xsl * T + pt0:xsl * T + pt0 + pw])
                wtb_free[xsl] = cc.tm.n
                if et + 1 < ET:
                    m3_pre(et + 1)
                kb.wait("t", *kb.last("v"))
                exp_done = [0, 0]
                dbu_done = [0, 0]
                scan_rd = [0, 0]
                hc_rd = [0, 0]
                y_rd = [0, 0]
                for ci in range(NCH):
                    t0 = ci * 512
                    w = min(512, T - t0)
                    pend_y = None
                    for g in range(16):
                        pb = g % 2
                        if exp_done[pb]:
                            kb.wait("t", kb.self_sem["s"], exp_done[pb])
                        kb.op("t", "matmul", P[4 + pb][:, 0:w],
                              SEL[:, g * 128:(g + 1) * 128], DTB[:, t0:t0 + w],
                              start=True, stop=True)
                        tv1 = kb.mark("t", cc.tm)
                        if dbu_done[pb]:
                            kb.wait("t", kb.self_sem["v"], dbu_done[pb])
                        kb.op("t", "matmul", P[6 + pb][:, 0:w],
                              SEL[:, g * 128:(g + 1) * 128], UB[:, t0:t0 + w],
                              start=True, stop=True)
                        tv2 = kb.mark("t", cc.tm)
                        if pend_y is not None:
                            gy, pby, hcy = pend_y
                            kb.wait("t", kb.self_sem["g"], hcy)
                            kb.op("t", "matmul", P[ci][:, 0:w],
                                  SELT[:, gy * 128:(gy + 1) * 128],
                                  HCB[:, pby * 512:pby * 512 + w],
                                  start=(gy == 0), stop=False)
                            y_rd[pby] = kb.mark("t", cc.tm)
                        kb.wait("s", cc.tm, tv1)
                        if scan_rd[pb]:
                            kb.wait("s", kb.self_sem["v"], scan_rd[pb])
                        kb.op("s", "activation", DA[:, pb * 512:pb * 512 + w],
                              P[4 + pb][:, 0:w], AF.Exp,
                              scale=ACOL[:, xsl * NST + g:xsl * NST + g + 1])
                        exp_done[pb] = kb.last("s")[1]
                        kb.wait("v", cc.tm, tv2)
                        kb.op("v", "tensor_tensor", DBU[:, pb * 512:pb * 512 + w],
                              P[6 + pb][:, 0:w], BREP[:, t0:t0 + w], OP.mult)
                        dbu_done[pb] = kb.last("v")[1]
                        init = 0.0 if ci == 0 else HSLAST[:, g:g + 1]
                        kb.op("v", "tensor_tensor_scan",
                              HS[:, pb * 512:pb * 512 + w],
                              DA[:, pb * 512:pb * 512 + w],
                              DBU[:, pb * 512:pb * 512 + w],
                              init, OP.mult, OP.add,
                              waits=([(kb.self_sem["s"], exp_done[pb])] +
                                     ([(kb.self_sem["g"], hc_rd[pb])]
                                      if hc_rd[pb] else [])))
                        scan_rd[pb] = kb.last("v")[1]
                        if ci < NCH - 1:
                            kb.op("v", "tensor_copy", HSLAST[:, g:g + 1],
                                  HS[:, pb * 512 + w - 1:pb * 512 + w])
                        gw = [(kb.self_sem["v"], kb.last("v")[1])]
                        if y_rd[pb]:
                            gw.append((cc.tm, y_rd[pb]))
                        kb.op("g", "tensor_tensor", HCB[:, pb * 512:pb * 512 + w],
                              HS[:, pb * 512:pb * 512 + w], CREP[:, t0:t0 + w],
                              OP.mult, waits=gw)
                        hc_rd[pb] = kb.last("g")[1]
                        pend_y = (g, pb, hc_rd[pb])
                    gy, pby, hcy = pend_y
                    kb.wait("t", kb.self_sem["g"], hcy)
                    kb.op("t", "matmul", P[ci][:, 0:w],
                          SELT[:, gy * 128:(gy + 1) * 128],
                          HCB[:, pby * 512:pby * 512 + w],
                          start=False, stop=True)
                    y_rd[pby] = kb.mark("t", cc.tm)
                zl = kb.dma("g", ZST[:, :], Z_DR[et], cc.dws,
                            waits=([(kb.self_sem["v"], prev_cmb)]
                                   if prev_cmb else []))
                kb.wait("v", cc.dws, zl)
                kb.wait("v", cc.tm, cc.tm.n)
                if spill_last:
                    kb.wait("v", cc.dout, spill_last)
                for ci in range(NCH):
                    t0 = ci * 512
                    w = min(512, T - t0)
                    kb.op("v", "scalar_tensor_tensor", U1[:, 0:w],
                          XCT[:, xsl * T + t0:xsl * T + t0 + w],
                          DCOL[:, et:et + 1], P[ci][:, 0:w],
                          OP.mult, OP.add)
                    kb.op("v", "tensor_mul", YF[:, t0:t0 + w], U1[:, 0:w],
                          ZST[:, t0:t0 + w])
                    spill_last = kb.dma("y", YF_DR[et][:, t0:t0 + w],
                                        YF[:, t0:t0 + w], cc.dout,
                                        waits=[kb.last("v")])
                prev_cmb = kb.last("v")[1]
                xct_free[xsl] = prev_cmb
                acol_free[xsl] = kb.self_sem["s"].n
                prev_exp = list(exp_done)
                prev_hc = max(hc_rd)
                prev_tm = cc.tm.n
            kb.barrier()
            kb.wait("g", cc.dout, cc.dout.n)

            WOALL = (HN if DT * T >= DT * ET * 128
                     else kb.sb("l0a_WOALL", [128, DT * ET * 128], BF16))
            for dc in range(DT):
                wo_ld = kb.dma("g", WOALL[:, dc * ET * 128:(dc + 1) * ET * 128],
                               WT_OUT[li, dc], cc.dws)
            kb.wait("t", cc.dws, wo_ld)
            NCH4 = (T + 511) // 512
            for h0 in range(0, NCH4, 2) if "m4" not in L0A_SKIP else ():
                t0h = h0 * 512
                wh = min(1024, T - t0h)
                nci = (wh + 511) // 512
                nsl = 2 if 2 * wh <= T else 1
                for dh in range(0, DT, 4):
                    yf_ld = [0, 0]
                    yf_used = [0, 0]
                    war0 = [(cc.tm, cc.tm.n)] if cc.tm.n else []
                    yf_ld[0] = kb.dma("g", YF[:, 0:wh],
                                      YF_DR[0][:, t0h:t0h + wh], cc.dws,
                                      waits=war0)
                    for et in range(ET):
                        sl = et % nsl

                        def pref(et=et):
                            ns = (et + 1) % nsl
                            war = ([(cc.tm, yf_used[ns])]
                                   if yf_used[ns] else [])
                            yf_ld[ns] = kb.dma(
                                "g", YF[:, ns * wh:ns * wh + wh],
                                YF_DR[et + 1][:, t0h:t0h + wh], cc.dws, waits=war)
                        if et + 1 < ET and nsl == 2:
                            pref()
                        kb.wait("t", cc.dws, yf_ld[sl])
                        for dcl in range(4):
                            dc = dh + dcl
                            for ci2 in range(nci):
                                w = min(512, wh - ci2 * 512)
                                kb.op("t", "matmul", P[dcl * 2 + ci2][:, 0:w],
                                      WOALL[:, (dc * ET + et) * 128:
                                             (dc * ET + et + 1) * 128],
                                      YF[:, sl * wh + ci2 * 512:
                                          sl * wh + ci2 * 512 + w],
                                      start=(et == 0), stop=(et == ET - 1))
                        yf_used[sl] = kb.mark("t", cc.tm)
                        if et + 1 < ET and nsl == 1:
                            pref()
                    kb.wait("v", cc.tm, cc.tm.n)
                    for dcl in range(4):
                        for ci2 in range(nci):
                            w = min(512, wh - ci2 * 512)
                            sl2 = slice((dh + dcl) * T + t0h + ci2 * 512,
                                        (dh + dcl) * T + t0h + ci2 * 512 + w)
                            kb.op("v", "tensor_tensor", X[:, sl2], X[:, sl2],
                                  P[dcl * 2 + ci2][:, 0:w], OP.add)
                    kb.wait("t", *kb.last("v"))
            kb.barrier()

        def apply_m5(t0, w):
            xi = kb.dma("y", XI[:, 0:DT * w].rearrange("p (k t) -> p k t", k=DT),
                        HD["xin"][b].rearrange("(k p) t -> p k t", p=128)[:, :, t0:t0 + w],
                        cc.dsem, waits=[(cc.dout, cc.dout.n)])
            kb.wait("v", cc.dsem, xi)
            for k in range(DT):
                sl = slice(k * T + t0, k * T + t0 + w)
                kb.op("v", "tensor_mul", U1[:, 0:w], X[:, sl], cc.RS[:, 0:w])
                kb.op("v", "tensor_add", U1[:, 0:w], U1[:, 0:w], cc.NMR[:, 0:w])
                kb.op("v", "tensor_scalar", U1[:, 0:w], U1[:, 0:w],
                      LNW[:, 2 * DT + k:2 * DT + k + 1],
                      LNB[:, 2 * DT + k:2 * DT + k + 1], OP.mult, OP.add)
                kb.op("v", "tensor_add", XI[:, k * w:(k + 1) * w],
                      XI[:, k * w:(k + 1) * w], U1[:, 0:w])
            so = kb.dma("y", HD["xout"][b].rearrange("(k p) t -> p k t", p=128)[:, :, t0:t0 + w],
                        XI[:, 0:DT * w].rearrange("p (k t) -> p k t", k=DT),
                        cc.dout, waits=[kb.last("v")])
            kb.wait("v", cc.dout, so)
        cc.ln_chunked(cc.cast_tile_cb(X), apply_m5)
        kb.wait("y", cc.dout, cc.dout.n)
        kb.barrier()


def build_l0a(T, nbatch):
    nc = bass.Bass("TRN2", target_bir_lowering=False)
    HD = {}
    HD["xin"] = nc.dram_tensor("xin", [nbatch, D, T], F32, kind="ExternalInput").ap()
    HD["xout"] = nc.dram_tensor("xout", [nbatch, D, T], F32, kind="ExternalOutput").ap()
    for nm, shp in [("L0_attn_ln_w", [2, D]), ("L0_attn_ln_b", [2, D]),
                    ("L0_attn_in_proj_w", [2, 2 * E_IN, D]),
                    ("L0_attn_conv_w", [2, E_IN, 4]), ("L0_attn_conv_b", [2, E_IN]),
                    ("L0_attn_x_proj_w", [2, 96, E_IN]),
                    ("L0_attn_dt_proj_w", [2, E_IN, RNK]),
                    ("L0_attn_dt_proj_b", [2, E_IN]),
                    ("L0_attn_D_skip", [2, E_IN]),
                    ("L0_attn_out_proj_w", [2, D, E_IN]),
                    ("L0_attn_norm_f_w", [D]), ("L0_attn_norm_f_b", [D]),
                    ("L0A_Alog_r", [2, 16, 128, 16]),
                    ("c_sel", [128, 16 * 128]), ("c_selt", [128, 16 * 128]),
                    ("c_s16", [16, 128])]:
        dt = BF16 if nm in ("c_sel", "c_selt", "c_s16") else F32
        HD[nm] = nc.dram_tensor(nm, shp, dt, kind="ExternalInput")
    kb = KB(nc)
    cc = CC(kb, T)
    emit_l0a_vectors(kb, cc, HD)
    emit_l0a(kb, cc, HD, nbatch)
    kb.finalize()
    return nc


def emit_l0a_vectors(kb, cc, HD):
    pass


def l0a_consts():
    sel = np.zeros((128, 16, 128), np.float32)
    selt = np.zeros((128, 16, 128), np.float32)
    for g in range(16):
        for m in range(128):
            sel[8 * g + (m // 16), g, m] = 1.0
        for p in range(128):
            selt[p, g, 8 * g + (p // 16)] = 1.0
    s16 = np.zeros((16, 128), np.float32)
    for p in range(128):
        s16[p % 16, p] = 1.0
    return {
        "c_sel": sel.reshape(128, 16 * 128).astype(ml_dtypes.bfloat16),
        "c_selt": selt.reshape(128, 16 * 128).astype(ml_dtypes.bfloat16),
        "c_s16": s16.astype(ml_dtypes.bfloat16),
    }




def emit_l1f_a(kb, cc, HD, nbatch):
    T, TC = cc.T, cc.TC
    P = cc.P
    nc = kb.nc
    X = kb.sb("f1a_X", [128, DT * T], F32)
    ABF = kb.sb("f1a_ABF", [128, 2 * DT * T], BF16)
    XN = ABF[:, 0:DT * T]
    HSH = ABF[:, DT * T:2 * DT * T]
    WTB = kb.sb("f1a_WTB", [128, DT * 128], BF16)
    WGT = kb.sb("f1a_WGT", [128, DT * 8], BF16)
    TSTG = kb.sb("f1a_TSTG", [128, 1024], BF16)
    U1 = kb.sb("f1a_U1", [128, 512], F32)
    GP = kb.sb("f1a_GP", [128, 8], F32)
    GS = kb.sb("f1a_GS", [128, 4], F32)
    GM = kb.sb("f1a_GM", [128, 2], F32)
    KX = kb.sb("f1a_KX", [128, 16], F32)
    OH = kb.sb("f1a_OH", [128, 20], F32)
    OHB = kb.sb("f1a_OHB", [128, 16], BF16)
    BB = kb.sb("f1a_BB", [128, 8], F32)
    CNT = kb.sb("f1a_CNT", [1, 8], F32)
    EBR = kb.sb("f1a_EBR", [1, 8], F32)

    WT_S1 = kb.dram("f1a_wt_s1", [DT, 128, DT * 128], BF16)
    WT_S3 = kb.dram("f1a_wt_s3", [DT, 128, DT * 128], BF16)
    WT_S2 = kb.dram("f1a_wt_s2", [DT, 128, DT * 128], BF16)

    kb.op("g", "dma_start", out=EBR[:, :],
          in_=HD["L1_ffn_expert_bias"].ap().rearrange("(o n) -> o n", o=1),
          incs=[(cc.dgi, 16)], selfsync=False)
    kb.barrier()
    for e in ("v", "s", "t", "g"):
        kb.wait(e, cc.dsem, cc.dsem.n)
        kb.wait(e, cc.dgi, cc.dgi.n)

    PT = P[7][:, :].bitcast(BF16)
    tprev = 0
    for (wsrc, wdst) in ((HD["L1_ffn_sw1"], WT_S1), (HD["L1_ffn_sw3"], WT_S3),
                         (HD["L1_ffn_sw2"], WT_S2)):
        wr = wsrc.ap().rearrange("(oc p) d -> oc p d", p=128)
        for oc in range(DT):
            if tprev:
                kb.wait("g", cc.tm, tprev)
            wl = kb.dma("g", TSTG[:, 0:1024], wr[oc], cc.dws)
            kb.wait("t", cc.dws, wl)
            for k in range(DT):
                kb.op("t", "transpose", PT[:, k * 128:(k + 1) * 128],
                      TSTG[:, k * 128:(k + 1) * 128], cc.IDENT[:, :])
            tprev = kb.mark("t", cc.tm)
            kb.wait("v", cc.tm, tprev)
            kb.op("v", "tensor_copy", WTB[:, :], PT[:, 0:1024])
            kb.wait("t", *kb.last("v"))
            sv = kb.dma("y", wdst[oc], WTB[:, :], cc.dout, waits=[kb.last("v")])
            kb.wait("v", cc.dout, sv)
    kb.wait("g", cc.tm, tprev)
    wl = kb.dma("g", TSTG[0:8, 0:1024], HD["L1_ffn_w_gate"][:, :], cc.dws)
    kb.wait("t", cc.dws, wl)
    for k in range(DT):
        kb.op("t", "transpose", PT[:, k * 8:(k + 1) * 8],
              TSTG[0:8, k * 128:(k + 1) * 128], cc.IDENT[0:8, 0:8])
        tv = kb.mark("t", cc.tm)
    kb.wait("v", cc.tm, cc.tm.n)
    kb.op("v", "tensor_copy", WGT[:, :], PT[:, 0:DT * 8])
    kb.wait("t", *kb.last("v"))
    kb.op("t", "matmul", P[6][:, 0:8], cc.ONES_ROW32[0:1, :], EBR[0:1, :],
          start=True, stop=True)
    tv = kb.mark("t", cc.tm)
    kb.wait("v", cc.tm, tv)
    kb.op("v", "tensor_copy", BB[:, :], P[6][:, 0:8])
    kb.op("v", "memset", CNT[:, :], 0.0)
    kb.op("v", "memset", OH[:, 18:20], 0.0)
    kb.wait("t", *kb.last("v"))
    kb.barrier()

    cntv = 0
    for b in range(nbatch):
        xld = kb.dma("y", X[:, :].rearrange("p (k t) -> p k t", k=DT),
                     HD["xin"][b].rearrange("(k p) t -> p k t", p=128), cc.dsem)
        kb.wait("v", cc.dsem, xld)
        kb.wait("s", cc.dsem, xld)

        def apply_a(t0, w):
            for k in range(DT):
                sl = slice(k * T + t0, k * T + t0 + w)
                kb.op("v", "tensor_mul", U1[:, 0:w], X[:, sl], cc.RS[:, 0:w])
                kb.op("v", "tensor_add", XN[:, sl], U1[:, 0:w], cc.NMR[:, 0:w])
        cc.ln_chunked(cc.cast_tile_cb(X), apply_a)
        kb.barrier()
        if b == 0:
            x8 = kb.dma("y", HD["xf8"].rearrange("p (k c) -> p k c", c=8),
                        XN[:, :].rearrange("p (k t) -> p k t", t=T)[:, :, 0:8],
                        cc.dout, waits=[kb.last("v")])

        for tc in range(TC):
            t0 = tc * 128
            for k in range(DT):
                kb.op("t", "matmul", P[6][:, 0:8],
                      XN[:, k * T + t0:k * T + t0 + 128], WGT[:, k * 8:(k + 1) * 8],
                      start=(k == 0), stop=(k == DT - 1))
            tv = kb.mark("t", cc.tm)
            kb.wait("v", cc.tm, tv)
            kb.op("v", "tensor_reduce", GM[:, 0:1], P[6][:, 0:8], AX.X, OP.max)
            kb.op("v", "tensor_scalar", GM[:, 0:1], GM[:, 0:1], -1.0, None, OP.mult)
            kb.op("s", "activation", GP[:, 0:8], P[6][:, 0:8], AF.Exp,
                  bias=GM[:, 0:1], waits=[kb.last("v")])
            kb.wait("t", *kb.last("s"))
            kb.op("v", "tensor_reduce", GM[:, 1:2], GP[:, 0:8], AX.X, OP.add,
                  waits=[kb.last("s")])
            kb.op("v", "reciprocal", GM[:, 1:2], GM[:, 1:2])
            kb.op("v", "tensor_scalar", GP[:, 0:8], GP[:, 0:8], GM[:, 1:2],
                  None, OP.mult)
            kb.op("v", "tensor_add", OH[:, 0:8], GP[:, 0:8], BB[:, 0:8])
            sa = OH[:, 0:8].rearrange("p (g two) -> p g two", two=2)
            kb.op("v", "tensor_add", GS[:, 0:4], sa[:, :, 0], sa[:, :, 1])
            kb.op("v", "tensor_reduce", GM[:, 0:1], GS[:, 0:4], AX.X, OP.max)
            kb.op("v", "tensor_scalar", KX[:, 0:4], GS[:, 0:4], GM[:, 0:1],
                  None, OP.is_equal)
            kb.op("v", "scalar_tensor_tensor", KX[:, 0:4], KX[:, 0:4], -1e30,
                  GS[:, 0:4], OP.mult, OP.add)
            kb.op("v", "tensor_reduce", GM[:, 1:2], KX[:, 0:4], AX.X, OP.max)
            kb.op("v", "tensor_scalar", KX[:, 0:4], GS[:, 0:4], GM[:, 1:2],
                  None, OP.is_ge)
            kb.op("v", "tensor_copy",
                  KX[:, 8:16].rearrange("p (g two) -> p g two", two=2),
                  KX[:, 0:4].rearrange("p (g o) -> p g o", o=1).broadcast_to(
                      [128, 4, 2]))
            kb.op("v", "tensor_scalar", KX[:, 8:16], KX[:, 8:16], 1.0, None,
                  OP.subtract)
            kb.op("v", "scalar_tensor_tensor", KX[:, 8:16], KX[:, 8:16], 1e30,
                  OH[:, 0:8], OP.mult, OP.add)
            kb.op("v", "tensor_reduce", GM[:, 0:1], KX[:, 8:16], AX.X, OP.max)
            kb.op("v", "tensor_scalar", OH[:, 0:8], KX[:, 8:16], GM[:, 0:1],
                  None, OP.is_equal)
            kb.op("v", "scalar_tensor_tensor", KX[:, 8:16], OH[:, 0:8], -1e30,
                  KX[:, 8:16], OP.mult, OP.add)
            kb.op("v", "tensor_reduce", GM[:, 1:2], KX[:, 8:16], AX.X, OP.max)
            kb.op("v", "tensor_scalar", OH[:, 8:16], KX[:, 8:16], GM[:, 1:2],
                  None, OP.is_equal)
            kb.op("v", "scalar_tensor_tensor", KX[:, 0:8], OH[:, 0:8], 1.0,
                  GP[:, 0:8], OP.mult, OP.mult, accum_out=OH[:, 16:17])
            kb.op("v", "scalar_tensor_tensor", KX[:, 0:8], OH[:, 8:16], 1.0,
                  GP[:, 0:8], OP.mult, OP.mult, accum_out=OH[:, 17:18])
            kb.op("v", "tensor_add", KX[:, 0:8], OH[:, 0:8], OH[:, 8:16])
            kb.op("v", "tensor_copy", OHB[:, 0:8], KX[:, 0:8])
            kb.wait("t", *kb.last("v"))
            kb.op("t", "matmul", P[5][0:1, 0:8], cc.ONES_COL[:, :], OHB[:, 0:8],
                  start=(b == 0 and tc == 0), stop=(b == nbatch - 1 and tc == TC - 1))
            cntv = kb.mark("t", cc.tm)
            sv = kb.dma("y", HD["ohw"][b, tc], OH[:, :], cc.dout,
                        waits=[kb.last("v")])
            kb.wait("v", cc.dout, sv)

        w1free = 0
        for hc in range(DT):
            if w1free:
                kb.wait("g", cc.tm, w1free)
            l1 = kb.dma("g", WTB[:, :], WT_S1[hc], cc.dws)
            l3 = kb.dma("g", TSTG[:, 0:1024], WT_S3[hc], cc.dws)
            kb.wait("t", cc.dws, l3)
            for t0 in range(0, T, 512):
                w = min(512, T - t0)
                pb = (t0 // 512) % 2
                for k in range(DT):
                    kb.op("t", "matmul", P[pb][:, 0:w], WTB[:, k * 128:(k + 1) * 128],
                          XN[:, k * T + t0:k * T + t0 + w],
                          start=(k == 0), stop=(k == DT - 1))
                for k in range(DT):
                    kb.op("t", "matmul", P[2 + pb][:, 0:w],
                          TSTG[:, k * 128:(k + 1) * 128],
                          XN[:, k * T + t0:k * T + t0 + w],
                          start=(k == 0), stop=(k == DT - 1))
                tv = kb.mark("t", cc.tm)
                kb.wait("s", cc.tm, tv)
                kb.op("s", "activation", U1[:, 0:w], P[pb][:, 0:w], AF.Silu)
                kb.op("v", "tensor_tensor", HSH[:, hc * T + t0:hc * T + t0 + w],
                      U1[:, 0:w], P[2 + pb][:, 0:w], OP.mult,
                      waits=[kb.last("s")])
                kb.wait("t", *kb.last("v"))
            w1free = cc.tm.n
        for oc in range(DT):
            kb.wait("g", cc.tm, w1free)
            wl = kb.dma("g", WTB[:, :], WT_S2[oc], cc.dws)
            kb.wait("t", cc.dws, wl)
            for t0 in range(0, T, 512):
                w = min(512, T - t0)
                pb = (t0 // 512) % 2
                for k in range(DT):
                    kb.op("t", "matmul", P[pb][:, 0:w], WTB[:, k * 128:(k + 1) * 128],
                          HSH[:, k * T + t0:k * T + t0 + w],
                          start=(k == 0), stop=(k == DT - 1))
                tv = kb.mark("t", cc.tm)
                kb.wait("v", cc.tm, tv)
                sl = slice(oc * T + t0, oc * T + t0 + w)
                kb.op("v", "scalar_tensor_tensor", X[:, sl], P[pb][:, 0:w], 0.5,
                      X[:, sl], OP.mult, OP.add)
                kb.wait("t", *kb.last("v"))
            w1free = cc.tm.n
        so = kb.dma("y", HD["xout"][b].rearrange("(k p) t -> p k t", p=128),
                    X[:, :].rearrange("p (k t) -> p k t", k=DT), cc.dout,
                    waits=[kb.last("v")])
        kb.wait("y", cc.dout, so)
        kb.barrier()
    kb.wait("v", cc.tm, cntv)
    kb.op("v", "tensor_copy", CNT[:, :], P[5][0:1, 0:8])
    sv = kb.dma("y", HD["cnt8"][:, :], CNT[:, :], cc.dout, waits=[kb.last("v")])
    kb.wait("y", cc.dout, sv)


def build_l1f_a(T, nbatch):
    nc = bass.Bass("TRN2", target_bir_lowering=False)
    HD = {}
    HD["xin"] = nc.dram_tensor("xin", [nbatch, D, T], F32, kind="ExternalInput").ap()
    HD["xout"] = nc.dram_tensor("xout", [nbatch, D, T], F32, kind="ExternalOutput").ap()
    HD["cnt8"] = nc.dram_tensor("cnt8", [1, 8], F32, kind="ExternalOutput")
    HD["ohw"] = nc.dram_tensor("ohw", [nbatch, T // 128, 128, 20], F32,
                               kind="ExternalOutput").ap()
    HD["xf8"] = nc.dram_tensor("xf8", [128, DT * 8], BF16, kind="ExternalOutput").ap()
    for nm, shp in [("L1_ffn_w_gate", [8, D]), ("L1_ffn_sw1", [D, D]),
                    ("L1_ffn_sw2", [D, D]), ("L1_ffn_sw3", [D, D]),
                    ("L1_ffn_expert_bias", [8])]:
        HD[nm] = nc.dram_tensor(nm, shp, F32, kind="ExternalInput")
    kb = KB(nc)
    cc = CC(kb, T)
    emit_l1f_a(kb, cc, HD, nbatch)
    kb.finalize()
    return nc



def emit_l1f_b(kb, cc, HD, nbatch):
    T, TC = cc.T, cc.TC
    P = cc.P
    nc = kb.nc
    CNT = kb.sb("f1b_CNT", [8, 8], F32)
    GCC = kb.sb("f1b_GCC", [8, 1], F32)
    OFS = kb.sb("f1b_OFS", [1, 8], F32)
    OFB = kb.sb("f1b_OFB", [128, 8], F32)
    ONE8 = kb.sb("f1b_ONE8", [8, 1], F32)
    UT8 = kb.sb("f1b_UT8", [8, 8], F32)
    XF8 = kb.sb("f1b_XF8", [128, DT * 8], BF16)
    TB1 = kb.sb("f1b_TB1", [8, 8 * 512], F32)
    TB3 = kb.sb("f1b_TB3", [8, 8 * 512], F32)
    T64A = kb.sb("f1b_T64A", [64, 512], BF16)
    T64B = kb.sb("f1b_T64B", [64, 512], BF16)
    WE = kb.sb("f1b_WE", [128, 16 * 512], BF16)
    OH = kb.sb("f1b_OH", [128, 20], F32)
    JA = kb.sb("f1b_JA", [128, 2 * TC], F32)
    GA = kb.sb("f1b_GA", [128, 8], F32)
    CMP = kb.sb("f1b_CMP", [128, 8], F32)
    E64 = kb.sb("f1b_E64", [128, 64], BF16)
    E64T = kb.sb("f1b_E64T", [64, 128], BF16)
    HB = kb.sb("f1b_HB", [128, 512], BF16)
    T1 = kb.sb("f1b_T1", [128, 512], BF16)
    T2 = kb.sb("f1b_T2", [128, 512], BF16)
    HP = kb.sb("f1b_HP", [64, 512], F32)
    TDR1 = kb.dram("f1b_tdr1", [8, 8, 512], F32)
    TDR3 = kb.dram("f1b_tdr3", [8, 8, 512], F32)

    if "cnts_global" in HD:
        kb.dma("y", OFS[:, :], HD["cnts_global"][:, :], cc.dsem)
    else:
        kb.dma("y", CNT[:, :], HD["cnts"][:, :], cc.dsem)
    kb.dma("g", XF8[:, :], HD["xf8"][:, :], cc.dgi)
    kb.dma("y", ONE8[:, :], HD["c_one8"][:, :], cc.dsem)
    kb.dma("y", UT8[:, :], HD["c_ut8"][:, :], cc.dsem)
    kb.dma("y", JA[:, 0:nbatch * TC], HD["c_ja"][:, 0:nbatch * TC], cc.dsem)
    kb.barrier()
    for e in ("v", "s", "t", "g"):
        kb.wait(e, cc.dsem, cc.dsem.n)
        kb.wait(e, cc.dgi, cc.dgi.n)

    if "cnts_global" in HD:
        kb.op("t", "transpose", P[0][0:8, 0:1], OFS[0:1, 0:8],
              cc.ONES_ROW32[0:1, 0:1])
    else:
        kb.op("t", "matmul", P[0][0:8, 0:1], CNT[:, :], ONE8[:, :], start=True,
              stop=True)
    tv = kb.mark("t", cc.tm)
    kb.wait("v", cc.tm, tv)
    kb.op("v", "tensor_copy", GCC[:, :], P[0][0:8, 0:1])
    kb.wait("t", *kb.last("v"))
    kb.op("t", "matmul", P[0][0:1, 0:8], GCC[:, :], UT8[:, :], start=True,
          stop=True)
    tv = kb.mark("t", cc.tm)
    kb.wait("v", cc.tm, tv)
    kb.op("v", "tensor_copy", OFS[:, :], P[0][0:1, 0:8])
    kb.wait("t", *kb.last("v"))
    kb.op("t", "matmul", P[0][:, 0:8], cc.ONES_ROW32[0:1, :], OFS[0:1, :],
          start=True, stop=True)
    tv = kb.mark("t", cc.tm)
    kb.wait("v", cc.tm, tv)
    kb.op("v", "tensor_copy", OFB[:, :], P[0][:, 0:8])
    kb.wait("t", *kb.last("v"))

    we_free = 0
    for g in range(8):
        if we_free:
            kb.wait("g", cc.tm, we_free)
        wl = kb.dma("g", WE[:, 0:8 * 512].rearrange("p (k h) -> p k h", h=512),
                    HD["L1_ffn_w1"][g].rearrange("(k p) h -> p k h", p=128),
                    cc.dws)
        wl3 = kb.dma("g", WE[:, 8 * 512:16 * 512].rearrange("p (k h) -> p k h", h=512),
                     HD["L1_ffn_w3"][g].rearrange("(k p) h -> p k h", p=128),
                     cc.dws)
        kb.wait("t", cc.dws, wl3)
        if g > 0:
            kb.wait("t", *kb.last("v"))
        for k in range(DT):
            kb.op("t", "matmul", P[0][0:8, 0:512], XF8[:, k * 8:(k + 1) * 8],
                  WE[:, k * 512:(k + 1) * 512], start=(k == 0), stop=(k == DT - 1))
        for k in range(DT):
            kb.op("t", "matmul", P[1][0:8, 0:512], XF8[:, k * 8:(k + 1) * 8],
                  WE[:, (8 + k) * 512:(9 + k) * 512], start=(k == 0),
                  stop=(k == DT - 1))
        tv = kb.mark("t", cc.tm)
        we_free = tv
        kb.wait("v", cc.tm, tv)
        kb.op("v", "tensor_copy", TB1[0:8, g * 512:(g + 1) * 512], P[0][0:8, 0:512])
        kb.op("v", "tensor_copy", TB3[0:8, g * 512:(g + 1) * 512], P[1][0:8, 0:512])
    sv = kb.dma("y", TDR1.ap(), TB1[0:8, :].rearrange("e (g h) -> e g h", h=512),
                cc.dout, waits=[kb.last("v")])
    sv = kb.dma("y", TDR3.ap(), TB3[0:8, :].rearrange("e (g h) -> e g h", h=512),
                cc.dout)
    l1 = kb.dma("g", T64A[:, :], TDR1.ap().rearrange("e g h -> (e g) h"), cc.dgi,
                waits=[(cc.dout, sv)])
    l2 = kb.dma("g", T64B[:, :], TDR3.ap().rearrange("e g h -> (e g) h"), cc.dgi)
    kb.barrier()
    for e in ("v", "s", "t", "g"):
        kb.wait(e, cc.dsem, cc.dsem.n)
        kb.wait(e, cc.dgi, cc.dgi.n)

    PTB = P[7][:, :].bitcast(BF16)
    first = True
    for b in range(nbatch):
        for tc in range(TC):
            ol = kb.dma("y", OH[:, :], HD["ohw"][b, tc], cc.dsem,
                        waits=[kb.last("v")] if not first else ())
            kb.wait("v", cc.dsem, ol)
            jcol = JA[:, b * TC + tc:b * TC + tc + 1]
            for slot in range(2):
                if not first:
                    kb.wait("v", cc.tm, cc.tm.n)
                if slot == 0:
                    kb.op("v", "tensor_scalar", CMP[:, 0:8], OFB[:, :], jcol,
                          None, OP.is_gt)
                else:
                    kb.op("v", "tensor_scalar", CMP[:, 0:8], OFB[:, :], jcol,
                          1.0, OP.subtract, OP.is_gt)
                kb.op("v", "tensor_copy", GA[:, 0:1], CMP[:, 0:1])
                kb.op("v", "tensor_sub", GA[:, 1:8], CMP[:, 1:8], CMP[:, 0:7])
                oh = OH[:, 0:8] if slot == 0 else OH[:, 8:16]
                wa = OH[:, 16:17] if slot == 0 else OH[:, 17:18]
                kb.op("v", "tensor_mul", E64[:, :].rearrange("p (e g) -> p e g", g=8),
                      oh.rearrange("p (e o) -> p e o", o=1).broadcast_to([128, 8, 8]),
                      GA[:, 0:8].rearrange("p (o g) -> p o g", o=1).broadcast_to([128, 8, 8]))
                kb.wait("t", *kb.last("v"))
                kb.op("t", "transpose", PTB[:, 0:128][0:64, :] if False else PTB[0:64, 0:128],
                      E64[:, :], cc.IDENT[:, :])
                tv = kb.mark("t", cc.tm)
                kb.wait("v", cc.tm, tv)
                kb.op("v", "tensor_copy", E64T[:, :], PTB[0:64, 0:128])
                kb.wait("t", *kb.last("v"))
                kb.op("t", "matmul", P[0][:, 0:512], E64T[:, :], T64A[:, :],
                      start=True, stop=True)
                kb.op("t", "matmul", P[1][:, 0:512], E64T[:, :], T64B[:, :],
                      start=True, stop=True)
                tv = kb.mark("t", cc.tm)
                kb.wait("s", cc.tm, tv)
                kb.op("s", "activation", T1[:, :], P[0][:, 0:512], AF.Silu,
                      scale=wa)
                kb.wait("v", cc.tm, tv)
                kb.op("v", "tensor_scalar", T2[:, :], P[1][:, 0:512], wa, None,
                      OP.mult)
                kb.op("v", "tensor_tensor", HB[:, :], T1[:, :], T2[:, :],
                      OP.mult, waits=[kb.last("s")])
                kb.wait("t", *kb.last("v"))
                kb.op("t", "matmul", P[2][0:64, 0:512], E64[:, :], HB[:, :],
                      start=first, stop=(b == nbatch - 1 and tc == TC - 1 and
                                         slot == 1))
                kb.mark("t", cc.tm)
                first = False
    kb.wait("v", cc.tm, cc.tm.n)
    kb.op("v", "tensor_copy", HP[:, :], P[2][0:64, 0:512])
    sv = kb.dma("y", HD["hpart"][:, :], HP[:, :], cc.dout, waits=[kb.last("v")])
    kb.wait("y", cc.dout, sv)


def build_l1f_b(T, nbatch):
    nc = bass.Bass("TRN2", target_bir_lowering=False)
    HD = {}
    TC = T // 128
    HD["cnts"] = nc.dram_tensor("cnts", [8, 8], F32, kind="ExternalInput")
    HD["xf8"] = nc.dram_tensor("xf8", [128, DT * 8], BF16, kind="ExternalInput")
    HD["ohw"] = nc.dram_tensor("ohw", [nbatch, TC, 128, 20], F32,
                               kind="ExternalInput").ap()
    HD["c_one8"] = nc.dram_tensor("c_one8", [8, 1], F32, kind="ExternalInput")
    HD["c_ut8"] = nc.dram_tensor("c_ut8", [8, 8], F32, kind="ExternalInput")
    HD["c_ja"] = nc.dram_tensor("c_ja", [128, nbatch * TC], F32,
                                kind="ExternalInput")
    HD["L1_ffn_w1"] = nc.dram_tensor("L1_ffn_w1", [8, D, 512], F32,
                                     kind="ExternalInput").ap()
    HD["L1_ffn_w3"] = nc.dram_tensor("L1_ffn_w3", [8, D, 512], F32,
                                     kind="ExternalInput").ap()
    HD["hpart"] = nc.dram_tensor("hpart", [64, 512], F32, kind="ExternalOutput")
    kb = KB(nc)
    cc = CC(kb, T)
    emit_l1f_b(kb, cc, HD, nbatch)
    kb.finalize()
    return nc


def emit_l1f_c(kb, cc, HD, nbatch):
    T = cc.T
    P = cc.P
    nc = kb.nc
    HPA = kb.sb("f1c_HPA", [64, 512], F32)
    HPS = kb.sb("f1c_HPS", [64, 512], F32)
    HGT = kb.sb("f1c_HGT", [128, 4 * 8], BF16)
    W2 = kb.sb("f1c_W2", [128, 4 * 1024], BF16)
    RR = kb.sb("f1c_RR", [8, 1024], F32)
    RRB = kb.sb("f1c_RRB", [8, 1024], BF16)
    UPT = kb.sb("f1c_UPT", [128, DT * 8], F32)
    X8 = kb.sb("f1c_X8", [128, DT * 8], F32)
    MSK = kb.sb("f1c_MSK", [8, 1], F32)
    HDR = kb.dram("f1c_hdr", [64, 512], F32)

    kb.dma("y", MSK[:, :], HD["maskval"][:, :], cc.dsem)
    kb.barrier()
    for e in ("v", "s", "t", "g"):
        kb.wait(e, cc.dsem, cc.dsem.n)

    if "hpart_global" in HD:
        l = kb.dma("y", HPS[:, :], HD["hpart_global"][:, :], cc.dsem,
                   waits=[kb.last("v")])
        kb.wait("v", cc.dsem, l)
        kb.op("v", "tensor_copy", HPS[:, :], HPS[:, :])
    else:
        for c in range(8):
            l = kb.dma("y", HPA[:, :], HD["hparts"][c], cc.dsem,
                       waits=[kb.last("v")] if c else ())
            kb.wait("v", cc.dsem, l)
            if c == 0:
                kb.op("v", "tensor_copy", HPS[:, :], HPA[:, :])
            else:
                kb.op("v", "tensor_add", HPS[:, :], HPS[:, :], HPA[:, :])
    sv_hdr = kb.dma("y", HDR.ap(), HPS[:, :], cc.dout, waits=[kb.last("v")])
    for g in range(8):
        hsrc = HDR.ap().rearrange("(e g) (q p) -> g p q e", g=8, p=128)[g]
        for q in range(4):
            l = kb.dma("g", HGT[:, q * 8:(q + 1) * 8], hsrc[:, q, :],
                       cc.dws, allow_slow_non_contiguous=True,
                       waits=([(cc.tm, cc.tm.n)] if (g and q == 0) else []) +
                             ([(cc.dout, sv_hdr)] if (g == 0 and q == 0) else []))
        wl = kb.dma("g", W2[:, :].rearrange("p (q n) -> p q n", n=1024),
                    HD["L1_ffn_w2"][g].rearrange("(q p) n -> p q n", p=128),
                    cc.dws)
        kb.wait("t", cc.dws, wl)
        for q in range(4):
            for half in range(2):
                kb.op("t", "matmul", P[half][0:8, 0:512],
                      HGT[:, q * 8:(q + 1) * 8],
                      W2[:, q * 1024 + half * 512:q * 1024 + (half + 1) * 512],
                      start=(g == 0 and q == 0), stop=(g == 7 and q == 3))
        kb.mark("t", cc.tm)
    kb.wait("v", cc.tm, cc.tm.n)
    kb.op("v", "tensor_scalar", RR[:, 0:512], P[0][0:8, 0:512],
          MSK[:, 0:1], None, OP.mult)
    kb.op("v", "tensor_scalar", RR[:, 512:1024], P[1][0:8, 0:512],
          MSK[:, 0:1], None, OP.mult)
    kb.op("v", "tensor_copy", RRB[:, :], RR[:, :])
    kb.wait("t", *kb.last("v"))
    PTB = P[7][:, :].bitcast(BF16)
    for k in range(DT):
        kb.op("t", "transpose", PTB[:, k * 8:(k + 1) * 8],
              RRB[:, k * 128:(k + 1) * 128], cc.IDENT[0:8, 0:8])
    tv = kb.mark("t", cc.tm)
    kb.wait("v", cc.tm, tv)
    kb.op("v", "tensor_copy", UPT[:, :], PTB[:, 0:DT * 8])
    if not HD.get("inplace"):
        for b in range(nbatch):
            kb.dma("y", HD["xout"][b], HD["xa"][b], cc.dout)
    x8l = kb.dma("y", X8[:, :].rearrange("p (k c) -> p k c", c=8),
                 HD["xa"][0].rearrange("(k p) t -> p k t", p=128)[:, :, 0:8],
                 cc.dsem)
    kb.wait("v", cc.dsem, x8l)
    kb.op("v", "tensor_add", X8[:, :], X8[:, :], UPT[:, :])
    sv = kb.dma("y", HD["xout"][0].rearrange("(k p) t -> p k t", p=128)[:, :, 0:8],
                X8[:, :].rearrange("p (k c) -> p k c", c=8), cc.dout,
                waits=[kb.last("v")])
    kb.wait("y", cc.dout, sv)


def build_l1f_c(T, nbatch):
    nc = bass.Bass("TRN2", target_bir_lowering=False)
    HD = {}
    HD["xa"] = nc.dram_tensor("xa", [nbatch, D, T], F32, kind="ExternalInput").ap()
    HD["xout"] = nc.dram_tensor("xout", [nbatch, D, T], F32,
                                kind="ExternalOutput").ap()
    HD["hparts"] = nc.dram_tensor("hparts", [8, 64, 512], F32,
                                  kind="ExternalInput").ap()
    HD["maskval"] = nc.dram_tensor("maskval", [8, 1], F32, kind="ExternalInput")
    HD["L1_ffn_w2"] = nc.dram_tensor("L1_ffn_w2", [8, 512, D], F32,
                                     kind="ExternalInput").ap()
    kb = KB(nc)
    cc = CC(kb, T)
    emit_l1f_c(kb, cc, HD, nbatch)
    kb.finalize()
    return nc



def emit_l1a(kb, cc, HD, nbatch):
    T, TC = cc.T, cc.TC
    P = cc.P
    nc = kb.nc
    NH, DK, DV = 8, 128, 128
    NCH = T // 16
    NG = NCH // 8

    XNO = kb.sb("l1a_XNO", [128, DT * T], BF16)
    XN = XNO
    OF = XNO
    OALL = kb.sb("l1a_OALL", [128, DT * T], BF16)
    if DT * T * 2 >= DT * 512 * 4:
        XCHK = OALL[:, :].bitcast(F32)[:, 0:DT * 512]
    else:
        XCHK = kb.sb("l1a_XCHK", [128, DT * 512], F32)[:, :]
    WPRJ = kb.sb("l1a_WPRJ", [128, 4 * DT * 128], BF16)
    WOC = WPRJ[:, 0:2 * DT * 128]
    WA = kb.sb("l1a_WA", [128, DT * 8], BF16)

    QT = kb.sb("l1a_QT", [128, T], BF16)
    KT = kb.sb("l1a_KT", [128, T], BF16)
    VTK = kb.sb("l1a_VTK", [128, T], BF16)
    SCO = kb.sb("l1a_SCO", [128, T], F32)
    SMH = kb.sb("l1a_SMH", [128, T], BF16)
    FHALL = kb.sb("l1a_FHALL", [128, TC * 8], F32)
    TH = kb.sb("l1a_TH", [128, 4 * TC], F32)
    RST = kb.sb("l1a_RST", [128, 2 * TC], F32)
    NWB = kb.sb("l1a_NWB", [128, 128], F32)
    U1 = kb.sb("l1a_U1", [128, 1024], F32)
    U2 = kb.sb("l1a_U2", [128, max(T, 1024)], F32)
    NCQ = NCH // 4
    QF = kb.sb("l1a_QF", [128, T], BF16)
    SFM = kb.sb("l1a_SFM", [128, T], BF16)
    KPFX = kb.sb("l1a_KPFX", [128, T], BF16)
    FF2 = kb.sb("l1a_FF2", [128, T], F32)
    FF2X = None
    EF = kb.sb("l1a_EF", [128, T], BF16)
    QD = kb.sb("l1a_QD", [128, T], BF16)
    DPR = kb.sb("l1a_DPR", [128, 8 * 256], F32)
    EPB = kb.sb("l1a_EPB", [128, 8 * 256], BF16)
    PP = kb.sb("l1a_PP", [128, 8 * 256], BF16)
    ASR = kb.sb("l1a_ASR", [1, 2 * 8 * 256], BF16)
    MASKP = kb.sb("l1a_MASKP", [128, 256], F32)
    KVP = kb.sb("l1a_KVP", [128, 2 * (NCH // 4) * 128], BF16)
    A16 = kb.sb("l1a_A16", [128, NCH * 16], BF16)
    OBP = kb.sb("l1a_OBP", [128, (NCH // 4) * 128], BF16)
    FF = SCO
    SST = kb.sb("l1a_SST", [128, 128], F32)
    SSB = kb.sb("l1a_SSB", [128, 128], BF16)
    DCC = kb.sb("l1a_DCC", [128, NCH], F32)
    GNW = kb.sb("l1a_GNW", [128, 1024], BF16)
    NEA = kb.sb("l1a_NEA", [128, 8], F32)
    NEA8 = kb.sb("l1a_NEA8", [8, 1], F32)
    DTB8 = kb.sb("l1a_DTB8", [128, 8], F32)
    DTBR = kb.sb("l1a_DTBR", [8, 1], F32)
    DTROW = kb.sb("l1a_DTROW", [1, 8], F32)
    A8 = kb.sb("l1a_A8", [1, 8], F32)
    FHB = FF2
    FH8 = FF2[0:8, 0:T]
    U8 = U2[0:8, 0:T]
    EK = SMH
    EKT = EF

    TM_DR = kb.dram("l1a_tm_dr", [T, 128], BF16)
    TM2_DR = kb.dram("l1a_tm2_dr", [T, 128], BF16)
    A_DR = kb.dram("l1a_a_dr", [NCH, 16, 16], BF16)
    O_DR = kb.dram("l1a_o_dr", [16, NCH * 128], BF16)
    FH_DR = kb.dram("l1a_fh_dr", [8, T], F32)

    kb.dma("y", MASKP[:, :], HD["c_maskp"][:, :], cc.dsem)
    kb.dma("y", GNW[:, :], HD["c_gnw"][:, :], cc.dsem)
    kb.op("g", "dma_start", out=A8[:, :],
          in_=HD["L1_attn_A_log"].ap().rearrange("(o n) -> o n", o=1),
          incs=[(cc.dgi, 16)], selfsync=False)
    kb.op("g", "dma_start", out=DTBR[:, :],
          in_=HD["L1_attn_dt_bias"].ap().rearrange("(n o) -> n o", o=1),
          incs=[(cc.dgi, 16)], selfsync=False)
    kb.op("g", "dma_start", out=DTROW[:, :],
          in_=HD["L1_attn_dt_bias"].ap().rearrange("(o n) -> o n", o=1),
          incs=[(cc.dgi, 16)], selfsync=False)
    kb.barrier()
    for e in ("v", "s", "t", "g"):
        kb.wait(e, cc.dsem, cc.dsem.n)
        kb.wait(e, cc.dgi, cc.dgi.n)
    kb.op("t", "matmul", P[0][:, 0:8], cc.ONES_ROW32[0:1, :], A8[0:1, :],
          start=True, stop=True)
    tv = kb.mark("t", cc.tm)
    kb.wait("s", cc.tm, tv)
    kb.op("s", "activation", NEA[:, :], P[0][:, 0:8], AF.Exp)
    kb.op("v", "tensor_scalar", NEA[:, :], NEA[:, :], -1.0, None, OP.mult,
          waits=[kb.last("s")])
    kb.wait("t", *kb.last("v"))
    kb.op("t", "transpose", P[1][0:8, 0:1], A8[0:1, 0:8], cc.ONES_ROW32[0:1, 0:1])
    tv = kb.mark("t", cc.tm)
    kb.wait("s", cc.tm, tv)
    kb.op("s", "activation", NEA8[:, :], P[1][0:8, 0:1], AF.Exp)
    kb.op("v", "tensor_scalar", NEA8[:, :], NEA8[:, :], -1.0, None, OP.mult,
          waits=[kb.last("s")])
    kb.wait("t", *kb.last("v"))
    kb.op("t", "matmul", P[0][:, 0:8], cc.ONES_ROW32[0:1, :], DTROW[0:1, :],
          start=True, stop=True)
    tv = kb.mark("t", cc.tm)
    kb.wait("v", cc.tm, tv)
    kb.op("v", "tensor_copy", DTB8[:, :], P[0][:, 0:8])
    kb.wait("t", *kb.last("v"))
    kb.barrier()

    wq_r = HD["L1_attn_Wq"].ap().rearrange("(k p) (h c) -> p k h c", p=128, c=128)
    wk_r = HD["L1_attn_Wk"].ap().rearrange("(k p) (h c) -> p k h c", p=128, c=128)
    wv_r = HD["L1_attn_Wv"].ap().rearrange("(k p) (h c) -> p k h c", p=128, c=128)
    wr_r = HD["L1_attn_Wr"].ap().rearrange("(k p) (h c) -> p k h c", p=128, c=128)
    wa_r = HD["L1_attn_Wa"].ap().rearrange("(k p) h -> p k h", p=128)
    wo_r = HD["L1_attn_Wo"].ap().rearrange("(k p) (oc c) -> p k oc c", p=128, c=128)
    PTB = P[7][:, :].bitcast(BF16)
    PTB6 = P[6][:, :].bitcast(BF16)

    def tbatch(items):
        kb.wait("t", kb.self_sem["s"], kb.self_sem["s"].n)
        done = [0, 0]
        for i, (s_, d_) in enumerate(items):
            slb = [PTB6, PTB][i % 2]
            if done[i % 2]:
                kb.wait("t", kb.self_sem["v"], done[i % 2])
            kb.op("t", "transpose", slb[:, 0:128], s_, cc.IDENT[:, :])
            tv = kb.mark("t", cc.tm)
            kb.wait("v", cc.tm, tv)
            kb.op("v", "tensor_copy", d_, slb[:, 0:128])
            done[i % 2] = kb.last("v")[1]
        kb.wait("t", *kb.last("v"))

    for b in range(nbatch):
        def load_a(t0, w):
            xld = kb.dma("y", XCHK[:, 0:DT * w].rearrange("p (k t) -> p k t", k=DT),
                         HD["xin"][b].rearrange("(k p) t -> p k t", p=128)[:, :, t0:t0 + w],
                         cc.dsem, waits=[kb.last("v")])
            kb.wait("v", cc.dsem, xld)

            def get(k, w):
                return XCHK[:, k * w:(k + 1) * w]
            return get

        def apply_a(t0, w):
            for k in range(DT):
                sl = slice(k * T + t0, k * T + t0 + w)
                kb.op("v", "tensor_mul", U1[:, 0:w], XCHK[:, k * w:(k + 1) * w],
                      cc.RS[:, 0:w])
                kb.op("v", "tensor_add", XN[:, sl], U1[:, 0:w], cc.NMR[:, 0:w])
        cc.ln_stream(load_a, apply_a)
        kb.barrier()

        wal = kb.dma("g", WA[:, :].rearrange("p (k h) -> p k h", h=8), wa_r,
                     cc.dws)
        kb.wait("t", cc.dws, wal)
        for tc in range(TC):
            t0 = tc * 128
            for k in range(DT):
                kb.op("t", "matmul", P[4][:, 0:8],
                      XN[:, k * T + t0:k * T + t0 + 128], WA[:, k * 8:(k + 1) * 8],
                      start=(k == 0), stop=(k == DT - 1))
            tv = kb.mark("t", cc.tm)
            kb.wait("v", cc.tm, tv)
            kb.op("v", "tensor_add", U1[:, 0:8], P[4][:, 0:8], DTB8[:, :])
            kb.op("s", "activation", U2[:, 0:8], U1[:, 0:8], AF.Abs,
                  waits=[kb.last("v")])
            kb.op("s", "activation", U2[:, 0:8], U2[:, 0:8], AF.Exp,
                  scale=cc.kf(-1.0))
            kb.op("s", "activation", U2[:, 0:8], U2[:, 0:8], AF.Ln,
                  bias=cc.kf(1.0))
            kb.op("v", "tensor_scalar", U1[:, 0:8], U1[:, 0:8], 0.0, None,
                  OP.max, waits=[kb.last("s")])
            kb.op("v", "tensor_add", U1[:, 0:8], U1[:, 0:8], U2[:, 0:8])
            kb.op("v", "tensor_mul", FHALL[:, tc * 8:(tc + 1) * 8], U1[:, 0:8],
                  NEA[:, :])
            kb.wait("t", *kb.last("v"))
        for t0 in range(0, T, 512):
            w = min(512, T - t0)
            for k in range(DT):
                kb.op("t", "matmul", P[5][0:8, 0:w], WA[:, k * 8:(k + 1) * 8],
                      XN[:, k * T + t0:k * T + t0 + w],
                      start=(k == 0), stop=(k == DT - 1))
            tv = kb.mark("t", cc.tm)
            kb.wait("v", cc.tm, tv)
            kb.op("v", "tensor_scalar", FH8[:, t0:t0 + w], P[5][0:8, 0:w],
                  DTBR[:, 0:1], None, OP.add)
            kb.op("s", "activation", U8[:, t0:t0 + w], FH8[:, t0:t0 + w], AF.Abs,
                  waits=[kb.last("v")])
            kb.op("s", "activation", U8[:, t0:t0 + w], U8[:, t0:t0 + w], AF.Exp,
                  scale=cc.kf(-1.0, 8))
            kb.op("s", "activation", U8[:, t0:t0 + w], U8[:, t0:t0 + w], AF.Ln,
                  bias=cc.kf(1.0, 8))
            kb.op("v", "tensor_scalar", FH8[:, t0:t0 + w], FH8[:, t0:t0 + w],
                  0.0, None, OP.max, waits=[kb.last("s")])
            kb.op("v", "tensor_add", FH8[:, t0:t0 + w], FH8[:, t0:t0 + w],
                  U8[:, t0:t0 + w])
            kb.op("v", "tensor_scalar", FH8[:, t0:t0 + w], FH8[:, t0:t0 + w],
                  NEA8[:, 0:1], None, OP.mult)
            kb.wait("t", *kb.last("v"))
        fhs = kb.dma("y", FH_DR.ap(), FH8[:, :], cc.dout, waits=[kb.last("v")])
        kb.wait("v", cc.dout, fhs)
        kb.wait("s", cc.dout, fhs)

        ogather_v = 0
        for h in range(NH):
            for wi, wr_ in enumerate((wq_r, wk_r, wv_r, wr_r)):
                kb.dma("g", WPRJ[:, wi * DT * 128:(wi + 1) * DT * 128].rearrange(
                    "p (k c) -> p k c", c=128), wr_[:, :, h, :], cc.dws,
                    waits=[(cc.tm, cc.tm.n)] if wi == 0 else ())
            wl = cc.dws.n
            kb.wait("t", cc.dws, wl)
            pev = {}
            for tc in range(TC):
                t0 = tc * 128
                base = (tc % 2) * 4
                for wi in range(4):
                    bk = base + wi
                    if bk in pev:
                        kb.wait("t", kb.self_sem["s"], pev[bk])
                    for k in range(DT):
                        kb.op("t", "matmul", P[bk][:, 0:128],
                              XN[:, k * T + t0:k * T + t0 + 128],
                              WPRJ[:, wi * DT * 128 + k * 128:
                                   wi * DT * 128 + (k + 1) * 128],
                              start=(k == 0), stop=(k == DT - 1))
                tv = kb.mark("t", cc.tm)
                kb.wait("s", cc.tm, tv)
                for wi, (dst, af) in enumerate(((QT, AF.Silu), (KT, AF.Silu),
                                                (VTK, AF.Silu), (SCO, AF.Sigmoid))):
                    kb.op("s", "activation", dst[:, t0:t0 + 128],
                          P[base + wi][:, 0:128], af)
                    pev[base + wi] = kb.last("s")[1]
            kb.wait("t", *kb.last("s"))
            kb.wait("v", *kb.last("s"))
            nw = kb.dma("y", NWB[:, :], HD["c_qnw"][:, :], cc.dsem,
                        waits=[kb.last("v")])
            kb.wait("v", cc.dsem, nw)
            for wi, TT_ in enumerate((QT, KT)):
                if wi == 1:
                    nw = kb.dma("y", NWB[:, :], HD["c_knw"][:, :], cc.dsem,
                                waits=[kb.last("v")])
                    kb.wait("v", cc.dsem, nw)
                h1 = T // 2
                nh1 = h1 // 128
                kb.op("s", "activation", U1[:, 0:h1], TT_[:, 0:h1],
                      AF.Square, waits=[kb.last("v")])
                kb.op("v", "tensor_reduce", RST[:, 0:nh1],
                      U1[:, 0:h1].rearrange("p (tc m) -> p tc m", m=128),
                      AX.X, OP.add, waits=[kb.last("s")])
                kb.op("s", "activation", U2[:, 0:h1], TT_[:, h1:T], AF.Square)
                kb.op("v", "tensor_reduce", RST[:, nh1:TC],
                      U2[:, 0:h1].rearrange("p (tc m) -> p tc m", m=128),
                      AX.X, OP.add, waits=[kb.last("s")])
                kb.op("s", "activation", RST[:, TC:2 * TC], RST[:, 0:TC],
                      AF.Ln, bias=cc.kf(1e-5), scale=cc.kf(1.0 / 128),
                      waits=[kb.last("v")])
                kb.op("s", "activation", RST[:, TC:2 * TC], RST[:, TC:2 * TC],
                      AF.Exp, scale=cc.kf(-0.5))
                t3 = TT_[:, :].rearrange("p (tc m) -> p tc m", m=128)
                kb.op("v", "tensor_tensor", t3, t3,
                      RST[:, TC:2 * TC].rearrange("p (tc o) -> p tc o", o=1
                                                  ).broadcast_to([128, TC, 128]),
                      OP.mult, waits=[kb.last("s")])
                kb.op("v", "tensor_tensor", t3, t3,
                      NWB[:, :].rearrange("p (o m) -> p o m", o=1
                                          ).broadcast_to([128, TC, 128]),
                      OP.mult)
            sco3 = SCO[:, :].rearrange("p (tc m) -> p tc m", m=128)
            u13 = SMH[:, :].rearrange("p (tc m) -> p tc m", m=128)
            kb.op("v", "memset", TH[:, 0:TC], 0.0)
            kb.op("v", "memset", TH[:, TC:2 * TC], 1.0)
            for it in range(26):
                kb.op("v", "tensor_add", TH[:, 2 * TC:3 * TC], TH[:, 0:TC],
                      TH[:, TC:2 * TC])
                kb.op("v", "tensor_scalar", TH[:, 2 * TC:3 * TC],
                      TH[:, 2 * TC:3 * TC], 0.5, None, OP.mult)
                kb.op("v", "tensor_tensor", u13, sco3,
                      TH[:, 2 * TC:3 * TC].rearrange("p (tc o) -> p tc o", o=1
                                                     ).broadcast_to([128, TC, 128]),
                      OP.is_ge)
                kb.op("v", "tensor_reduce", TH[:, 3 * TC:4 * TC], u13, AX.X,
                      OP.add)
                kb.op("v", "tensor_scalar", TH[:, 3 * TC:4 * TC],
                      TH[:, 3 * TC:4 * TC], 32.0, None, OP.is_ge)
                kb.op("v", "tensor_mul", U2[:, 0:TC], TH[:, 3 * TC:4 * TC],
                      TH[:, 2 * TC:3 * TC])
                kb.op("v", "tensor_max", TH[:, 0:TC], TH[:, 0:TC], U2[:, 0:TC])
                kb.op("v", "scalar_tensor_tensor", U2[:, 0:TC],
                      TH[:, 3 * TC:4 * TC], 1e30, TH[:, 2 * TC:3 * TC],
                      OP.mult, OP.add)
                kb.op("v", "tensor_tensor", TH[:, TC:2 * TC], TH[:, TC:2 * TC],
                      U2[:, 0:TC], OP.min)
            kb.op("v", "tensor_tensor", u13, sco3,
                  TH[:, 0:TC].rearrange("p (tc o) -> p tc o", o=1
                                        ).broadcast_to([128, TC, 128]), OP.is_ge)
            kb.op("v", "tensor_mul", U2[:, 0:T], SMH[:, :], SCO[:, :])
            kb.op("v", "tensor_reduce", TH[:, 2 * TC:3 * TC],
                  U2[:, 0:T].rearrange("p (tc m) -> p tc m", m=128), AX.X, OP.add)
            kb.op("v", "tensor_scalar", TH[:, 2 * TC:3 * TC],
                  TH[:, 2 * TC:3 * TC], 1e-9, None, OP.add)
            kb.op("v", "reciprocal", TH[:, 3 * TC:4 * TC], TH[:, 2 * TC:3 * TC])
            kb.op("v", "tensor_tensor", SMH[:, :].rearrange("p (tc m) -> p tc m", m=128),
                  U2[:, 0:T].rearrange("p (tc m) -> p tc m", m=128),
                  TH[:, 3 * TC:4 * TC].rearrange("p (tc o) -> p tc o", o=1
                                                 ).broadcast_to([128, TC, 128]),
                  OP.mult)
            for tc in range(TC):
                t0 = tc * 128
                kb.op("v", "tensor_scalar", FF2[:, t0:t0 + 128],
                      SMH[:, t0:t0 + 128], FHALL[:, tc * 8 + h:tc * 8 + h + 1],
                      None, OP.mult)
            kb.op("s", "activation", FF2[:, 0:T], FF2[:, 0:T], AF.Exp,
                  waits=[kb.last("v")])
            kb.op("v", "tensor_scalar", FF2[:, 0:T], FF2[:, 0:T], -1.0, 1.0,
                  OP.mult, OP.add, waits=[kb.last("s")])
            kb.op("v", "tensor_mul", KT[:, :], FF2[:, 0:T], KT[:, :])
            kb.wait("t", *kb.last("v"))
            items = []
            for tc in range(TC):
                items.append((QT[:, tc * 128:(tc + 1) * 128],
                              QF[:, tc * 128:(tc + 1) * 128]))
                items.append((SMH[:, tc * 128:(tc + 1) * 128],
                              SFM[:, tc * 128:(tc + 1) * 128]))
                items.append((KT[:, tc * 128:(tc + 1) * 128],
                              KPFX[:, tc * 128:(tc + 1) * 128]))
            tbatch(items)
            fb = kb.dma("y", FHB[:, :],
                        FH_DR[h:h + 1, :].broadcast_to([128, T]), cc.dsem,
                        waits=[(cc.dout, fhs), kb.last("v")])
            kb.wait("v", cc.dsem, fb)
            kb.op("v", "tensor_mul", FF[:, :], SFM[:, :], FHB[:, :])
            cur, nxt = FF, FF2
            for sh in (1, 2, 4, 8):
                c3 = cur[:, :].rearrange("p (ch s) -> p ch s", s=16)
                n3 = nxt[:, :].rearrange("p (ch s) -> p ch s", s=16)
                kb.op("v", "tensor_add", n3[:, :, sh:16], c3[:, :, sh:16],
                      c3[:, :, 0:16 - sh])
                kb.op("v", "tensor_copy", n3[:, :, 0:sh], c3[:, :, 0:sh])
                cur, nxt = nxt, cur
            FCUM = cur
            kb.op("s", "activation", EF[:, :], FCUM[:, :], AF.Exp,
                  waits=[kb.last("v")])
            f3 = FCUM[:, :].rearrange("p (ch s) -> p ch s", s=16)
            kb.op("v", "tensor_tensor", FF2[:, :].rearrange("p (ch s) -> p ch s", s=16),
                  f3[:, :, 15:16].broadcast_to([128, NCH, 16]), f3, OP.subtract)
            kb.op("s", "activation", EK[:, :], FF2[:, :], AF.Exp,
                  waits=[kb.last("v")])
            kb.op("s", "activation", DCC[:, :],
                  FCUM[:, :].rearrange("p (ch s) -> p ch s", s=16)[:, :, 15],
                  AF.Exp)
            kb.op("v", "tensor_mul", QD[:, :], QF[:, :], EF[:, :],
                  waits=[kb.last("s")])
            pa_done = 0
            ppg_done = 0
            ppmm_done = 0
            exp_done = 0
            asr_sv = [0, 0]
            for g in range(NG):
                fg = FCUM[:, g * 128:(g + 1) * 128].rearrange(
                    "p (ch s) -> p ch s", s=16)
                d4 = DPR[:, :].rearrange("p (ch s t) -> p ch s t", s=16, t=16)
                if exp_done:
                    kb.wait("v", kb.self_sem["s"], exp_done)
                kb.op("v", "tensor_tensor", d4,
                      fg.rearrange("p ch (o t) -> p ch o t", o=1).broadcast_to(
                          [128, 8, 16, 16]),
                      fg.rearrange("p ch (s o) -> p ch s o", o=1).broadcast_to(
                          [128, 8, 16, 16]),
                      OP.subtract)
                kb.op("v", "tensor_tensor", DPR[:, :].rearrange(
                    "p (ch q) -> p ch q", q=256),
                    DPR[:, :].rearrange("p (ch q) -> p ch q", q=256),
                    MASKP[:, :].rearrange("p (o q) -> p o q", o=1
                                          ).broadcast_to([128, 8, 256]), OP.add)
                if ppg_done:
                    kb.wait("s", kb.self_sem["g"], ppg_done)
                kb.op("s", "activation", EPB[:, :], DPR[:, :], AF.Exp,
                      waits=[kb.last("v")])
                exp_done = kb.last("s")[1]
                qg = QF[:, g * 128:(g + 1) * 128].rearrange(
                    "p (ch t) -> p ch t", t=16)
                kgf = KPFX[:, g * 128:(g + 1) * 128].rearrange(
                    "p (ch s) -> p ch s", s=16)
                if ppmm_done:
                    kb.wait("v", cc.tm, ppmm_done)
                kb.op("v", "tensor_tensor", PP[:, :].rearrange(
                    "p (ch s t) -> p ch s t", s=16, t=16),
                    qg.rearrange("p ch (o t) -> p ch o t", o=1).broadcast_to(
                        [128, 8, 16, 16]),
                    kgf.rearrange("p ch (s o) -> p ch s o", o=1).broadcast_to(
                        [128, 8, 16, 16]),
                    OP.mult)
                kb.op("g", "tensor_tensor", PP[:, :], PP[:, :], EPB[:, :],
                      OP.mult, waits=[kb.last("v"), (kb.self_sem["s"], exp_done)])
                ppg_done = kb.last("g")[1]
                kb.wait("t", kb.self_sem["g"], ppg_done)
                asl = g % 2
                for q in range(4):
                    pb = 5 + q % 2
                    if pa_done:
                        kb.wait("t", kb.self_sem["s"], pa_done)
                    kb.op("t", "matmul", P[pb][0:1, 0:512], cc.ONES_COL[:, :],
                          PP[:, q * 512:(q + 1) * 512], start=True, stop=True)
                    tv = kb.mark("t", cc.tm)
                    if q == 3:
                        ppmm_done = tv
                    kb.wait("s", cc.tm, tv)
                    if q == 0 and asr_sv[asl]:
                        kb.wait("s", cc.dout, asr_sv[asl])
                    kb.op("s", "activation",
                          ASR[0:1, asl * 2048 + q * 512:asl * 2048 + (q + 1) * 512],
                          P[pb][0:1, 0:512], AF.Copy)
                    pa_done = kb.last("s")[1]
                asr_sv[asl] = kb.dma(
                    "y", A_DR.ap().rearrange("ch s t -> (ch s t)"
                        ).rearrange("(g n) -> g n", g=NG)[g].rearrange(
                        "(o n) -> o n", o=1),
                    ASR[0:1, asl * 2048:asl * 2048 + 2048], cc.dout,
                    waits=[kb.last("s")])
            tbatch([(EK[:, tc * 128:(tc + 1) * 128],
                     EKT[:, tc * 128:(tc + 1) * 128]) for tc in range(TC)])
            kb.op("v", "tensor_mul", EKT[:, :], KT[:, :], EKT[:, :])
            s1 = kb.dma("y", TM_DR.ap().rearrange("(tc p) c -> p tc c", p=128),
                        EKT[:, :].rearrange("p (tc c) -> p tc c", c=128),
                        cc.dout, waits=[kb.last("v")])
            s2 = kb.dma("y", TM2_DR.ap().rearrange("(tc p) c -> p tc c", p=128),
                        VTK[:, :].rearrange("p (tc c) -> p tc c", c=128),
                        cc.dout)
            for bb in (0, 32):
                for qq in range(4):
                    chs = slice(qq * NCQ, (qq + 1) * NCQ)
                    kb.dma("g", A16[bb:bb + 16,
                                    qq * NCQ * 16:(qq + 1) * NCQ * 16].rearrange(
                        "s (ch t) -> s ch t", t=16),
                        A_DR.ap().rearrange("ch s t -> s ch t")[:, chs, :],
                        cc.dws, allow_slow_non_contiguous=True,
                        waits=[(cc.dout, s2)] if (qq == 0 and bb == 0) else ())
            kb.op("v", "memset", SST[:, :], 0.0)
            kb.op("v", "memset", SSB[:, :], 0.0)
            kb.wait("t", *kb.last("v"))
            ob_done = [0, 0]
            kv_done = [0, 0]
            ssb_done = 0
            q_used = [0, 0]
            kv_ld = [0, 0]
            ob_sv = [0, 0]
            QW = NCQ * 128

            def kv_pre(qq):
                sl2 = qq % 2
                bb = 32 * sl2
                war = [(cc.tm, q_used[sl2])] if q_used[sl2] else []
                kb.dma("g", KVP[bb:bb + 16, 0:QW].rearrange(
                    "s (ch d) -> s ch d", d=128),
                    TM_DR.ap().rearrange("(ch s) d -> s ch d", s=16)[
                        :, qq * NCQ:(qq + 1) * NCQ, :],
                    cc.dws, allow_slow_non_contiguous=True, waits=war)
                kb.dma("g", KVP[bb:bb + 16, QW:2 * QW].rearrange(
                    "s (ch d) -> s ch d", d=128),
                    TM2_DR.ap().rearrange("(ch s) d -> s ch d", s=16)[
                        :, qq * NCQ:(qq + 1) * NCQ, :],
                    cc.dws, allow_slow_non_contiguous=True)
                kv_ld[sl2] = cc.dws.n
            kv_pre(0)
            kv_pre(1)
            for qq in range(4):
                sl2 = qq % 2
                bb = 32 * sl2
                K16 = KVP[bb:bb + 16, 0:QW]
                V16 = KVP[bb:bb + 16, QW:2 * QW]
                A16s = A16[bb:bb + 16, :]
                OB16 = OBP[bb:bb + 16, :]
                kb.wait("t", cc.dws, kv_ld[sl2])
                if qq == 0:
                    kb.wait("v", cc.dws, kv_ld[sl2])
                for ci in range(NCQ):
                    ch = qq * NCQ + ci
                    pb = ci % 2
                    if ssb_done:
                        kb.wait("t", kb.self_sem["v"], ssb_done)
                    if ob_done[pb]:
                        kb.wait("t", kb.self_sem["v"], ob_done[pb])
                    if ch > 0:
                        kb.op("t", "matmul", P[pb][0:16, 0:128],
                              QD[:, ch * 16:(ch + 1) * 16], SSB[:, :],
                              start=True, stop=False)
                    kb.op("t", "matmul", P[pb][0:16, 0:128],
                          A16s[:, ch * 16:(ch + 1) * 16],
                          V16[:, ci * 128:(ci + 1) * 128],
                          start=(ch == 0), stop=True)
                    tvo = kb.mark("t", cc.tm)
                    if kv_done[pb]:
                        kb.wait("t", kb.self_sem["v"], kv_done[pb])
                    kb.op("t", "matmul", P[2 + pb][:, 0:128],
                          K16[:, ci * 128:(ci + 1) * 128],
                          V16[:, ci * 128:(ci + 1) * 128], start=True, stop=True)
                    tvk = kb.mark("t", cc.tm)
                    kb.wait("v", cc.tm, tvo)
                    if ci == 0 and ob_sv[sl2]:
                        kb.wait("v", cc.dout, ob_sv[sl2])
                    kb.op("v", "tensor_copy", OB16[:, ci * 128:(ci + 1) * 128],
                          P[pb][0:16, 0:128])
                    ob_done[pb] = kb.last("v")[1]
                    kb.wait("v", cc.tm, tvk)
                    kb.op("v", "scalar_tensor_tensor", SST[:, :], SST[:, :],
                          DCC[:, ch:ch + 1], P[2 + pb][:, 0:128], OP.mult, OP.add)
                    kv_done[pb] = kb.last("v")[1]
                    kb.op("v", "tensor_copy", SSB[:, :], SST[:, :])
                    ssb_done = kb.last("v")[1]
                q_used[sl2] = cc.tm.n
                if qq + 2 < 4:
                    kv_pre(qq + 2)
                ow = [kb.last("v")]
                if ogather_v:
                    ow.append((cc.dgi, ogather_v))
                ob_sv[sl2] = kb.dma("y",
                                    O_DR[:, qq * NCQ * 128:(qq + 1) * NCQ * 128],
                                    OB16[:, :], cc.dout, waits=ow)
            so = cc.dout.n
            od = OALL[:, :].rearrange("p (tc hh c) -> p tc hh c", hh=8, c=128)
            os_ = O_DR.ap().rearrange("s (tc chp d) -> s tc chp d",
                                      tc=T // 128, d=128)
            for chp in range(8):
                kb.dma("g", od[16 * chp:16 * chp + 16, :, h, :],
                       os_[:, :, chp, :], cc.dgi,
                       waits=[(cc.dout, so)] if chp == 0 else ())
            ogather_v = cc.dgi.n
        for e in ("v", "s", "t"):
            kb.wait(e, cc.dgi, cc.dgi.n)
        for tc in range(TC):
            sl = slice(tc * 1024, (tc + 1) * 1024)
            kb.op("s", "activation", U1[:, 0:1024], OALL[:, sl], AF.Square,
                  waits=[kb.last("v")])
            kb.op("v", "tensor_scalar", U2[:, 0:1024], U1[:, 0:1024], 1.0, 0.0,
                  OP.mult, OP.add, accum_out=RST[:, tc:tc + 1],
                  waits=[kb.last("s")])
            kb.op("s", "activation", RST[:, TC + tc:TC + tc + 1],
                  RST[:, tc:tc + 1], AF.Ln, bias=cc.kf(1e-5),
                  scale=cc.kf(1.0 / 1024), waits=[kb.last("v")])
            kb.op("s", "activation", RST[:, TC + tc:TC + tc + 1],
                  RST[:, TC + tc:TC + tc + 1], AF.Exp, scale=cc.kf(-0.5))
            kb.op("v", "tensor_scalar", U1[:, 0:1024], OALL[:, sl],
                  RST[:, TC + tc:TC + tc + 1], None, OP.mult,
                  waits=[kb.last("s")])
            kb.op("v", "tensor_mul", OALL[:, sl], U1[:, 0:1024], GNW[:, :])
            kb.wait("t", *kb.last("v"))
            tbatch([(OALL[:, tc * 1024 + k * 128:tc * 1024 + (k + 1) * 128],
                     OF[:, k * T + tc * 128:k * T + (tc + 1) * 128])
                    for k in range(DT)])
        woq = [0, 0]
        for t0 in range(0, T, 512):
            w = min(512, T - t0)
            xi = kb.dma("y", XCHK[:, 0:DT * w].rearrange("p (k t) -> p k t", k=DT),
                        HD["xin"][b].rearrange("(k p) t -> p k t", p=128)[:, :, t0:t0 + w],
                        cc.dsem,
                        waits=[(cc.dout, cc.dout.n), kb.last("v")])
            kb.wait("v", cc.dsem, xi)
            for oc in range(DT):
                dbuf = oc % 2
                if woq[dbuf]:
                    kb.wait("g", cc.tm, woq[dbuf])
                wl = kb.dma("g", WOC[:, dbuf * DT * 128:(dbuf + 1) * DT * 128].rearrange(
                    "p (k c) -> p k c", c=128), wo_r[:, :, oc, :], cc.dws)
                kb.wait("t", cc.dws, wl)
                pb = 4 + oc % 2
                for k in range(DT):
                    kb.op("t", "matmul", P[pb][:, 0:w],
                          WOC[:, dbuf * DT * 128 + k * 128:dbuf * DT * 128 + (k + 1) * 128],
                          OF[:, k * T + t0:k * T + t0 + w],
                          start=(k == 0), stop=(k == DT - 1))
                tv = kb.mark("t", cc.tm)
                woq[dbuf] = tv
                kb.wait("v", cc.tm, tv)
                kb.op("v", "tensor_tensor", XCHK[:, oc * w:(oc + 1) * w],
                      XCHK[:, oc * w:(oc + 1) * w], P[pb][:, 0:w], OP.add)
                kb.wait("t", *kb.last("v"))
            so = kb.dma("y", HD["xout"][b].rearrange("(k p) t -> p k t", p=128)[:, :, t0:t0 + w],
                        XCHK[:, 0:DT * w].rearrange("p (k t) -> p k t", k=DT),
                        cc.dout, waits=[kb.last("v")])
        kb.wait("y", cc.dout, cc.dout.n)
        kb.barrier()


def build_l1a(T, nbatch):
    nc = bass.Bass("TRN2", target_bir_lowering=False)
    HD = {}
    HD["xin"] = nc.dram_tensor("xin", [nbatch, D, T], F32, kind="ExternalInput").ap()
    HD["xout"] = nc.dram_tensor("xout", [nbatch, D, T], F32, kind="ExternalOutput").ap()
    for nm, shp, dt in [("L1_attn_Wq", [D, D], F32), ("L1_attn_Wk", [D, D], F32),
                        ("L1_attn_Wv", [D, D], F32), ("L1_attn_Wr", [D, D], F32),
                        ("L1_attn_Wa", [D, 8], F32), ("L1_attn_A_log", [8], F32),
                        ("L1_attn_dt_bias", [8], F32), ("L1_attn_Wo", [D, D], F32),
                        ("c_maskp", [128, 256], F32), ("c_qnw", [128, 128], F32),
                        ("c_knw", [128, 128], F32), ("c_gnw", [128, 1024], BF16)]:
        HD[nm] = nc.dram_tensor(nm, shp, dt, kind="ExternalInput")
    kb = KB(nc)
    cc = CC(kb, T)
    emit_l1a(kb, cc, HD, nbatch)
    kb.finalize()
    return nc


def l1a_consts(P):
    mask = np.zeros((16, 16), np.float32)
    for s in range(16):
        for t in range(16):
            if t < s:
                mask[s, t] = -1e30
    return {
        "c_maskp": np.tile(mask.reshape(1, 256), (128, 1)).astype(np.float32),
        "c_qnw": np.tile(P["L1_attn_q_norm_w"].reshape(1, 128), (128, 1)),
        "c_knw": np.tile(P["L1_attn_k_norm_w"].reshape(1, 128), (128, 1)),
        "c_gnw": np.tile(P["L1_attn_g_norm_w"].reshape(1, 1024), (128, 1)).astype(ml_dtypes.bfloat16),
    }




def emit_l1f_fused(kb, cc, HD, nbatch, tag):
    from concourse.replica_groups import maybe_share_collective_output_space
    nc = kb.nc
    T = cc.T
    TC = cc.TC
    rg = [list(range(8))]
    sp = maybe_share_collective_output_space("AllReduce", rg)
    cnt_in = kb.dram("ccf_cnt_in", [1, 8], F32)
    cnt_out = kb.dram("ccf_cnt_out", [1, 8], F32)
    xf8_in = kb.dram("ccf_xf8_in", [128, DT * 8], F32)
    xf8_out = kb.dram("ccf_xf8_out", [128, DT * 8], F32)
    hp_in = kb.dram("ccf_hp_in", [64, 512], F32)
    hp_out = kb.dram("ccf_hp_out", [64, 512], F32)
    ohw_dr = kb.dram("ccf_ohw", [nbatch, TC, 128, 20], F32)
    cnt8_dr = kb.dram("ccf_cnt8", [1, 8], F32)
    xf8_dr = kb.dram("ccf_xf8b", [128, DT * 8], BF16)
    ccs = kb.sem("ccs")

    HDa = dict(HD)
    HDa["cnt8"] = cnt8_dr
    HDa["ohw"] = ohw_dr.ap()
    HDa["xf8"] = xf8_dr.ap()
    emit_l1f_a(kb, cc, HDa, nbatch)
    kb.new_phase(cc)

    XFT = kb.sb("ccf_XFT", [128, DT * 8], F32)
    XFB = kb.sb("ccf_XFB", [128, DT * 8], BF16)
    C0 = kb.sb("ccf_C0", [128, 1], F32)
    l1 = kb.dma("y", XFB[:, :], xf8_dr.ap(), cc.dsem)
    l2 = kb.dma("y", C0[:, :], HD["c_core0"][:, :], cc.dsem)
    kb.wait("v", cc.dsem, l2)
    kb.op("v", "tensor_scalar", XFT[:, :], XFB[:, :], C0[:, 0:1], None, OP.mult)
    s1 = kb.dma("y", xf8_in.ap(), XFT[:, :], cc.dout, waits=[kb.last("v")])
    s2 = kb.dma("y", cnt_in.ap(), cnt8_dr.ap(), cc.dout)
    kb.wait("g", cc.dout, s2)
    kb.op("g", "collective_compute", "AllReduce", OP.add,
          replica_groups=rg, ins=[cnt_in.ap().opt()], outs=[cnt_out.ap().opt()],
          incs=[(ccs, "cc")], selfsync=False)
    kb.op("g", "collective_compute", "AllReduce", OP.add,
          replica_groups=rg, ins=[xf8_in.ap().opt()], outs=[xf8_out.ap().opt()],
          incs=[(ccs, "cc")], selfsync=False)
    kb.wait("g", ccs, ccs.n)
    gl = kb.dma("g", xf8_dr.ap(), xf8_out.ap(), cc.dgi)
    kb.wait("y", cc.dgi, gl)
    kb.new_phase(cc)

    HDb = dict(HD)
    HDb["cnts_global"] = cnt_out
    HDb["xf8"] = xf8_dr.ap()
    HDb["ohw"] = ohw_dr.ap()
    HDb["hpart"] = kb.dram("ccf_hpart_loc", [64, 512], F32)
    emit_l1f_b(kb, cc, HDb, nbatch)
    kb.new_phase(cc)

    s3 = kb.dma("y", hp_in.ap(), HDb["hpart"].ap(), cc.dout)
    kb.wait("g", cc.dout, s3)
    kb.op("g", "collective_compute", "AllReduce", OP.add,
          replica_groups=rg, ins=[hp_in.ap().opt()], outs=[hp_out.ap().opt()],
          incs=[(ccs, "cc")], selfsync=False)
    kb.wait("g", ccs, ccs.n)
    kb.wait("y", ccs, ccs.n)
    kb.new_phase(cc)

    HDc = dict(HD)
    HDc["hpart_global"] = hp_out
    HDc["xa"] = HD["xout"]
    HDc["inplace"] = True
    emit_l1f_c(kb, cc, HDc, nbatch)



def _declare_all_inputs(nc, nbatch, T):
    TC = T // 128
    HD = {}
    specs = [
        ("L0_ffn_Wq", [D, D], F32), ("L0_ffn_bq", [D], F32),
        ("L0_ffn_Wk", [D, D], F32), ("L0_ffn_bk", [D], F32),
        ("L0_ffn_Wv", [D, D], F32), ("L0_ffn_bv", [D], F32),
        ("L0_ffn_Wo", [D, D], F32), ("L0_ffn_bo", [D], F32),
        ("L0_ffn_g1", [D], F32), ("L0_ffn_be1", [D], F32),
        ("L0_ffn_g2", [D], F32), ("L0_ffn_be2", [D], F32),
        ("L0_ffn_Wg", [D, 8], F32), ("L0_ffn_bg", [8], F32),
        ("L0_ffn_We", [8, D, D], F32), ("L0_ffn_bexp", [8, D], F32),
        ("L0_attn_ln_w", [2, D], F32), ("L0_attn_ln_b", [2, D], F32),
        ("L0_attn_in_proj_w", [2, 2 * E_IN, D], F32),
        ("L0_attn_conv_w", [2, E_IN, 4], F32), ("L0_attn_conv_b", [2, E_IN], F32),
        ("L0_attn_x_proj_w", [2, 96, E_IN], F32),
        ("L0_attn_dt_proj_w", [2, E_IN, RNK], F32),
        ("L0_attn_dt_proj_b", [2, E_IN], F32),
        ("L0_attn_D_skip", [2, E_IN], F32),
        ("L0_attn_out_proj_w", [2, D, E_IN], F32),
        ("L0_attn_norm_f_w", [D], F32), ("L0_attn_norm_f_b", [D], F32),
        ("L0A_Alog_r", [2, 16, 128, 16], F32),
        ("c_sel", [128, 16 * 128], BF16), ("c_selt", [128, 16 * 128], BF16),
        ("c_s16", [16, 128], BF16),
        ("L1_ffn_w_gate", [8, D], F32), ("L1_ffn_sw1", [D, D], F32),
        ("L1_ffn_sw2", [D, D], F32), ("L1_ffn_sw3", [D, D], F32),
        ("L1_ffn_expert_bias", [8], F32),
        ("L1_ffn_w1", [8, D, 512], F32), ("L1_ffn_w3", [8, D, 512], F32),
        ("L1_ffn_w2", [8, 512, D], F32),
        ("c_one8", [8, 1], F32), ("c_ut8", [8, 8], F32),
        ("c_ja", [128, nbatch * TC], F32), ("maskval", [8, 1], F32),
        ("c_core0", [128, 1], F32),
        ("L1_attn_Wq", [D, D], F32), ("L1_attn_Wk", [D, D], F32),
        ("L1_attn_Wv", [D, D], F32), ("L1_attn_Wr", [D, D], F32),
        ("L1_attn_Wa", [D, 8], F32), ("L1_attn_A_log", [8], F32),
        ("L1_attn_dt_bias", [8], F32), ("L1_attn_Wo", [D, D], F32),
        ("c_maskp", [128, 256], F32), ("c_qnw", [128, 128], F32),
        ("c_knw", [128, 128], F32), ("c_gnw", [128, 1024], BF16),
    ]
    for nm, shp, dt in specs:
        HD[nm] = nc.dram_tensor(nm, shp, dt, kind="ExternalInput")
    for nm in ("L1_ffn_w1", "L1_ffn_w3", "L1_ffn_w2"):
        HD[nm] = HD[nm].ap()
    return HD


ARENA_COLS = 99456


def build_fused(T, nbatch, upto=None):
    nc = bass.Bass("TRN2", target_bir_lowering=False)
    xin = nc.dram_tensor("xin", [nbatch, D, T], F32, kind="ExternalInput")
    xout = nc.dram_tensor("xout", [nbatch, D, T], F32, kind="ExternalOutput")
    HDW = _declare_all_inputs(nc, nbatch, T)
    kb = KB(nc)
    cc = CC(kb, T)
    xa = kb.dram("xmid_a", [nbatch, D, T], F32)
    xb = kb.dram("xmid_b", [nbatch, D, T], F32)
    kb.enable_arena(ARENA_COLS)
    offs = []
    nph = [0]

    def phase(emit, xi, xo, *extra):
        if upto is not None and nph[0] >= upto:
            return
        nph[0] += 1
        HDp = dict(HDW)
        HDp["xin"] = xi.ap()
        HDp["xout"] = xo.ap()
        emit(kb, cc, HDp, nbatch, *extra)
        offs.append(kb._ar_off)
        kb.new_phase(cc)

    phase(emit_l0f, xin, xa)
    phase(emit_l0a, xa, xb)
    phase(emit_l0f, xb, xa)
    phase(emit_l1f_fused, xa, xb, 0)
    phase(emit_l1a, xb, xa)
    if upto is None or nph[0] < upto:
        HDp = dict(HDW)
        HDp["xin"] = xa.ap()
        HDp["xout"] = xout.ap()
        emit_l1f_fused(kb, cc, HDp, nbatch, 1)
        offs.append(kb._ar_off)
    kb.finalize()
    build_fused.arena_offs = offs
    return nc


def const_inputs():
    return {
        "c_ones_col_bf": np.ones((128, 1), ml_dtypes.bfloat16),
        "c_ones_row_bf": np.ones((1, 128), ml_dtypes.bfloat16),
        "c_ones_row_f32": np.ones((1, 128), np.float32),
        "c_ident_bf": np.eye(128, dtype=ml_dtypes.bfloat16),
        "c_scal": np.tile(np.array([1e-5, -10.0, -0.5, 0.125, 1.0, 0.5,
                                    -1.0, 1.0 / 1024, 1.0 / 128, 2.0],
                                   np.float32), (128, 1)),
        "c_ones_r65": np.ones((65, 128), np.float32),
    }


USE_FUSED = True
_BUILD_CACHE = {}


def _get(build_fn, key, *a):
    if key not in _BUILD_CACHE:
        _BUILD_CACHE[key] = build_fn(*a)
    return _BUILD_CACHE[key]


def kernel(**inputs):
    T = 2048
    NB = 2
    NC = 8
    TC = T // 128

    x = np.ascontiguousarray(
        np.asarray(inputs["L0_attn_x"], np.float32).transpose(0, 2, 1))
    consts = dict(const_inputs())
    l0ac = l0a_consts()
    Pw = {k: np.ascontiguousarray(np.asarray(v)) for k, v in inputs.items()}
    l1ac = l1a_consts(Pw)
    al = Pw["L0_attn_A_log"].reshape(2, 16, 16, 8, 16)
    alog_r = np.ascontiguousarray(al.transpose(0, 1, 3, 4, 2)).reshape(2, 16, 128, 16)

    def shard(xf):
        return [np.ascontiguousarray(xf[c * NB:(c + 1) * NB]) for c in range(NC)]

    def launch(nc, per_core, outs):
        res = run_bass_kernel_spmd(nc, per_core, core_ids=list(range(NC)))
        return [{k: np.asarray(res.results[c][k]) for k in outs}
                for c in range(NC)]

    def base_ins(keys):
        d = dict(consts)
        for k in keys:
            d[k] = Pw[k]
        return d

    L0F_KEYS = [k for k in Pw if k.startswith("L0_ffn")]
    L0A_KEYS = [k for k in Pw if k.startswith("L0_attn") and
                k not in ("L0_attn_x", "L0_attn_A_log")]
    L1A_KEYS = ["L1_attn_Wq", "L1_attn_Wk", "L1_attn_Wv", "L1_attn_Wr",
                "L1_attn_Wa", "L1_attn_A_log", "L1_attn_dt_bias", "L1_attn_Wo"]

    cur = shard(x)

    def run_l0f(cur):
        nc = _get(build_l0f, "l0f", T, NB)
        ins = base_ins(L0F_KEYS)
        per = [dict(ins, xin=cur[c]) for c in range(NC)]
        r = launch(nc, per, ["xout"])
        return [r[c]["xout"].reshape(NB, D, T) for c in range(NC)]

    def run_l0a(cur):
        nc = _get(build_l0a, "l0a", T, NB)
        ins = base_ins(L0A_KEYS)
        ins.update(l0ac)
        ins["L0A_Alog_r"] = alog_r
        per = [dict(ins, xin=cur[c]) for c in range(NC)]
        r = launch(nc, per, ["xout"])
        return [r[c]["xout"].reshape(NB, D, T) for c in range(NC)]

    def run_l1f(cur):
        nca = _get(build_l1f_a, "l1fa", T, NB)
        insa = base_ins(["L1_ffn_w_gate", "L1_ffn_sw1", "L1_ffn_sw2",
                         "L1_ffn_sw3", "L1_ffn_expert_bias"])
        per = [dict(insa, xin=cur[c]) for c in range(NC)]
        ra = launch(nca, per, ["xout", "cnt8", "ohw", "xf8"])
        cnts = np.stack([ra[c]["cnt8"].reshape(8) for c in range(NC)])
        ncb = _get(build_l1f_b, "l1fb", T, NB)
        insb = dict(consts)
        insb["cnts"] = cnts.astype(np.float32)
        insb["xf8"] = ra[0]["xf8"]
        insb["c_one8"] = np.ones((8, 1), np.float32)
        insb["c_ut8"] = np.triu(np.ones((8, 8), np.float32))
        insb["L1_ffn_w1"] = Pw["L1_ffn_w1"]
        insb["L1_ffn_w3"] = Pw["L1_ffn_w3"]
        perb = []
        for c in range(NC):
            ja = np.zeros((128, NB * TC), np.float32)
            for b in range(NB):
                for tc in range(TC):
                    base = 2.0 * (c * NB * T + b * T + tc * 128)
                    ja[:, b * TC + tc] = base + 2.0 * np.arange(128)
            perb.append(dict(insb, ohw=ra[c]["ohw"].reshape(NB, TC, 128, 20),
                             c_ja=ja))
        rb = launch(ncb, perb, ["hpart"])
        hparts = np.stack([rb[c]["hpart"].reshape(64, 512) for c in range(NC)])
        ncc = _get(build_l1f_c, "l1fc", T, NB)
        insc = dict(consts)
        insc["hparts"] = hparts.astype(np.float32)
        insc["L1_ffn_w2"] = Pw["L1_ffn_w2"]
        perc = []
        for c in range(NC):
            mv = np.full((8, 1), 0.5 if c == 0 else 0.0, np.float32)
            perc.append(dict(insc, xa=ra[c]["xout"].reshape(NB, D, T),
                             maskval=mv))
        rc = launch(ncc, perc, ["xout"])
        return [rc[c]["xout"].reshape(NB, D, T) for c in range(NC)]

    def run_l1a(cur):
        nc = _get(build_l1a, "l1a", T, NB)
        ins = base_ins(L1A_KEYS)
        ins.update(l1ac)
        per = [dict(ins, xin=cur[c]) for c in range(NC)]
        r = launch(nc, per, ["xout"])
        return [r[c]["xout"].reshape(NB, D, T) for c in range(NC)]

    if not USE_FUSED:
        cur = run_l0f(cur)
        cur = run_l0a(cur)
        cur = run_l0f(cur)
        cur = run_l1f(cur)
        cur = run_l1a(cur)
        cur = run_l1f(cur)
        out = np.concatenate(cur, axis=0)
        return np.ascontiguousarray(out.transpose(0, 2, 1)).astype(np.float32)

    fc = dict(consts)
    fc.update(l0ac)
    fc.update(l1ac)
    fc["L0A_Alog_r"] = alog_r
    fc["c_one8"] = np.ones((8, 1), np.float32)
    fc["c_ut8"] = np.triu(np.ones((8, 8), np.float32))
    for k in Pw:
        if k not in ("L0_attn_x", "L0_attn_A_log"):
            fc[k] = Pw[k]
    nc = _get(build_fused, "fused", T, NB)
    per = []
    for c in range(NC):
        d = dict(fc)
        d["xin"] = cur[c]
        ja = np.zeros((128, NB * TC), np.float32)
        for b in range(NB):
            for tc in range(TC):
                ja[:, b * TC + tc] = (2.0 * (c * NB * T + b * T + tc * 128) +
                                      2.0 * np.arange(128))
        d["c_ja"] = ja
        d["maskval"] = np.full((8, 1), 0.5 if c == 0 else 0.0, np.float32)
        d["c_core0"] = np.full((128, 1), 1.0 if c == 0 else 0.0, np.float32)
        per.append(d)
    res = run_bass_kernel_spmd(nc, per, core_ids=list(range(NC)))
    out = np.concatenate([np.asarray(res.results[c]["xout"]).reshape(NB, D, T)
                          for c in range(NC)], 0)
    return np.ascontiguousarray(out.transpose(0, 2, 1)).astype(np.float32)
